# version 03
# baseline (speedup 1.0000x reference)
import numpy as np
import concourse.bass as bass
import concourse.mybir as mybir
from concourse.bass_utils import run_bass_kernel_spmd
from concourse import library_config
from concourse.library_overlay import lower_extended_insts

F32 = mybir.dt.float32
BF16 = mybir.dt.bfloat16
I16 = mybir.dt.int16
I32 = mybir.dt.int32
U32 = mybir.dt.uint32
AX = mybir.AxisListType
OP = mybir.AluOpType
AF = mybir.ActivationFunctionType

TE, VE, SE, GE, SY = "tensor", "vector", "scalar", "gpsimd", "sync"
ENGINES = (TE, VE, SE, GE, SY)

PH = {n: i for i, n in enumerate("ABCDEFGHI")}


class Cfg:
    def __init__(self, T=4096, I=4096, IS=2048, CAP=640, BIS=36, n_cores=8):
        self.T = T
        self.D = 1024
        self.NH = 16
        self.HS = 64
        self.KVL = 256
        self.E = 16
        self.I = I
        self.IS = IS
        self.CAP = CAP
        self.BIS = BIS
        self.n_cores = n_cores
        self.CSEL = n_cores * T * 2 // 16
        self.QT = 512
        self.IB = min(256, I)
        import os
        self.stop = os.environ.get("KSTOP", "")
        assert T % 512 == 0 and I % self.IB == 0 and CAP % 128 == 0


class Sem:
    def __init__(self, nc, name):
        self.h = nc.semaphore(name).__enter__()
        self.n = 0

    def inc(self, k=1):
        self.n += k
        return self.n


class Arena:

    def __init__(self, nc, name, dtype, esz):
        self.nc, self.name, self.dtype, self.esz = nc, name, dtype, esz
        self.reqs = []
        self.views = {}

    def alloc(self, name, shape, ph0, ph1):
        self.reqs.append((name, shape, PH[ph0], PH[ph1]))
        return name

    def finalize(self):
        placed = []
        for name, shape, p0, p1 in sorted(self.reqs, key=lambda r: -np.prod(r[1])):
            cols = int(np.prod(shape[1:]))
            cands = sorted({0} | {off + c for off, c, q0, q1, *_ in placed
                                  if not (q1 < p0 or p1 < q0)})
            for off in cands:
                ok = True
                for o2, c2, q0, q1, *_ in placed:
                    if q1 < p0 or p1 < q0:
                        continue
                    if not (off + cols <= o2 or o2 + c2 <= off):
                        ok = False
                        break
                if ok:
                    placed.append((off, cols, p0, p1, name, shape))
                    break
        total = max(o + c for o, c, *_ in placed) if placed else 0
        import os
        if os.environ.get("ARENA_DEBUG"):
            print(f"arena {self.name}: total {total} cols = {total*self.esz} B/part")
            for off, cols, p0, p1, name, shape in sorted(placed):
                print(f"  {off:7d}..{off+cols:7d} ph{p0}-{p1} {name} {shape}")
        base = self.nc.alloc_sbuf_tensor(self.name, [128, total], self.dtype).ap()
        for off, cols, p0, p1, name, shape in placed:
            v = base[0:shape[0], off:off + cols]
            if len(shape) > 2:
                pat = "p (" + " ".join(f"a{i}" for i in range(len(shape) - 1)) + ") -> p " \
                      + " ".join(f"a{i}" for i in range(len(shape) - 1))
                kw = {f"a{i}": shape[i + 1] for i in range(len(shape) - 1)}
                v = v.rearrange(pat, **kw)
            self.views[name] = v
        self.total_bytes = total * self.esz
        return self.views


class Prog:
    def __init__(self, nc):
        self.nc = nc
        self.ops = {e: [] for e in ENGINES}
        self.last = {e: None for e in ENGINES}
        self.n_real = {e: 0 for e in ENGINES}
        self.mark_at = {e: -1 for e in ENGINES}
        self.mark_vals = {e: {} for e in ENGINES}
        self.auto = {}

    def on(self, eng, fn):
        a = self.auto.get(eng)
        if a is not None:
            prev = a.n
            self.ops[eng].append(lambda e, s=a.h, vv=prev: e.wait_ge(s, vv))
            v = a.inc()

            def wrapped(e, f=fn, eng_=eng, s=a.h):
                r = f(e)
                if r is not None:
                    self.last[eng_] = r
                    r.then_inc(s, 1)
                else:
                    e.nop().then_inc(s, 1)
            self.ops[eng].append(wrapped)
            self.n_real[eng] += 1
            return
        def wrapped(e, f=fn, eng_=eng):
            r = f(e)
            if r is not None:
                self.last[eng_] = r
        self.ops[eng].append(wrapped)
        self.n_real[eng] += 1

    def wait(self, eng, sem, val=None):
        v = sem.n if val is None else val
        self.ops[eng].append(lambda e, s=sem.h, vv=v: e.wait_ge(s, vv))

    def mark(self, eng, sem):
        a = self.auto.get(eng)
        if a is not None:
            assert sem is a, (eng, "marks on auto engines use the auto sem")
            return a.n
        if self.mark_at[eng] == self.n_real[eng]:
            if sem in self.mark_vals[eng]:
                return self.mark_vals[eng][sem]
            o_sem, o_v = next(iter(self.mark_vals[eng].items()))
            self.wait(eng, o_sem, o_v)
            v = sem.inc()
            self.ops[eng].append(lambda e, s=sem.h: e.nop().then_inc(s, 1))
            self.mark_vals[eng][sem] = v
            return v
        v = sem.inc()
        self.ops[eng].append(lambda e, s=sem.h, eng_=eng:
                             self.last[eng_].then_inc(s, 1))
        self.mark_at[eng] = self.n_real[eng]
        self.mark_vals[eng] = {sem: v}
        return v

    def ckpt(self, tag):
        if not hasattr(self, "ckpts"):
            self.ckpts = {}
        self.ckpts[tag] = {e: len(self.ops[e]) for e in ENGINES}

    def _dma_raw(self, eng, fn):
        def wrapped(e, f=fn, eng_=eng):
            r = f(e)
            if r is not None:
                self.last[eng_] = r
        self.ops[eng].append(wrapped)
        self.n_real[eng] += 1

    def dma(self, eng, out_ap, in_ap, sem, **kw):
        v = sem.inc(16)
        self._dma_raw(eng, lambda e, o=out_ap, i=in_ap, s=sem.h, k=dict(kw):
                      e.dma_start(out=o, in_=i, **k).then_inc(s, 16))
        return v

    def dmat(self, eng, out_ap, in_ap, sem):
        v = sem.inc(16)
        self._dma_raw(eng, lambda e, o=out_ap, i=in_ap, s=sem.h:
                      e.dma_start_transpose(o, i).then_inc(s, 16))
        return v

    def run(self):
        nc = self.nc
        with nc.Block() as block:
            @block.tensor
            def _(eng):
                for f in self.ops[TE]:
                    f(eng)

            @block.vector
            def _(eng):
                for f in self.ops[VE]:
                    f(eng)

            @block.scalar
            def _(eng):
                for f in self.ops[SE]:
                    f(eng)

            @block.gpsimd
            def _(eng):
                for f in self.ops[GE]:
                    f(eng)

            @block.sync
            def _(eng):
                for f in self.ops[SY]:
                    f(eng)


def host_prep(cfg, inputs):
    T, D, I, IS, E, IB = cfg.T, cfg.D, cfg.I, cfg.IS, cfg.E, cfg.IB
    f = np.ascontiguousarray
    x = np.asarray(inputs["L0_attn_x"], dtype=np.float32)
    com = {}
    com["wdqT"] = f(np.asarray(inputs["L0_attn_W_dq"], np.float32).T)
    com["wdq"] = f(np.asarray(inputs["L0_attn_W_dq"], np.float32))
    com["wuq"] = f(np.asarray(inputs["L0_attn_W_uq"], np.float32))
    com["wdkvT"] = f(np.asarray(inputs["L0_attn_W_dkv"], np.float32).T)
    com["wuk"] = f(np.asarray(inputs["L0_attn_W_uk"], np.float32))
    com["wuv"] = f(np.asarray(inputs["L0_attn_W_uv"], np.float32))
    com["woT"] = f(np.asarray(inputs["L0_attn_W_o"], np.float32).T)
    com["gwT"] = f(np.asarray(inputs["L0_ffn_gate_w"], np.float32).T)
    com["sgT"] = f(np.asarray(inputs["L0_ffn_Sg"], np.float32).T)
    com["suT"] = f(np.asarray(inputs["L0_ffn_Su"], np.float32).T)
    com["sdT"] = f(np.asarray(inputs["L0_ffn_Sd"], np.float32).T)
    wg = np.asarray(inputs["L0_ffn_Wg"], np.float32)
    wu = np.asarray(inputs["L0_ffn_Wu"], np.float32)
    wd = np.asarray(inputs["L0_ffn_Wd"], np.float32)
    nIS = I // IB
    com["wgT_p"] = f(wg.reshape(E, nIS, IB, 8, 128).transpose(0, 1, 4, 3, 2))
    com["wuT_p"] = f(wu.reshape(E, nIS, IB, 8, 128).transpose(0, 1, 4, 3, 2))
    nWS = I // 512
    com["wdT_p"] = f(wd.transpose(0, 2, 1).reshape(E, nWS, 4, 128, D)
                     .transpose(0, 1, 3, 2, 4))
    com["c_ident"] = np.eye(128, dtype=np.float32)
    m4 = np.zeros((4, 128, cfg.QT), dtype=np.float32)
    r = np.arange(128)[:, None]
    c = np.arange(cfg.QT)[None, :]
    for o in range(4):
        m4[o][(o * 128 + r) > c] = -1e30
    com["c_mask4"] = m4
    com["c_ones_col"] = np.ones((128, 1), dtype=np.float32)
    SX = T // 16
    ids = np.arange(16)[:, None] * SX + np.arange(SX)[None, :]
    com["c_idsp1"] = (ids + 1).astype(np.float32)
    NIDX = cfg.CAP // 16
    rk = np.arange(NIDX)[None, :] * 16 + np.arange(16)[:, None]
    com["c_rk1"] = (rk + 1).astype(np.float32)
    blkm = (np.arange(128)[:, None] // 8 == np.arange(16)[None, :]).astype(np.float32)
    com["c_blk"] = f(blkm)
    com["c_blkT"] = f(blkm.T)
    com["c_ones_1_16"] = np.ones((1, 16), dtype=np.float32)
    cores = []
    for b in range(cfg.n_cores):
        xb = f(x[b])
        cores.append({
            "x_tm": xb,
            "x_fmt": f(xb.T.reshape(D, T // 512, 512).transpose(1, 0, 2)),
        })
    return com, cores


def build(cfg):
    T, D, I, IS, E, CAP, IB, QT = (cfg.T, cfg.D, cfg.I, cfg.IS, cfg.E,
                                   cfg.CAP, cfg.IB, cfg.QT)
    NH = cfg.NH
    TC = T // 128
    T5 = T // 512
    nIS = I // IB
    IC = I // 128
    ISC = IS // 128
    NCAP = CAP // 128
    SX = T // 16
    NR = cfg.n_cores
    NKQ = QT // 128

    nc = bass.Bass("TRN2", target_bir_lowering=False, debug=False)

    def din(name, shape):
        return nc.dram_tensor(name, list(shape), F32, kind="ExternalInput").ap()

    x_tm = din("x_tm", (T, D))
    x_fmt = din("x_fmt", (T5, D, 512))
    wdqT = din("wdqT", (D, 256)); wdq = din("wdq", (256, D))
    wuq = din("wuq", (D, 256)); wdkvT = din("wdkvT", (D, 256))
    wuk = din("wuk", (D, 256)); wuv = din("wuv", (D, 256))
    woT = din("woT", (D, D)); gwT = din("gwT", (D, 16))
    sgT = din("sgT", (D, IS)); suT = din("suT", (D, IS)); sdT = din("sdT", (IS, D))
    wgT_p = din("wgT_p", (E, nIS, 128, 8, IB))
    wuT_p = din("wuT_p", (E, nIS, 128, 8, IB))
    wdT_p = din("wdT_p", (E, I // 512, 128, 4, D))
    c_ident = din("c_ident", (128, 128))
    c_mask4 = din("c_mask4", (4, 128, QT))
    c_ones_col = din("c_ones_col", (128, 1))
    c_idsp1 = din("c_idsp1", (16, SX))
    c_rk1 = din("c_rk1", (16, CAP // 16))
    c_blk = din("c_blk", (128, 16))
    c_blkT = din("c_blkT", (16, 128))
    c_ones_1_16 = din("c_ones_1_16", (1, 16))

    out = nc.dram_tensor("out", [T, D], F32, kind="ExternalOutput").ap()
    x1bf_dram = nc.dram_tensor("x1bf_dram", [T, D], BF16, kind="Internal").ap()
    x1f_dram = nc.dram_tensor("x1f_dram", [T, D], F32, kind="Internal").ap()
    moe_dram = nc.dram_tensor("moe_dram", [T, D], F32, kind="Internal").ap()
    cc_in = nc.dram_tensor("cc_in", [16, T], F32, kind="Internal").ap()
    cc_out = nc.dram_tensor("cc_out", [16 * NR, T], F32, kind="Internal").ap()

    a16 = Arena(nc, "a16", BF16, 2)
    a32 = Arena(nc, "a32", F32, 4)

    def A16(name, shape, ph0, ph1):
        return a16.alloc(name, shape, ph0, ph1)

    def A32(name, shape, ph0, ph1):
        return a32.alloc(name, shape, ph0, ph1)

    A16("ident16", (128, 128), "A", "I")
    A16("ones_col", (128, 1), "A", "I")
    A16("gwT_sb", (128, 8 * 16), "A", "I")
    A16("wdqT_sb", (128, 8 * 256), "A", "B")
    A16("wdkvT_sb", (128, 8 * 256), "A", "B")
    A16("wuq_sb", (128, 8 * 256), "A", "A")
    A16("wuk_sb", (128, 8 * 256), "A", "A")
    A16("wuv_sb", (128, 8 * 256), "A", "A")
    A16("wdq_sb", (128, 2 * D), "A", "A")
    A16("woT_sb", (128, 8 * D), "A", "A")
    A16("b_sb", (128, 2 * 256), "A", "A")
    A16("keff_sb", (128, 8 * 256), "A", "A")
    A16("m_sb", (128, NH * 2 * 256), "A", "C")
    A16("ve_sb", (128, 2 * D), "A", "C")
    A16("xfm_t", (128, 2, 8, 512), "B", "B")
    A16("q1T", (128, 2, T), "B", "C")
    A16("ckvT", (128, 2, T), "B", "C")
    A16("ckvtm", (128, TC, 256), "B", "C")
    A16("qlatT", (128, 2, T), "C", "C")
    A16("pT", (128, 3, QT), "C", "C")
    A16("ctxT", (128, 2, QT), "C", "C")
    A16("y_sb", (128, TC, D), "C", "D")
    A16("x1b_t", (128, 2, D), "D", "D")
    A16("x1g_t", (128, 2, 8, 512), "E", "E")
    A16("sgT_sb", (128, 8, IS), "D", "F")
    A16("suT_sb", (128, 8, IS), "D", "F")
    A16("sdT_sb", (128, ISC, D), "E", "F")
    A16("x1s_t", (128, 2, 8, 512), "F", "F")
    A16("hs_sb", (128, ISC, 512), "F", "F")
    A16("junk_bf", (128, T), "F", "F")
    A16("wg_buf", (128, 2, 8, IB), "G", "I")
    A16("wu_buf", (128, 2, 8, IB), "G", "I")
    A16("wd_buf", (128, IC, D), "G", "I")
    A16("xg_sb", (128, 2, 8, CAP), "G", "I")
    A16("h_sb", (128, IC, CAP), "G", "I")

    A32("identf", (128, 128), "A", "I")
    A32("mask4", (128, 4, QT), "A", "C")
    A32("blk", (128, 16), "A", "I")
    A32("blkT", (16, 128), "A", "I")
    A32("ones_1_16", (1, 16), "A", "I")
    A32("idsp1", (16, SX), "A", "I")
    A32("rk1", (16, CAP // 16), "A", "I")
    A32("nf16", (16, 16), "G", "H")
    A32("mwin", (16, CAP // 16), "G", "G")
    A32("dn_row", (1, QT), "C", "C")
    A32("recip_pp", (128, 4), "C", "C")
    A32("xt_t", (128, 2, D), "D", "D")
    A32("x1f_t", (128, 2, D), "D", "D")
    A32("lnjunk", (128, D), "D", "I")
    A32("st_t", (128, 16), "D", "I")
    A32("logit_t", (128, 16), "E", "E")
    A32("sc_t", (128, 16), "E", "E")
    A32("scT", (16, T), "E", "E")
    A32("gsc", (128, T), "E", "F")
    A32("sloc", (16, E, SX), "F", "G")
    A32("cnt_pp", (128, 1), "F", "F")
    A32("cnt_e", (16, 1), "F", "F")
    A32("lo_v", (16, 1), "F", "G")
    A32("hi_v", (16, 1), "F", "F")
    A32("mid_v", (16, 4), "F", "F")
    A32("tau_pp", (128, 1), "F", "F")
    A32("lo_row", (1, 16), "F", "G")
    A32("tau16", (16, 16), "F", "G")
    A32("mids", (16, 2, SX), "G", "G")
    A32("comp", (16, 2, SX), "G", "G")
    A32("nf_f", (1, 16), "G", "H")
    A32("gtmp", (128, 512), "F", "F")
    A32("sh_t", (128, 2, D), "F", "F")
    A32("gat_t", (128, 16), "H", "H")
    A32("gat_s", (128, 8), "E", "H")
    A32("moegelu", (128, CAP), "H", "H")
    A32("eo_sb", (128, NCAP, D), "H", "H")
    A32("mo_t", (128, 2, D), "I", "I")
    A32("xf2_t", (128, 2, D), "I", "I")
    A32("o_t", (128, 2, D), "I", "I")

    V16 = a16.finalize()
    V32 = a32.finalize()
    V = {}
    V.update(V16)
    V.update(V32)
    assert a16.total_bytes + a32.total_bytes < 205 * 1024, \
        (a16.total_bytes // 1024, a32.total_bytes // 1024)

    NIDX = CAP // 16
    i16tmp = nc.alloc_sbuf_tensor("i16tmp", [16, NIDX], I16).ap()
    idx16 = nc.alloc_sbuf_tensor("idx16", [128, E * NIDX], I16).ap()
    idx16 = idx16.rearrange("p (a b) -> p a b", a=E)
    nf_u = nc.alloc_sbuf_tensor("nf_u", [1, 16], U32).ap()
    nf_i = nc.alloc_sbuf_tensor("nf_i", [1, 16], I32).ap()
    c_eps = nc.alloc_sbuf_tensor("c_eps", [128, 1], F32).ap()
    nc.const_aps.register(F32, 1e-5, c_eps)

    (ident16, ones_col, gwT_sb, wdqT_sb, wdkvT_sb, wuq_sb, wuk_sb, wuv_sb,
     wdq_sb, woT_sb, b_sb, keff_sb, m_sb, ve_sb, xfm_t, q1T, ckvT, ckvtm,
     qlatT, pT, ctxT, y_sb, x1b_t, x1g_t, sgT_sb, suT_sb, sdT_sb, x1s_t,
     hs_sb, junk_bf, wg_buf, wu_buf, wd_buf, xg_sb, h_sb) = (
        V[k] for k in (
            "ident16", "ones_col", "gwT_sb", "wdqT_sb", "wdkvT_sb", "wuq_sb",
            "wuk_sb", "wuv_sb", "wdq_sb", "woT_sb", "b_sb", "keff_sb", "m_sb",
            "ve_sb", "xfm_t", "q1T", "ckvT", "ckvtm", "qlatT", "pT", "ctxT",
            "y_sb", "x1b_t", "x1g_t", "sgT_sb", "suT_sb", "sdT_sb", "x1s_t",
            "hs_sb", "junk_bf", "wg_buf", "wu_buf", "wd_buf",
            "xg_sb", "h_sb"))
    (identf, mask4, blk, blkT, ones_1_16, idsp1, dn_row, recip_pp, xt_t,
     x1f_t, lnjunk, st_t, logit_t, sc_t, scT, gsc, sloc, cnt_pp, cnt_e, lo_v,
     hi_v, mid_v, tau_pp, lo_row, tau16, mids, comp, nf_f, gtmp, sh_t, gat_t,
     gat_s, moegelu, eo_sb, mo_t, xf2_t, o_t, rk1, nf16, mwin) = (
        V[k] for k in (
            "identf", "mask4", "blk", "blkT", "ones_1_16", "idsp1", "dn_row",
            "recip_pp", "xt_t", "x1f_t", "lnjunk", "st_t", "logit_t", "sc_t",
            "scT", "gsc", "sloc", "cnt_pp", "cnt_e", "lo_v", "hi_v", "mid_v",
            "tau_pp", "lo_row", "tau16", "mids", "comp", "nf_f", "gtmp",
            "sh_t", "gat_t", "gat_s", "moegelu", "eo_sb", "mo_t", "xf2_t",
            "o_t", "rk1", "nf16", "mwin"))

    PS = [nc.alloc_psum_tensor(f"ps{i}", [128, 512], F32).ap() for i in range(8)]

    p = Prog(nc)
    dma_sems = {SY: [], GE: []}

    def dsem(eng, name):
        s = Sem(nc, name)
        dma_sems[eng].append(s)
        return s

    s_cst = dsem(SY, "s_cst")
    s_aw = dsem(GE, "s_aw")
    s_xf = [dsem(GE, "s_xf0"), dsem(GE, "s_xf1")]
    s_shw = dsem(GE, "s_shw")
    s_xt = [dsem(SY, "s_xt0"), dsem(SY, "s_xt1")]
    s_x1s = [dsem(SY, "s_x1s0"), dsem(SY, "s_x1s1")]
    s_tp = [dsem(SY, "s_tp0"), dsem(SY, "s_tp1")]
    s_ccb = dsem(SY, "s_ccb")
    s_msh = [dsem(SY, "s_msh0"), dsem(SY, "s_msh1")]
    s_idx = dsem(SY, "s_idx")
    s_wg = [dsem(GE, "s_wg0"), dsem(GE, "s_wg1")]
    s_wd = dsem(GE, "s_wd")
    s_gt = [dsem(GE, "s_gt0"), dsem(GE, "s_gt1")]
    s_sca = dsem(GE, "s_sca")
    s_rb = [dsem(SY, "s_rb0"), dsem(SY, "s_rb1")]
    s_dbg = [dsem(SY, "s_dbg0"), dsem(SY, "s_dbg1")]
    s_dbo = [dsem(SY, "s_dbo0"), dsem(SY, "s_dbo1")]
    s_ou = [dsem(SY, "s_ou0"), dsem(SY, "s_ou1")]
    s_te = Sem(nc, "s_te")
    s_ve = Sem(nc, "s_ve")
    s_se = Sem(nc, "s_se")
    s_ge = Sem(nc, "s_ge")
    s_cc = Sem(nc, "s_cc")
    s_b = Sem(nc, "s_b")
    p.auto = {VE: s_ve, SE: s_se, GE: s_ge}

    def barrier():
        for s in dma_sems[SY]:
            p.wait(SY, s, s.n)
        for s in dma_sems[GE]:
            p.wait(GE, s, s.n)
        p.wait(GE, s_ge, s_ge.n)
        targets = [(s_te, p.mark(TE, s_te)), (s_ve, s_ve.n), (s_se, s_se.n)]
        for eng, s in ((GE, s_ge), (SY, s_b)):
            v = s.inc()
            p.ops[eng].append(lambda e, ss=s.h: e.nop().then_inc(ss, 1))
            targets.append((s, v))
        for eng in ENGINES:
            for s, v in targets:
                p.wait(eng, s, v)

    def mm(ps_ap, lhsT_ap, rhs_ap, start, stop):
        p.on(TE, lambda e, o=ps_ap, l=lhsT_ap, r=rhs_ap, st=start, sp=stop:
             e.matmul(o, l, r, start=st, stop=sp, skip_group_check=True))

    def trans(ps_ap, in_ap):
        np_ = in_ap.shape[0]
        idt = ident16 if in_ap.dtype == BF16 else identf
        p.on(TE, lambda e, o=ps_ap, i=in_ap, d=idt[0:np_, 0:np_]:
             e.transpose(o, i, d))

    def vcopy(eng, out_ap, in_ap):
        p.on(eng, lambda e, o=out_ap, i=in_ap: e.tensor_copy(o, i))

    def scopy(out_ap, in_ap, scale=1.0):
        p.on(SE, lambda e, o=out_ap, i=in_ap, s=scale:
             e.activation(o, i, AF.Copy, bias=0.0, scale=s))

    p.dma(SY, identf[:, :], c_ident[:, :], s_cst)
    for o_ in range(4):
        p.dma(SY, mask4[:, o_, :], c_mask4[o_, :, :], s_cst)
    p.dma(SY, idsp1[:, :], c_idsp1[:, :], s_cst)
    p.dma(SY, rk1[:, :], c_rk1[:, :], s_cst)
    p.dma(SY, blk[:, :], c_blk[:, :], s_cst)
    p.dma(SY, blkT[:, :], c_blkT[:, :], s_cst)
    p.dma(SY, ones_1_16[:, :], c_ones_1_16[:, :], s_cst)
    w_consts = s_cst.n

    def load_fm(dst, src, ncols, nch, sem):
        v = 0
        for ch in range(nch):
            v = p.dma(GE, dst[:, ch * ncols:(ch + 1) * ncols],
                      src[ch * 128:(ch + 1) * 128, :], sem)
        return v

    nreg = [None, None]

    def _alloc_regs(e):
        nreg[0] = e.alloc_register("nidx0")
        nreg[1] = e.alloc_register("nidx1")
    p.on(GE, _alloc_regs)

    p.dma(GE, ones_col[:, :], c_ones_col[:, :], s_aw)
    load_fm(wdqT_sb, wdqT, 256, 8, s_aw)
    load_fm(wdkvT_sb, wdkvT, 256, 8, s_aw)
    load_fm(wuq_sb, wuq, 256, 8, s_aw)
    load_fm(wuk_sb, wuk, 256, 8, s_aw)
    load_fm(wuv_sb, wuv, 256, 8, s_aw)
    load_fm(wdq_sb, wdq, D, 2, s_aw)
    load_fm(woT_sb, woT, D, 8, s_aw)
    w_attn = load_fm(gwT_sb, gwT, 16, 8, s_aw)

    p.on(VE, lambda e: e.memset(c_eps[:, :], 1e-5))
    p.wait(VE, s_cst, w_consts)
    vcopy(VE, ident16[:, :], identf[:, :])
    v_ident = p.mark(VE, s_ve)

    p.wait(TE, s_aw, w_attn)
    p.wait(TE, s_ve, v_ident)
    for mc in range(2):
        for dc in range(8):
            mm(PS[mc][:, 0:256], wuq_sb[:, dc * 256 + mc * 128: dc * 256 + mc * 128 + 128],
               wuk_sb[:, dc * 256: dc * 256 + 256], dc == 0, dc == 7)
    te_b = p.mark(TE, s_te)
    p.wait(VE, s_te, te_b)
    for mc in range(2):
        vcopy(VE, b_sb[:, mc * 256:(mc + 1) * 256], PS[mc][:, 0:256])
    v_b = p.mark(VE, s_ve)

    p.wait(TE, s_ve, v_b)
    for mc8 in range(8):
        psn = PS[2 + (mc8 % 2)]
        for cc in range(2):
            mm(psn[:, 0:256], wdq_sb[:, cc * D + mc8 * 128: cc * D + mc8 * 128 + 128],
               b_sb[:, cc * 256:(cc + 1) * 256], cc == 0, cc == 1)
        te_k = p.mark(TE, s_te)
        p.wait(SE, s_te, te_k)
        scopy(keff_sb[:, mc8 * 256:(mc8 + 1) * 256], psn[:, 0:256], scale=0.125)
        se_k = p.mark(SE, s_se)
        if mc8 >= 1:
            p.wait(TE, s_se, se_k - 1)
    se_keff = s_se.n

    for mc in range(2):
        for ns in range(2):
            for dc in range(8):
                mm(PS[4 + ns][:, :], wuv_sb[:, dc * 256 + mc * 128: dc * 256 + mc * 128 + 128],
                   woT_sb[:, dc * D + ns * 512: dc * D + ns * 512 + 512], dc == 0, dc == 7)
        te_v = p.mark(TE, s_te)
        p.wait(VE, s_te, te_v)
        for ns in range(2):
            vcopy(VE, ve_sb[:, mc * D + ns * 512: mc * D + ns * 512 + 512],
                  PS[4 + ns][:, :])
        v_v = p.mark(VE, s_ve)
        p.wait(TE, s_ve, v_v)

    p.wait(TE, s_se, se_keff)
    for h in range(NH):
        pr = (h % 2) * 64
        dc = h // 2
        for mc in range(2):
            mm(PS[6][:, 0:256],
               wuq_sb[pr:pr + 64, dc * 256 + mc * 128: dc * 256 + mc * 128 + 128],
               keff_sb[pr:pr + 64, dc * 256: dc * 256 + 256], True, True)
            te_m = p.mark(TE, s_te)
            p.wait(SE, s_te, te_m)
            scopy(m_sb[:, h * 512 + mc * 256: h * 512 + mc * 256 + 256], PS[6][:, 0:256])
            se_m = p.mark(SE, s_se)
            p.wait(TE, s_se, se_m)

    te_a_done = p.mark(TE, s_te)
    barrier()

    b_te_done = [0] * T5
    for t5 in range(T5):
        buf = t5 % 2
        xf = xfm_t[:, buf]
        if t5 == 0:
            p.wait(GE, s_te, te_a_done)
        if t5 >= 2:
            p.wait(GE, s_te, b_te_done[t5 - 2])
        w_x = p.dma(GE, xf,
                    x_fmt[t5, :, :].rearrange("(dc p) q -> p dc q", p=128),
                    s_xf[buf])
        p.wait(TE, s_xf[buf], w_x)
        evs = []
        for which, (wsb, dst) in enumerate(((wdqT_sb, q1T), (wdkvT_sb, ckvT))):
            for mc in range(2):
                psn = PS[which * 2 + mc]
                for dc in range(8):
                    mm(psn[:, :], wsb[:, dc * 256 + mc * 128: dc * 256 + mc * 128 + 128],
                       xf[:, dc, :], dc == 0, dc == 7)
                te_p = p.mark(TE, s_te)
                ev = VE if (which * 2 + mc) % 2 == 0 else SE
                p.wait(ev, s_te, te_p)
                dap = dst[:, mc, t5 * 512: t5 * 512 + 512]
                if ev == VE:
                    vcopy(VE, dap, psn[:, :])
                else:
                    scopy(dap, psn[:, :])
                evs.append(ev)
        for sub in range(4):
            tc = t5 * 4 + sub
            for dc in range(8):
                mm(PS[4][:, 0:256], xf[:, dc, sub * 128: sub * 128 + 128],
                   wdkvT_sb[:, dc * 256: dc * 256 + 256], dc == 0, dc == 7)
            te_c = p.mark(TE, s_te)
            p.wait(SE, s_te, te_c)
            scopy(ckvtm[:, tc, :], PS[4][:, 0:256])
            se_c = p.mark(SE, s_se)
            p.wait(TE, s_se, se_c)
        v_e = p.mark(VE, s_ve)
        se_e = p.mark(SE, s_se)
        p.wait(TE, s_ve, v_e)
        p.wait(TE, s_se, se_e)
        b_te_done[t5] = p.mark(TE, s_te)

    barrier()
    p.ckpt("B")

    for h in range(NH):
        for mc in range(2):
            for t5 in range(T5):
                for cc in range(2):
                    mm(PS[6][:, :],
                       m_sb[:, h * 512 + cc * 256 + mc * 128: h * 512 + cc * 256 + mc * 128 + 128],
                       q1T[:, cc, t5 * 512: t5 * 512 + 512], cc == 0, cc == 1)
                te_q = p.mark(TE, s_te)
                p.wait(SE, s_te, te_q)
                scopy(qlatT[:, mc, t5 * 512: t5 * 512 + 512], PS[6][:, :])
                se_q = p.mark(SE, s_se)
                p.wait(TE, s_se, se_q)

        for qt in range(T // QT):
            nk = (qt + 1) * NKQ
            se_exp = [0] * nk
            for kc in range(nk):
                sps = PS[kc % 2]
                for lc in range(2):
                    mm(sps[:, :], ckvT[:, lc, kc * 128: kc * 128 + 128],
                       qlatT[:, lc, qt * QT: qt * QT + QT], lc == 0, lc == 1)
                te_s = p.mark(TE, s_te)
                diag_o = kc - qt * NKQ
                if diag_o >= 0:
                    p.wait(VE, s_te, te_s)
                    p.on(VE, lambda e, o=sps[:, :], b=mask4[:, diag_o, :]:
                         e.tensor_tensor(o, o, b, OP.add))
                    v_m = p.mark(VE, s_ve)
                    p.wait(SE, s_ve, v_m)
                else:
                    p.wait(SE, s_te, te_s)
                pt = pT[:, kc % 3]
                p.on(SE, lambda e, o=pt, i=sps[:, :]:
                     e.activation(o, i, AF.Exp, bias=0.0, scale=1.0))
                se_exp[kc] = p.mark(SE, s_se)
                p.wait(TE, s_se, se_exp[kc])
                for lc in range(2):
                    mm(PS[2 + lc][:, :], ckvtm[:, kc, lc * 128: lc * 128 + 128],
                       pt, kc == 0, kc == nk - 1)
                mm(PS[4][0:1, :], ones_col[:, :], pt, kc == 0, kc == nk - 1)
            te_pv = p.mark(TE, s_te)
            p.wait(VE, s_te, te_pv)
            for lc in range(2):
                vcopy(VE, ctxT[:, lc], PS[2 + lc][:, :])
            p.on(VE, lambda e, o=dn_row[0:1, :], i=PS[4][0:1, :]: e.reciprocal(o, i))
            v_cx = p.mark(VE, s_ve)
            p.wait(TE, s_ve, v_cx)
            for tcq in range(NKQ):
                trans(PS[5][:, tcq: tcq + 1],
                      dn_row[0:1, tcq * 128: tcq * 128 + 128])
            te_tr = p.mark(TE, s_te)
            p.wait(VE, s_te, te_tr)
            for tcq in range(NKQ):
                vcopy(VE, recip_pp[:, tcq: tcq + 1], PS[5][:, tcq: tcq + 1])
            v_rc = p.mark(VE, s_ve)
            p.wait(TE, s_ve, v_rc)
            for tcq in range(NKQ):
                for lc in range(2):
                    mm(PS[6][:, 0:64], ctxT[:, lc, tcq * 128: tcq * 128 + 128],
                       ve_sb[:, lc * D + h * 64: lc * D + h * 64 + 64], lc == 0, lc == 1)
                te_y = p.mark(TE, s_te)
                p.wait(VE, s_te, te_y)
                tcg = qt * NKQ + tcq
                p.on(VE, lambda e, o=y_sb[:, tcg, h * 64: h * 64 + 64],
                     i=PS[6][:, 0:64], s=recip_pp[:, tcq: tcq + 1]:
                     e.tensor_scalar(o, i, s, None, OP.mult))
                v_y = p.mark(VE, s_ve)
                p.wait(TE, s_ve, v_y)
    te_flash = p.mark(TE, s_te)
    barrier()
    p.ckpt("C")

    load_fm(sgT_sb.rearrange("p a b -> p (a b)"), sgT, IS, 8, s_shw)
    load_fm(suT_sb.rearrange("p a b -> p (a b)"), suT, IS, 8, s_shw)

    def emit_ln(src_f32_fn, tc, out_f32, out_bf16, extra_in=None):
        s1 = st_t[:, 0:1]; mean = st_t[:, 1:2]; s2 = st_t[:, 2:3]
        tt = st_t[:, 3:4]; rstd = st_t[:, 4:5]
        src_f32_fn(out_f32)
        p.on(VE, lambda e, o=s1, i=out_f32: e.reduce_sum(o, i, AX.X))
        p.on(VE, lambda e, o=mean, i=s1: e.tensor_scalar(o, i, 1.0 / D, None, OP.mult))
        p.on(VE, lambda e, o=lnjunk[:, :], i=out_f32, s=mean, b=out_f32, a=s2:
             e.scalar_tensor_tensor(o, i, s, b, OP.subtract, OP.mult, accum_out=a))
        v_st = p.mark(VE, s_ve)
        p.wait(SE, s_ve, v_st)
        p.on(SE, lambda e, o=tt, i=s2:
             e.activation(o, i, AF.Ln, bias=1e-5, scale=1.0 / D))
        p.on(SE, lambda e, o=rstd, i=tt:
             e.activation(o, i, AF.Exp, bias=0.0, scale=-0.5))
        se_r = p.mark(SE, s_se)
        p.wait(VE, s_se, se_r)
        p.on(VE, lambda e, o=out_f32, i=out_f32, s=mean, s2_=rstd:
             e.tensor_scalar(o, i, s, s2_, OP.subtract, OP.mult))
        if out_bf16 is not None:
            vcopy(VE, out_bf16, out_f32)

    d_ve_done = [0] * TC
    d_sy_done = [0] * TC
    for tc in range(TC):
        buf = tc % 2
        xt = xt_t[:, buf]
        x1f = x1f_t[:, buf]
        x1b = x1b_t[:, buf]
        if tc >= 2:
            p.wait(SY, s_ve, d_ve_done[tc - 2])
        w_xt = p.dma(SY, xt, x_tm[tc * 128:(tc + 1) * 128, :], s_xt[buf])
        p.wait(VE, s_xt[buf], w_xt)
        if tc >= 2:
            p.wait(VE, s_x1s[buf], d_sy_done[tc - 2])
        emit_ln(lambda o, tc_=tc, xt_=xt: p.on(
            VE, lambda e, oo=o, a=xt_, b=y_sb[:, tc_, :]:
            e.tensor_tensor(oo, a, b, OP.add)), tc, x1f, x1b)
        d_ve_done[tc] = p.mark(VE, s_ve)
        p.wait(SY, s_ve, d_ve_done[tc])
        p.dma(SY, x1f_dram[tc * 128:(tc + 1) * 128, :], x1f, s_x1s[buf])
        d_sy_done[tc] = p.dma(SY, x1bf_dram[tc * 128:(tc + 1) * 128, :], x1b,
                              s_x1s[buf])
    x1s_totals = [s_x1s[0].n, s_x1s[1].n]
    barrier()
    p.ckpt("D")
    load_fm(sdT_sb.rearrange("p a b -> p (a b)"), sdT, D, ISC, s_shw)
    w_shw = s_shw.n

    e_te_done = [0] * T5
    for t5 in range(T5):
        buf = t5 % 2
        xg5 = x1g_t[:, buf]
        if t5 >= 2:
            p.wait(SY, s_te, e_te_done[t5 - 2])
        if t5 == 0:
            p.wait(SY, s_x1s[0], x1s_totals[0])
            p.wait(SY, s_x1s[1], x1s_totals[1])
        w_tp = 0
        for dc in range(8):
            w_tp = p.dmat(SY, xg5[:, dc, :],
                          x1bf_dram[t5 * 512:(t5 + 1) * 512, dc * 128:(dc + 1) * 128],
                          s_tp[buf])
        p.wait(TE, s_tp[buf], w_tp)
        for sub in range(4):
            tc = t5 * 4 + sub
            for dc in range(8):
                mm(PS[0][:, 0:16], xg5[:, dc, sub * 128: sub * 128 + 128],
                   gwT_sb[:, dc * 16:(dc + 1) * 16], dc == 0, dc == 7)
            te_l = p.mark(TE, s_te)
            p.wait(VE, s_te, te_l)
            p.on(VE, lambda e, o=gat_s[:, 0:1], i=PS[0][:, 0:16]:
                 e.reduce_max(o, i, AX.X))
            p.on(VE, lambda e, o=logit_t[:, :], i=PS[0][:, 0:16], s=gat_s[:, 0:1]:
                 e.tensor_scalar(o, i, s, None, OP.subtract))
            v_l = p.mark(VE, s_ve)
            p.wait(SE, s_ve, v_l)
            p.on(SE, lambda e, o=sc_t[:, :], i=logit_t[:, :]:
                 e.activation(o, i, AF.Exp, bias=0.0, scale=1.0))
            se_l = p.mark(SE, s_se)
            p.wait(VE, s_se, se_l)
            p.on(VE, lambda e, o=gat_s[:, 1:2], i=sc_t[:, :]: e.reduce_sum(o, i, AX.X))
            p.on(VE, lambda e, o=gat_s[:, 2:3], i=gat_s[:, 1:2]: e.reciprocal(o, i))
            p.on(VE, lambda e, o=sc_t[:, :], i=sc_t[:, :], s=gat_s[:, 2:3]:
                 e.tensor_scalar(o, i, s, None, OP.mult))
            v_s = p.mark(VE, s_ve)
            p.wait(TE, s_ve, v_s)
            trans(PS[1][0:16, 0:128], sc_t[:, :])
            te_sc = p.mark(TE, s_te)
            p.wait(VE, s_te, te_sc)
            vcopy(VE, scT[0:16, tc * 128:(tc + 1) * 128], PS[1][0:16, 0:128])
            v_sc = p.mark(VE, s_ve)
            p.wait(TE, s_ve, v_sc)
        e_te_done[t5] = p.mark(TE, s_te)
    v_gate = s_ve.n
    p.wait(SY, s_ve, v_gate)
    w_ccin = p.dma(SY, cc_in[:, :], scT[0:16, :], s_ccb)
    p.wait(GE, s_ccb, w_ccin)
    v_cc = s_cc.inc()
    p._dma_raw(GE, lambda e: e.collective_compute(
        "AllGather", OP.bypass, replica_groups=[list(range(NR))],
        ins=[cc_in[:, :]], outs=[cc_out[:, :]]).then_inc(s_cc.h, 1))
    p.wait(SY, s_cc, v_cc)
    cc_v = cc_out.rearrange("(r e) j -> r e j", e=16)
    for e_ in range(E):
        p.dma(SY, gsc[e_ * (NR): (e_ + 1) * NR, :], cc_v[:, e_, :], s_ccb)
    w_gsc = s_ccb.n

    barrier()
    p.ckpt("E")

    for e_ in range(E):
        p.dma(SY, sloc[:, e_, :],
              cc_in[e_: e_ + 1, :].rearrange("o (q s) -> o q s", q=16), s_ccb)
    w_route = s_ccb.n
    p.wait(VE, s_ccb, w_route)
    p.on(VE, lambda e: e.memset(lo_v[:, :], 0.0))
    p.on(VE, lambda e: e.memset(hi_v[:, :], 1.0))
    p.on(VE, lambda e: e.memset(tau_pp[:, :], 0.5))
    p.on(VE, lambda e: e.memset(mid_v[:, 0:1], 0.5))
    p.mark(VE, s_ve)

    CSEL = float(cfg.CSEL)
    tau_ready = []

    def emit_bisect_iter(it):
        if it > 0:
            p.wait(VE, s_ve, tau_ready[it - 1])
        p.on(VE, lambda e, o=junk_bf[:, :], i=gsc[:, :], s=tau_pp[:, :], a=cnt_pp[:, :]:
             e.tensor_scalar(o, i, s, None, OP.is_ge, OP.add, accum_out=a))
        v_c = p.mark(VE, s_ve)
        p.wait(TE, s_ve, v_c)
        mm(PS[7][0:16, 0:1], blk[:, :], cnt_pp[:, :], True, True)
        te_s_ = p.mark(TE, s_te)
        p.wait(VE, s_te, te_s_)
        vcopy(VE, cnt_e[:, :], PS[7][0:16, 0:1])
        ge_ = mid_v[:, 1:2]; t0 = mid_v[:, 2:3]; t1 = mid_v[:, 3:4]; mid = mid_v[:, 0:1]
        p.on(VE, lambda e, o=ge_, i=cnt_e[:, :]:
             e.tensor_scalar(o, i, CSEL, None, OP.is_ge))
        p.on(VE, lambda e, o=t0, a=mid, b=lo_v[:, :]: e.tensor_tensor(o, a, b, OP.subtract))
        p.on(VE, lambda e, o=t0, a=t0, b=ge_: e.tensor_tensor(o, a, b, OP.mult))
        p.on(VE, lambda e, o=lo_v[:, :], a=lo_v[:, :], b=t0: e.tensor_tensor(o, a, b, OP.add))
        p.on(VE, lambda e, o=t1, a=mid, b=hi_v[:, :]: e.tensor_tensor(o, a, b, OP.subtract))
        p.on(VE, lambda e, o=t0, i=ge_: e.tensor_scalar(o, i, -1.0, 1.0, OP.mult, OP.add))
        p.on(VE, lambda e, o=t1, a=t1, b=t0: e.tensor_tensor(o, a, b, OP.mult))
        p.on(VE, lambda e, o=hi_v[:, :], a=hi_v[:, :], b=t1: e.tensor_tensor(o, a, b, OP.add))
        p.on(VE, lambda e, o=mid, a=lo_v[:, :], b=hi_v[:, :]: e.tensor_tensor(o, a, b, OP.add))
        p.on(VE, lambda e, o=mid, i=mid: e.tensor_scalar(o, i, 0.5, None, OP.mult))
        v_u = p.mark(VE, s_ve)
        p.wait(TE, s_ve, v_u)
        mm(PS[7][:, 0:1], blkT[:, :], mid, True, True)
        te_b_ = p.mark(TE, s_te)
        p.wait(VE, s_te, te_b_)
        vcopy(VE, tau_pp[:, :], PS[7][:, 0:1])
        tau_ready.append(p.mark(VE, s_ve))

    f_te_done = [0] * T5
    sh_hist = []

    def emit_shared_t5(t5):
        buf = t5 % 2
        xs5 = x1s_t[:, buf]
        if t5 == 0:
            p.wait(SY, s_ccb, w_route)
        if t5 >= 2:
            p.wait(SY, s_te, f_te_done[t5 - 2])
        w_tp = 0
        for dc in range(8):
            w_tp = p.dmat(SY, xs5[:, dc, :],
                          x1bf_dram[t5 * 512:(t5 + 1) * 512, dc * 128:(dc + 1) * 128],
                          s_tp[buf])
        p.wait(TE, s_tp[buf], w_tp)
        if t5 == 0:
            p.wait(TE, s_shw, w_shw)
        for ic in range(ISC):
            for which, wsb in ((0, sgT_sb), (1, suT_sb)):
                for dc in range(8):
                    mm(PS[which][:, :], wsb[:, dc, ic * 128: ic * 128 + 128],
                       xs5[:, dc, :], dc == 0, dc == 7)
            te_g = p.mark(TE, s_te)
            p.wait(SE, s_te, te_g)
            p.on(SE, lambda e, o=gtmp[:, :], i=PS[0][:, :]:
                 e.activation(o, i, AF.Gelu, bias=0.0, scale=1.0))
            se_g = p.mark(SE, s_se)
            p.wait(VE, s_se, se_g)
            p.on(VE, lambda e, o=hs_sb[:, ic, :], a=gtmp[:, :], b=PS[1][:, :]:
                 e.tensor_tensor(o, a, b, OP.mult))
            v_h = p.mark(VE, s_ve)
            p.wait(TE, s_ve, v_h)
        for tcq in range(4):
            for ns in range(2):
                for ic in range(ISC):
                    mm(PS[2 + ns][:, :], hs_sb[:, ic, tcq * 128: tcq * 128 + 128],
                       sdT_sb[:, ic, ns * 512: ns * 512 + 512], ic == 0, ic == ISC - 1)
            te_d = p.mark(TE, s_te)
            p.wait(VE, s_te, te_d)
            sbuf_ = tcq % 2
            sh = sh_t[:, sbuf_]
            if len(sh_hist) >= 2:
                slot, vv = sh_hist[-2]
                p.wait(VE, s_msh[slot], vv)
            for ns in range(2):
                vcopy(VE, sh[:, ns * 512:(ns + 1) * 512], PS[2 + ns][:, :])
            v_sh = p.mark(VE, s_ve)
            p.wait(TE, s_ve, v_sh)
            p.wait(SY, s_ve, v_sh)
            tcg = t5 * 4 + tcq
            slot = len(sh_hist) % 2
            sh_hist.append(
                (slot, p.dma(SY, moe_dram[tcg * 128:(tcg + 1) * 128, :], sh,
                             s_msh[slot])))
        f_te_done[t5] = p.mark(TE, s_te)

    for u in range(max(cfg.BIS, T5)):
        if u < cfg.BIS:
            emit_bisect_iter(u)
        if u < T5:
            emit_shared_t5(u)
    msh_totals = [s_msh[0].n, s_msh[1].n]

    p.wait(TE, s_ve, s_ve.n)
    trans(PS[7][0:1, 0:16], lo_v[:, :])
    te_lr = p.mark(TE, s_te)
    p.wait(VE, s_te, te_lr)
    vcopy(VE, lo_row[0:1, :], PS[7][0:1, 0:16])
    v_lr = p.mark(VE, s_ve)
    p.wait(TE, s_ve, v_lr)
    mm(PS[7][0:16, 0:16], ones_1_16[0:1, :], lo_row[0:1, :], True, True)
    te_t16 = p.mark(TE, s_te)
    p.wait(VE, s_te, te_t16)
    vcopy(VE, tau16[:, :], PS[7][0:16, 0:16])
    v_tau = p.mark(VE, s_ve)

    barrier()
    p.ckpt("F")

    p._dma_raw(GE, lambda e: e.load_library(library_config.sparse_gather))
    p.ckpt("Ga")
    sg_done = [0] * E
    cast_done = [0] * E
    for e_ in range(E):
        sl = e_ % 2
        if e_ >= 2:
            p.wait(VE, s_ge, sg_done[e_ - 2])
        p.on(VE, lambda e, o=mids[:, sl, :], i=sloc[:, e_, :],
             s=tau16[:, e_: e_ + 1]: e.tensor_scalar(o, i, s, None, OP.is_ge))
        p.on(VE, lambda e, o=mids[:, sl, :], a=mids[:, sl, :], b=idsp1[:, :]:
             e.tensor_tensor(o, a, b, OP.mult))
        p.on(VE, lambda e, o=mids[:, sl, :], i=mids[:, sl, :]:
             e.tensor_scalar(o, i, -1.0, None, OP.add))
        v_m = p.mark(VE, s_ve)
        p.wait(GE, s_ve, v_m)
        if e_ >= 2:
            p.wait(GE, s_ve, cast_done[e_ - 2])
        p.on(GE, lambda e, o=comp[:, sl, :], i=mids[:, sl, :],
             n=nf_u[0:1, e_: e_ + 1]: e.sparse_gather(o, i, num_found=n))
        sg_done[e_] = p.mark(GE, s_ge)
        p.wait(VE, s_ge, sg_done[e_])
        vcopy(VE, idx16[0:16, e_, :], comp[:, sl, 0:NIDX])
        cast_done[e_] = p.mark(VE, s_ve)
        if e_ == 0:
            p.ckpt("Gb")
    p.ckpt("Gc")
    p._dma_raw(GE, lambda e: e.load_library(library_config.mlp))
    p.on(GE, lambda e: e.nop())
    ge_mlp = s_ge.n
    p.ckpt("Gd")
    p.wait(VE, s_ge, sg_done[E - 1])
    vcopy(VE, nf_f[:, :], nf_u[:, :])
    p.on(VE, lambda e, o=nf_f[:, :], i=nf_f[:, :]:
         e.tensor_scalar(o, i, float(CAP), None, OP.min))
    vcopy(VE, nf_i[:, :], nf_f[:, :])
    v_nf = p.mark(VE, s_ve)
    p.wait(TE, s_ve, v_nf)
    mm(PS[7][0:16, 0:16], ones_1_16[0:1, :], nf_f[0:1, :], True, True)
    te_nf = p.mark(TE, s_te)
    p.wait(VE, s_te, te_nf)
    vcopy(VE, nf16[:, :], PS[7][0:16, 0:16])
    for e_ in range(E):
        p.on(VE, lambda e, o=mwin[:, :], i=rk1[:, :], s=nf16[:, e_: e_ + 1]:
             e.tensor_scalar(o, i, s, None, OP.is_le))
        p.on(VE, lambda e, o=i16tmp[:, :], a=idx16[0:16, e_, :],
             b=mwin[:, :]: e.tensor_tensor(o, a, b, OP.mult))
        p.on(VE, lambda e, o=idx16[0:16, e_, :], a=mwin[:, :],
             b=i16tmp[:, :]:
             e.scalar_tensor_tensor(o, a, -1.0, b, OP.add, OP.add))
    v_idx = p.mark(VE, s_ve)
    p.ckpt("Ge")
    p.wait(SY, s_tp[0], s_tp[0].n)
    p.wait(SY, s_tp[1], s_tp[1].n)
    p.wait(SY, s_ve, v_idx)
    i16f = idx16.rearrange("p a b -> p (a b)")
    w_r1 = p.dma(SY, i16f[16:32, :], i16f[0:16, :], s_idx)
    p.wait(SY, s_idx, w_r1)
    w_r2 = p.dma(SY, i16f[32:64, :], i16f[0:32, :], s_idx)
    p.wait(SY, s_idx, w_r2)
    w_idxrep = p.dma(SY, i16f[64:128, :], i16f[0:64, :], s_idx)
    p.wait(SY, s_idx, w_idxrep)
    p.ckpt("Gf")

    def fetch_wgwu(e_, isl, buf, wait_te=None):
        if wait_te is not None:
            p.wait(GE, s_te, wait_te)
        p.dma(GE, wg_buf[:, buf].rearrange("p a b -> p (a b)"),
              wgT_p[e_, isl].rearrange("p a b -> p (a b)"), s_wg[buf])
        p.dma(GE, wu_buf[:, buf].rearrange("p a b -> p (a b)"),
              wuT_p[e_, isl].rearrange("p a b -> p (a b)"), s_wg[buf])
        return (buf, s_wg[buf].n)

    def fetch_wd(e_):
        for ws in range(I // 512):
            p.dma(GE, wd_buf[:, ws * 4:(ws + 1) * 4].rearrange("p a b -> p (a b)"),
                  wdT_p[e_, ws].rearrange("p a b -> p (a b)"), s_wd)
        return s_wd.n

    p.ckpt("G0")
    wgwu_ready = {(0, 0): fetch_wgwu(0, 0, 0)}
    wd_ready = [None] * E
    wd_ready[0] = fetch_wd(0)

    NSPL = [(0, 512), (512, CAP - 512)] if CAP > 512 else [(0, CAP)]
    gat_ready = [None] * E
    reg_n = [None] * E

    def emit_gather(e_, wait_te=None):
        if wait_te is not None:
            p.wait(GE, s_te, wait_te)
        p.wait(GE, s_ge, ge_mlp)
        p.wait(GE, s_ve, v_idx)
        p.wait(GE, s_idx, w_idxrep)

        def ld(eng, ee=e_):
            eng.load(nreg[ee % 2], nf_i[0:1, ee: ee + 1])
            reg_n[ee] = nreg[ee % 2]
        p.on(GE, ld)
        buf = e_ % 2
        gat_ready[e_] = s_gt[buf].inc(16)
        p._dma_raw(GE, lambda e, b=buf, ee=e_:
                   e.dma_gather(xg_sb[:, b], x1bf_dram[:, :],
                                i16f[:, ee * NIDX: (ee + 1) * NIDX],
                                CAP, reg_n[ee], D,
                                transpose=True).then_inc(s_gt[b].h, 16))

    p.wait(GE, s_x1s[0], x1s_totals[0])
    p.wait(GE, s_x1s[1], x1s_totals[1])
    emit_gather(0)
    p.ckpt("G")

    te_after_up = [0] * E
    te_after_down = [0] * E
    for e_ in range(E):
        xg = xg_sb[:, e_ % 2]
        p.wait(TE, s_gt[e_ % 2], gat_ready[e_])
        for isl in range(nIS):
            buf = isl % 2
            nxt = (e_, isl + 1) if isl + 1 < nIS else \
                ((e_ + 1, 0) if e_ + 1 < E else None)
            wbuf_, wv_ = wgwu_ready[(e_, isl)]
            p.wait(TE, s_wg[wbuf_], wv_)
            for mc in range(IB // 128):
                img = isl * (IB // 128) + mc
                for (n0, nw) in NSPL:
                    for which, wb in ((0, wg_buf), (1, wu_buf)):
                        for dc in range(8):
                            mm(PS[which][:, 0:nw],
                               wb[:, buf, dc, mc * 128: mc * 128 + 128],
                               xg[:, dc, n0: n0 + nw], dc == 0, dc == 7)
                    te_u = p.mark(TE, s_te)
                    p.wait(SE, s_te, te_u)
                    p.on(SE, lambda e, o=moegelu[:, 0:nw], i=PS[0][:, 0:nw]:
                         e.activation(o, i, AF.Gelu, bias=0.0, scale=1.0))
                    se_u = p.mark(SE, s_se)
                    p.wait(VE, s_se, se_u)
                    p.on(VE, lambda e, o=h_sb[:, img, n0: n0 + nw],
                         a=moegelu[:, 0:nw], b=PS[1][:, 0:nw]:
                         e.tensor_tensor(o, a, b, OP.mult))
                    v_u = p.mark(VE, s_ve)
                    p.wait(TE, s_ve, v_u)
            if nxt is not None and nxt not in wgwu_ready:
                wgwu_ready[nxt] = fetch_wgwu(nxt[0], nxt[1], (buf + 1) % 2,
                                             wait_te=s_te.n)
        te_after_up[e_] = p.mark(TE, s_te)
        if e_ + 1 < E:
            emit_gather(e_ + 1, wait_te=(te_after_down[e_ - 1] if e_ >= 1 else None))
        p.wait(TE, s_wd, wd_ready[e_])
        for tcq in range(NCAP):
            for dc in range(8):
                mm(PS[4][:, 0:16], xg[:, dc, tcq * 128: tcq * 128 + 128],
                   gwT_sb[:, dc * 16:(dc + 1) * 16], dc == 0, dc == 7)
            te_gl = p.mark(TE, s_te)
            p.wait(VE, s_te, te_gl)
            p.on(VE, lambda e, o=gat_s[:, 0:1], i=PS[4][:, 0:16]:
                 e.reduce_max(o, i, AX.X))
            p.on(VE, lambda e, o=gat_t[:, :], i=PS[4][:, 0:16], s=gat_s[:, 0:1]:
                 e.tensor_scalar(o, i, s, None, OP.subtract))
            v_gl = p.mark(VE, s_ve)
            p.wait(SE, s_ve, v_gl)
            p.on(SE, lambda e, o=gat_t[:, :], i=gat_t[:, :]:
                 e.activation(o, i, AF.Exp, bias=0.0, scale=1.0))
            se_gl = p.mark(SE, s_se)
            p.wait(VE, s_se, se_gl)
            p.on(VE, lambda e, o=gat_s[:, 1:2], i=gat_t[:, :]: e.reduce_sum(o, i, AX.X))
            p.on(VE, lambda e, o=gat_s[:, 2:3], i=gat_s[:, 1:2]: e.reciprocal(o, i))
            p.on(VE, lambda e, o=gat_s[:, 3:4], a=gat_t[:, e_: e_ + 1], b=gat_s[:, 2:3]:
                 e.tensor_tensor(o, a, b, OP.mult))
            v_gt = p.mark(VE, s_ve)
            for ns in range(2):
                for ic in range(IC):
                    mm(PS[5 + ns][:, :], h_sb[:, ic, tcq * 128: tcq * 128 + 128],
                       wd_buf[:, ic, ns * 512: ns * 512 + 512], ic == 0, ic == IC - 1)
            te_d = p.mark(TE, s_te)
            p.wait(VE, s_te, te_d)
            for ns in range(2):
                p.on(VE, lambda e, o=eo_sb[:, tcq, ns * 512: ns * 512 + 512],
                     i=PS[5 + ns][:, :], s=gat_s[:, 3:4]:
                     e.tensor_scalar(o, i, s, None, OP.mult))
            v_eo = p.mark(VE, s_ve)
            p.wait(TE, s_ve, v_eo)
        te_after_down[e_] = p.mark(TE, s_te)
        v_eo_done = s_ve.n
        if e_ + 1 < E:
            p.wait(GE, s_ve, v_eo_done)
            wd_ready[e_ + 1] = fetch_wd(e_ + 1)
        p.wait(GE, s_ve, v_eo_done)
        if e_ == 0:
            p.wait(GE, s_msh[0], msh_totals[0])
            p.wait(GE, s_msh[1], msh_totals[1])
        v_sc_ = s_sca.inc(16)
        p._dma_raw(GE, lambda e, ee=e_:
                   e.dma_scatter_add(moe_dram[:, :], eo_sb,
                                     i16f[:, ee * NIDX: (ee + 1) * NIDX],
                                     CAP, reg_n[ee], D).then_inc(s_sca.h, 16))
        p.wait(GE, s_sca, v_sc_)
    scatters_done = s_sca.n
    barrier()
    p.ckpt("H")

    p.wait(SY, s_sca, scatters_done)
    i_ve_done = [0] * TC
    ou_done = [0] * TC
    for tc in range(TC):
        buf = tc % 2
        mo = mo_t[:, buf]
        xf2 = xf2_t[:, buf]
        ot = o_t[:, buf]
        if tc >= 2:
            p.wait(SY, s_ve, i_ve_done[tc - 2])
        p.dma(SY, mo, moe_dram[tc * 128:(tc + 1) * 128, :], s_rb[buf])
        w_rb = p.dma(SY, xf2, x1f_dram[tc * 128:(tc + 1) * 128, :], s_rb[buf])
        p.wait(VE, s_rb[buf], w_rb)
        if tc >= 2:
            p.wait(VE, s_ou[buf], ou_done[tc - 2])
        emit_ln(lambda o, m_=mo, x_=xf2: p.on(
            VE, lambda e, oo=o, a=x_, b=m_: e.tensor_tensor(oo, a, b, OP.add)),
            tc, ot, None)
        i_ve_done[tc] = p.mark(VE, s_ve)
        p.wait(SY, s_ve, i_ve_done[tc])
        ou_done[tc] = p.dma(SY, out[tc * 128:(tc + 1) * 128, :], ot, s_ou[buf])
    p.wait(SY, s_ou[0], s_ou[0].n)
    p.wait(SY, s_ou[1], s_ou[1].n)

    if cfg.stop:
        n = p.ckpts[cfg.stop]
        for e in ENGINES:
            p.ops[e] = p.ops[e][:n[e]]
        srcd = x1f_dram if cfg.stop in ("D", "E", "F", "G", "G0", "Ga", "Gb", "Gc", "Gd", "Ge", "Gf") else \
            (moe_dram if cfg.stop == "H" else x1bf_dram)
        dbg_done = [0] * TC
        for tc in range(TC):
            buf = tc % 2
            ot = o_t[:, buf]
            if tc >= 2:
                p.wait(SY, s_dbo[buf], dbg_done[tc - 2])
            if cfg.stop in ("B", "C"):
                p.wait(SY, s_dbg[buf], s_dbg[buf].n)
                w = p.dma(SY, ot[0:128, 0:1], x_tm[tc * 128:(tc + 1) * 128, 0:1],
                          s_dbg[buf])
            else:
                w = p.dma(SY, ot, srcd[tc * 128:(tc + 1) * 128, :], s_dbg[buf])
            p.wait(SY, s_dbg[buf], w)
            dbg_done[tc] = p.dma(SY, out[tc * 128:(tc + 1) * 128, :], ot,
                                 s_dbo[buf])
        p.wait(SY, s_dbo[0], s_dbo[0].n)
        p.wait(SY, s_dbo[1], s_dbo[1].n)

    p.run()
    lower_extended_insts(nc)
    return nc


def kernel(**inputs):
    cfg = Cfg()
    com, cores = host_prep(cfg, inputs)
    nc = build(cfg)
    in_maps = []
    for b in range(cfg.n_cores):
        m = dict(com)
        m.update(cores[b])
        in_maps.append(m)
    res = run_bass_kernel_spmd(nc, in_maps, core_ids=list(range(cfg.n_cores)))
    outs = [np.asarray(r["out"]) for r in res.results]
    return np.stack(outs, axis=0)


# version 04
# speedup vs baseline: 1.4450x; 1.4450x over previous
import numpy as np
import concourse.bass as bass
import concourse.mybir as mybir
from concourse.bass_utils import run_bass_kernel_spmd
from concourse import library_config
from concourse.library_overlay import lower_extended_insts

F32 = mybir.dt.float32
BF16 = mybir.dt.bfloat16
I16 = mybir.dt.int16
I32 = mybir.dt.int32
U32 = mybir.dt.uint32
AX = mybir.AxisListType
OP = mybir.AluOpType
AF = mybir.ActivationFunctionType

TE, VE, SE, GE, SY = "tensor", "vector", "scalar", "gpsimd", "sync"
ENGINES = (TE, VE, SE, GE, SY)

PH = {n: i for i, n in enumerate("ABCDEFGHI")}


class Cfg:
    def __init__(self, T=4096, I=4096, IS=2048, CAP=640, BIS=36, n_cores=8):
        self.T = T
        self.D = 1024
        self.NH = 16
        self.HS = 64
        self.KVL = 256
        self.E = 16
        self.I = I
        self.IS = IS
        self.CAP = CAP
        self.BIS = BIS
        self.n_cores = n_cores
        self.CSEL = n_cores * T * 2 // 16
        self.QT = 512
        self.IB = min(256, I)
        import os
        self.stop = os.environ.get("KSTOP", "")
        assert T % 512 == 0 and I % self.IB == 0 and CAP % 128 == 0


class Sem:
    def __init__(self, nc, name):
        self.h = nc.semaphore(name).__enter__()
        self.n = 0

    def inc(self, k=1):
        self.n += k
        return self.n


class Arena:

    def __init__(self, nc, name, dtype, esz):
        self.nc, self.name, self.dtype, self.esz = nc, name, dtype, esz
        self.reqs = []
        self.views = {}

    def alloc(self, name, shape, ph0, ph1):
        self.reqs.append((name, shape, PH[ph0], PH[ph1]))
        return name

    def finalize(self):
        placed = []
        for name, shape, p0, p1 in sorted(self.reqs, key=lambda r: -np.prod(r[1])):
            cols = int(np.prod(shape[1:]))
            cands = sorted({0} | {off + c for off, c, q0, q1, *_ in placed
                                  if not (q1 < p0 or p1 < q0)})
            for off in cands:
                ok = True
                for o2, c2, q0, q1, *_ in placed:
                    if q1 < p0 or p1 < q0:
                        continue
                    if not (off + cols <= o2 or o2 + c2 <= off):
                        ok = False
                        break
                if ok:
                    placed.append((off, cols, p0, p1, name, shape))
                    break
        total = max(o + c for o, c, *_ in placed) if placed else 0
        import os
        if os.environ.get("ARENA_DEBUG"):
            print(f"arena {self.name}: total {total} cols = {total*self.esz} B/part")
            for off, cols, p0, p1, name, shape in sorted(placed):
                print(f"  {off:7d}..{off+cols:7d} ph{p0}-{p1} {name} {shape}")
        base = self.nc.alloc_sbuf_tensor(self.name, [128, total], self.dtype).ap()
        for off, cols, p0, p1, name, shape in placed:
            v = base[0:shape[0], off:off + cols]
            if len(shape) > 2:
                pat = "p (" + " ".join(f"a{i}" for i in range(len(shape) - 1)) + ") -> p " \
                      + " ".join(f"a{i}" for i in range(len(shape) - 1))
                kw = {f"a{i}": shape[i + 1] for i in range(len(shape) - 1)}
                v = v.rearrange(pat, **kw)
            self.views[name] = v
        self.total_bytes = total * self.esz
        return self.views


class Prog:
    def __init__(self, nc):
        self.nc = nc
        self.ops = {e: [] for e in ENGINES}
        self.last = {e: None for e in ENGINES}
        self.n_real = {e: 0 for e in ENGINES}
        self.mark_at = {e: -1 for e in ENGINES}
        self.mark_vals = {e: {} for e in ENGINES}
        self.auto = {}

    def on(self, eng, fn):
        a = self.auto.get(eng)
        if a is not None:
            prev = a.n
            self.ops[eng].append(lambda e, s=a.h, vv=prev: e.wait_ge(s, vv))
            v = a.inc()

            def wrapped(e, f=fn, eng_=eng, s=a.h):
                r = f(e)
                if r is not None:
                    self.last[eng_] = r
                    r.then_inc(s, 1)
                else:
                    e.nop().then_inc(s, 1)
            self.ops[eng].append(wrapped)
            self.n_real[eng] += 1
            return
        def wrapped(e, f=fn, eng_=eng):
            r = f(e)
            if r is not None:
                self.last[eng_] = r
        self.ops[eng].append(wrapped)
        self.n_real[eng] += 1

    def wait(self, eng, sem, val=None):
        v = sem.n if val is None else val
        self.ops[eng].append(lambda e, s=sem.h, vv=v: e.wait_ge(s, vv))

    def mark(self, eng, sem):
        a = self.auto.get(eng)
        if a is not None:
            assert sem is a, (eng, "marks on auto engines use the auto sem")
            return a.n
        if self.mark_at[eng] == self.n_real[eng]:
            if sem in self.mark_vals[eng]:
                return self.mark_vals[eng][sem]
            o_sem, o_v = next(iter(self.mark_vals[eng].items()))
            self.wait(eng, o_sem, o_v)
            v = sem.inc()
            self.ops[eng].append(lambda e, s=sem.h: e.nop().then_inc(s, 1))
            self.mark_vals[eng][sem] = v
            return v
        v = sem.inc()
        self.ops[eng].append(lambda e, s=sem.h, eng_=eng:
                             self.last[eng_].then_inc(s, 1))
        self.mark_at[eng] = self.n_real[eng]
        self.mark_vals[eng] = {sem: v}
        return v

    def ckpt(self, tag):
        if not hasattr(self, "ckpts"):
            self.ckpts = {}
        self.ckpts[tag] = {e: len(self.ops[e]) for e in ENGINES}

    def _dma_raw(self, eng, fn):
        def wrapped(e, f=fn, eng_=eng):
            r = f(e)
            if r is not None:
                self.last[eng_] = r
        self.ops[eng].append(wrapped)
        self.n_real[eng] += 1

    def dma(self, eng, out_ap, in_ap, sem, **kw):
        v = sem.inc(16)
        self._dma_raw(eng, lambda e, o=out_ap, i=in_ap, s=sem.h, k=dict(kw):
                      e.dma_start(out=o, in_=i, **k).then_inc(s, 16))
        return v

    def dmat(self, eng, out_ap, in_ap, sem):
        v = sem.inc(16)
        self._dma_raw(eng, lambda e, o=out_ap, i=in_ap, s=sem.h:
                      e.dma_start_transpose(o, i).then_inc(s, 16))
        return v

    def run(self):
        nc = self.nc
        with nc.Block() as block:
            @block.tensor
            def _(eng):
                for f in self.ops[TE]:
                    f(eng)

            @block.vector
            def _(eng):
                for f in self.ops[VE]:
                    f(eng)

            @block.scalar
            def _(eng):
                for f in self.ops[SE]:
                    f(eng)

            @block.gpsimd
            def _(eng):
                for f in self.ops[GE]:
                    f(eng)

            @block.sync
            def _(eng):
                for f in self.ops[SY]:
                    f(eng)


def host_prep(cfg, inputs):
    T, D, I, IS, E, IB = cfg.T, cfg.D, cfg.I, cfg.IS, cfg.E, cfg.IB
    f = np.ascontiguousarray
    x = np.asarray(inputs["L0_attn_x"], dtype=np.float32)
    com = {}
    com["wdqT"] = f(np.asarray(inputs["L0_attn_W_dq"], np.float32).T)
    com["wdq"] = f(np.asarray(inputs["L0_attn_W_dq"], np.float32))
    com["wuq"] = f(np.asarray(inputs["L0_attn_W_uq"], np.float32))
    com["wdkvT"] = f(np.asarray(inputs["L0_attn_W_dkv"], np.float32).T)
    com["wuk"] = f(np.asarray(inputs["L0_attn_W_uk"], np.float32))
    com["wuv"] = f(np.asarray(inputs["L0_attn_W_uv"], np.float32))
    com["woT"] = f(np.asarray(inputs["L0_attn_W_o"], np.float32).T)
    com["gwT"] = f(np.asarray(inputs["L0_ffn_gate_w"], np.float32).T)
    com["sgT"] = f(np.asarray(inputs["L0_ffn_Sg"], np.float32).T)
    com["suT"] = f(np.asarray(inputs["L0_ffn_Su"], np.float32).T)
    com["sdT"] = f(np.asarray(inputs["L0_ffn_Sd"], np.float32).T)
    wg = np.asarray(inputs["L0_ffn_Wg"], np.float32)
    wu = np.asarray(inputs["L0_ffn_Wu"], np.float32)
    wd = np.asarray(inputs["L0_ffn_Wd"], np.float32)
    nIS = I // IB
    com["wgT_p"] = f(wg.reshape(E, nIS, IB, 8, 128).transpose(0, 1, 4, 3, 2))
    com["wuT_p"] = f(wu.reshape(E, nIS, IB, 8, 128).transpose(0, 1, 4, 3, 2))
    nWS = I // 512
    com["wdT_p"] = f(wd.transpose(0, 2, 1).reshape(E, nWS, 4, 128, D)
                     .transpose(0, 1, 3, 2, 4))
    com["c_ident"] = np.eye(128, dtype=np.float32)
    m4 = np.zeros((4, 128, cfg.QT), dtype=np.float32)
    r = np.arange(128)[:, None]
    c = np.arange(cfg.QT)[None, :]
    for o in range(4):
        m4[o][(o * 128 + r) > c] = -1e30
    com["c_mask4"] = m4
    com["c_ones_col"] = np.ones((128, 1), dtype=np.float32)
    SX = T // 16
    ids = np.arange(16)[:, None] * SX + np.arange(SX)[None, :]
    com["c_idsp1"] = (ids + 1).astype(np.float32)
    NIDX = cfg.CAP // 16
    rk = np.arange(NIDX)[None, :] * 16 + np.arange(16)[:, None]
    com["c_rk1"] = (rk + 1).astype(np.float32)
    blkm = (np.arange(128)[:, None] // 8 == np.arange(16)[None, :]).astype(np.float32)
    com["c_blk"] = f(blkm)
    com["c_blkT"] = f(blkm.T)
    com["c_ones_1_16"] = np.ones((1, 16), dtype=np.float32)
    cores = []
    for b in range(cfg.n_cores):
        xb = f(x[b])
        cores.append({
            "x_tm": xb,
            "x_fmt": f(xb.T.reshape(D, T // 512, 512).transpose(1, 0, 2)),
        })
    return com, cores


def build(cfg):
    T, D, I, IS, E, CAP, IB, QT = (cfg.T, cfg.D, cfg.I, cfg.IS, cfg.E,
                                   cfg.CAP, cfg.IB, cfg.QT)
    NH = cfg.NH
    TC = T // 128
    T5 = T // 512
    nIS = I // IB
    IC = I // 128
    ISC = IS // 128
    NCAP = CAP // 128
    SX = T // 16
    NR = cfg.n_cores
    NKQ = QT // 128

    nc = bass.Bass("TRN2", target_bir_lowering=False, debug=False)

    def din(name, shape):
        return nc.dram_tensor(name, list(shape), F32, kind="ExternalInput").ap()

    x_tm = din("x_tm", (T, D))
    x_fmt = din("x_fmt", (T5, D, 512))
    wdqT = din("wdqT", (D, 256)); wdq = din("wdq", (256, D))
    wuq = din("wuq", (D, 256)); wdkvT = din("wdkvT", (D, 256))
    wuk = din("wuk", (D, 256)); wuv = din("wuv", (D, 256))
    woT = din("woT", (D, D)); gwT = din("gwT", (D, 16))
    sgT = din("sgT", (D, IS)); suT = din("suT", (D, IS)); sdT = din("sdT", (IS, D))
    wgT_p = din("wgT_p", (E, nIS, 128, 8, IB))
    wuT_p = din("wuT_p", (E, nIS, 128, 8, IB))
    wdT_p = din("wdT_p", (E, I // 512, 128, 4, D))
    c_ident = din("c_ident", (128, 128))
    c_mask4 = din("c_mask4", (4, 128, QT))
    c_ones_col = din("c_ones_col", (128, 1))
    c_idsp1 = din("c_idsp1", (16, SX))
    c_rk1 = din("c_rk1", (16, CAP // 16))
    c_blk = din("c_blk", (128, 16))
    c_blkT = din("c_blkT", (16, 128))
    c_ones_1_16 = din("c_ones_1_16", (1, 16))

    out = nc.dram_tensor("out", [T, D], F32, kind="ExternalOutput").ap()
    x1bf_dram = nc.dram_tensor("x1bf_dram", [T, D], BF16, kind="Internal").ap()
    x1f_dram = nc.dram_tensor("x1f_dram", [T, D], F32, kind="Internal").ap()
    moe_dram = nc.dram_tensor("moe_dram", [T, D], F32, kind="Internal").ap()
    cc_in = nc.dram_tensor("cc_in", [16, T], F32, kind="Internal").ap()
    cc_out = nc.dram_tensor("cc_out", [16 * NR, T], F32, kind="Internal").ap()

    a16 = Arena(nc, "a16", BF16, 2)
    a32 = Arena(nc, "a32", F32, 4)

    def A16(name, shape, ph0, ph1):
        return a16.alloc(name, shape, ph0, ph1)

    def A32(name, shape, ph0, ph1):
        return a32.alloc(name, shape, ph0, ph1)

    A16("ident16", (128, 128), "A", "I")
    A16("ones_col", (128, 1), "A", "I")
    A16("gwT_sb", (128, 8 * 16), "A", "I")
    A16("wdqT_sb", (128, 8 * 256), "A", "B")
    A16("wdkvT_sb", (128, 8 * 256), "A", "B")
    A16("wuq_sb", (128, 8 * 256), "A", "A")
    A16("wuk_sb", (128, 8 * 256), "A", "A")
    A16("wuv_sb", (128, 8 * 256), "A", "A")
    A16("wdq_sb", (128, 2 * D), "A", "A")
    A16("woT_sb", (128, 8 * D), "A", "A")
    A16("b_sb", (128, 2 * 256), "A", "A")
    A16("keff_sb", (128, 8 * 256), "A", "A")
    A16("m_sb", (128, NH * 2 * 256), "A", "C")
    A16("ve_sb", (128, 2 * D), "A", "C")
    A16("xfm_t", (128, 2, 8, 512), "B", "B")
    A16("q1T", (128, 2, T), "B", "C")
    A16("ckvT", (128, 2, T), "B", "C")
    A16("ckvtm", (128, TC, 256), "B", "C")
    A16("qlatT", (128, 2, T), "C", "C")
    A16("pT", (128, 3, QT), "C", "C")
    A16("ctxT", (128, 2, QT), "C", "C")
    A16("y_sb", (128, TC, D), "C", "D")
    A16("x1b_t", (128, 2, D), "D", "D")
    A16("x1g_t", (128, 2, 8, 512), "E", "E")
    A16("sgT_sb", (128, 8, IS), "D", "F")
    A16("suT_sb", (128, 8, IS), "D", "F")
    A16("sdT_sb", (128, ISC, D), "E", "F")
    A16("x1s_t", (128, 2, 8, 512), "F", "F")
    A16("hs_sb", (128, ISC, 512), "F", "F")
    A16("junk_bf", (128, T), "F", "F")
    A16("wg_buf", (128, 2, 8, IB), "G", "I")
    A16("wu_buf", (128, 2, 8, IB), "G", "I")
    A16("wd_buf", (128, IC, D), "G", "I")
    A16("xg_sb", (128, 2, 8, CAP), "G", "I")
    A16("h_sb", (128, IC, CAP), "G", "I")

    A32("identf", (128, 128), "A", "I")
    A32("mask4", (128, 4, QT), "A", "C")
    A32("blk", (128, 16), "A", "I")
    A32("blkT", (16, 128), "A", "I")
    A32("ones_1_16", (1, 16), "A", "I")
    A32("idsp1", (16, SX), "A", "I")
    A32("rk1", (16, CAP // 16), "A", "I")
    A32("nf16", (16, 16), "G", "H")
    A32("mwin", (16, CAP // 16), "G", "G")
    A32("dn_row", (1, QT), "C", "C")
    A32("recip_pp", (128, 4), "C", "C")
    A32("xt_t", (128, 2, D), "D", "D")
    A32("x1f_t", (128, 2, D), "D", "D")
    A32("lnjunk", (128, D), "D", "I")
    A32("st_t", (128, 16), "D", "I")
    A32("logit_t", (128, 16), "E", "E")
    A32("sc_t", (128, 16), "E", "E")
    A32("scT", (16, T), "E", "E")
    A32("gsc", (128, T), "E", "F")
    A32("sloc", (16, E, SX), "F", "G")
    A32("cnt_pp", (128, 1), "F", "F")
    A32("cnt_e", (16, 1), "F", "F")
    A32("lo_v", (16, 1), "F", "G")
    A32("hi_v", (16, 1), "F", "F")
    A32("mid_v", (16, 4), "F", "F")
    A32("tau_pp", (128, 1), "F", "F")
    A32("lo_row", (1, 16), "F", "G")
    A32("tau16", (16, 16), "F", "G")
    A32("mids", (16, 2, SX), "G", "G")
    A32("comp", (16, 2, SX), "G", "G")
    A32("nf_f", (1, 16), "G", "H")
    A32("gtmp", (128, 2, 512), "F", "F")
    A32("sh_t", (128, 2, D), "F", "F")
    A32("gat_t", (128, 16), "H", "H")
    A32("gat_s", (128, 8), "E", "H")
    A32("moegelu", (128, 2, CAP), "H", "H")
    A32("eo_sb", (128, NCAP, D), "H", "H")
    A32("mo_t", (128, 2, D), "I", "I")
    A32("xf2_t", (128, 2, D), "I", "I")
    A32("o_t", (128, 2, D), "I", "I")

    V16 = a16.finalize()
    V32 = a32.finalize()
    V = {}
    V.update(V16)
    V.update(V32)
    assert a16.total_bytes + a32.total_bytes < 205 * 1024, \
        (a16.total_bytes // 1024, a32.total_bytes // 1024)

    NIDX = CAP // 16
    i16tmp = nc.alloc_sbuf_tensor("i16tmp", [16, NIDX], I16).ap()
    idx16 = nc.alloc_sbuf_tensor("idx16", [128, E * NIDX], I16).ap()
    idx16 = idx16.rearrange("p (a b) -> p a b", a=E)
    nf_u = nc.alloc_sbuf_tensor("nf_u", [1, 16], U32).ap()
    nf_i = nc.alloc_sbuf_tensor("nf_i", [1, 16], I32).ap()
    c_eps = nc.alloc_sbuf_tensor("c_eps", [128, 1], F32).ap()
    nc.const_aps.register(F32, 1e-5, c_eps)

    (ident16, ones_col, gwT_sb, wdqT_sb, wdkvT_sb, wuq_sb, wuk_sb, wuv_sb,
     wdq_sb, woT_sb, b_sb, keff_sb, m_sb, ve_sb, xfm_t, q1T, ckvT, ckvtm,
     qlatT, pT, ctxT, y_sb, x1b_t, x1g_t, sgT_sb, suT_sb, sdT_sb, x1s_t,
     hs_sb, junk_bf, wg_buf, wu_buf, wd_buf, xg_sb, h_sb) = (
        V[k] for k in (
            "ident16", "ones_col", "gwT_sb", "wdqT_sb", "wdkvT_sb", "wuq_sb",
            "wuk_sb", "wuv_sb", "wdq_sb", "woT_sb", "b_sb", "keff_sb", "m_sb",
            "ve_sb", "xfm_t", "q1T", "ckvT", "ckvtm", "qlatT", "pT", "ctxT",
            "y_sb", "x1b_t", "x1g_t", "sgT_sb", "suT_sb", "sdT_sb", "x1s_t",
            "hs_sb", "junk_bf", "wg_buf", "wu_buf", "wd_buf",
            "xg_sb", "h_sb"))
    (identf, mask4, blk, blkT, ones_1_16, idsp1, dn_row, recip_pp, xt_t,
     x1f_t, lnjunk, st_t, logit_t, sc_t, scT, gsc, sloc, cnt_pp, cnt_e, lo_v,
     hi_v, mid_v, tau_pp, lo_row, tau16, mids, comp, nf_f, gtmp, sh_t, gat_t,
     gat_s, moegelu, eo_sb, mo_t, xf2_t, o_t, rk1, nf16, mwin) = (
        V[k] for k in (
            "identf", "mask4", "blk", "blkT", "ones_1_16", "idsp1", "dn_row",
            "recip_pp", "xt_t", "x1f_t", "lnjunk", "st_t", "logit_t", "sc_t",
            "scT", "gsc", "sloc", "cnt_pp", "cnt_e", "lo_v", "hi_v", "mid_v",
            "tau_pp", "lo_row", "tau16", "mids", "comp", "nf_f", "gtmp",
            "sh_t", "gat_t", "gat_s", "moegelu", "eo_sb", "mo_t", "xf2_t",
            "o_t", "rk1", "nf16", "mwin"))

    PS = [nc.alloc_psum_tensor(f"ps{i}", [128, 512], F32).ap() for i in range(8)]

    p = Prog(nc)
    dma_sems = {SY: [], GE: []}

    def dsem(eng, name):
        s = Sem(nc, name)
        dma_sems[eng].append(s)
        return s

    s_cst = dsem(SY, "s_cst")
    s_aw = dsem(GE, "s_aw")
    s_xf = [dsem(GE, "s_xf0"), dsem(GE, "s_xf1")]
    s_shw = dsem(GE, "s_shw")
    s_xt = [dsem(SY, "s_xt0"), dsem(SY, "s_xt1")]
    s_x1s = [dsem(SY, "s_x1s0"), dsem(SY, "s_x1s1")]
    s_tp = [dsem(SY, "s_tp0"), dsem(SY, "s_tp1")]
    s_ccb = dsem(SY, "s_ccb")
    s_msh = [dsem(SY, "s_msh0"), dsem(SY, "s_msh1")]
    s_idx = dsem(SY, "s_idx")
    s_wg = [dsem(GE, "s_wg0"), dsem(GE, "s_wg1")]
    s_wd = dsem(GE, "s_wd")
    s_gt = [dsem(GE, "s_gt0"), dsem(GE, "s_gt1")]
    s_sca = dsem(GE, "s_sca")
    s_rb = [dsem(SY, "s_rb0"), dsem(SY, "s_rb1")]
    s_dbg = [dsem(SY, "s_dbg0"), dsem(SY, "s_dbg1")]
    s_dbo = [dsem(SY, "s_dbo0"), dsem(SY, "s_dbo1")]
    s_ou = [dsem(SY, "s_ou0"), dsem(SY, "s_ou1")]
    s_te = Sem(nc, "s_te")
    s_ve = Sem(nc, "s_ve")
    s_se = Sem(nc, "s_se")
    s_ge = Sem(nc, "s_ge")
    s_cc = Sem(nc, "s_cc")
    s_b = Sem(nc, "s_b")
    p.auto = {VE: s_ve, SE: s_se, GE: s_ge}

    def barrier():
        for s in dma_sems[SY]:
            p.wait(SY, s, s.n)
        for s in dma_sems[GE]:
            p.wait(GE, s, s.n)
        p.wait(GE, s_ge, s_ge.n)
        targets = [(s_te, p.mark(TE, s_te)), (s_ve, s_ve.n), (s_se, s_se.n)]
        for eng, s in ((GE, s_ge), (SY, s_b)):
            v = s.inc()
            p.ops[eng].append(lambda e, ss=s.h: e.nop().then_inc(ss, 1))
            targets.append((s, v))
        for eng in ENGINES:
            for s, v in targets:
                p.wait(eng, s, v)

    def mm(ps_ap, lhsT_ap, rhs_ap, start, stop):
        p.on(TE, lambda e, o=ps_ap, l=lhsT_ap, r=rhs_ap, st=start, sp=stop:
             e.matmul(o, l, r, start=st, stop=sp, skip_group_check=True))

    def trans(ps_ap, in_ap):
        np_ = in_ap.shape[0]
        idt = ident16 if in_ap.dtype == BF16 else identf
        p.on(TE, lambda e, o=ps_ap, i=in_ap, d=idt[0:np_, 0:np_]:
             e.transpose(o, i, d))

    def vcopy(eng, out_ap, in_ap):
        p.on(eng, lambda e, o=out_ap, i=in_ap: e.tensor_copy(o, i))

    def scopy(out_ap, in_ap, scale=1.0):
        p.on(SE, lambda e, o=out_ap, i=in_ap, s=scale:
             e.activation(o, i, AF.Copy, bias=0.0, scale=s))

    p.dma(SY, identf[:, :], c_ident[:, :], s_cst)
    for o_ in range(4):
        p.dma(SY, mask4[:, o_, :], c_mask4[o_, :, :], s_cst)
    p.dma(SY, idsp1[:, :], c_idsp1[:, :], s_cst)
    p.dma(SY, rk1[:, :], c_rk1[:, :], s_cst)
    p.dma(SY, blk[:, :], c_blk[:, :], s_cst)
    p.dma(SY, blkT[:, :], c_blkT[:, :], s_cst)
    p.dma(SY, ones_1_16[:, :], c_ones_1_16[:, :], s_cst)
    w_consts = s_cst.n

    def load_fm(dst, src, ncols, nch, sem):
        v = 0
        for ch in range(nch):
            v = p.dma(GE, dst[:, ch * ncols:(ch + 1) * ncols],
                      src[ch * 128:(ch + 1) * 128, :], sem)
        return v

    nreg = [None, None]

    def _alloc_regs(e):
        nreg[0] = e.alloc_register("nidx0")
        nreg[1] = e.alloc_register("nidx1")
    p.on(GE, _alloc_regs)

    p.dma(GE, ones_col[:, :], c_ones_col[:, :], s_aw)
    load_fm(wdqT_sb, wdqT, 256, 8, s_aw)
    load_fm(wdkvT_sb, wdkvT, 256, 8, s_aw)
    load_fm(wuq_sb, wuq, 256, 8, s_aw)
    load_fm(wuk_sb, wuk, 256, 8, s_aw)
    load_fm(wuv_sb, wuv, 256, 8, s_aw)
    load_fm(wdq_sb, wdq, D, 2, s_aw)
    load_fm(woT_sb, woT, D, 8, s_aw)
    w_attn = load_fm(gwT_sb, gwT, 16, 8, s_aw)

    p.on(VE, lambda e: e.memset(c_eps[:, :], 1e-5))
    p.wait(VE, s_cst, w_consts)
    vcopy(VE, ident16[:, :], identf[:, :])
    v_ident = p.mark(VE, s_ve)

    p.wait(TE, s_aw, w_attn)
    p.wait(TE, s_ve, v_ident)
    for mc in range(2):
        for dc in range(8):
            mm(PS[mc][:, 0:256], wuq_sb[:, dc * 256 + mc * 128: dc * 256 + mc * 128 + 128],
               wuk_sb[:, dc * 256: dc * 256 + 256], dc == 0, dc == 7)
    te_b = p.mark(TE, s_te)
    p.wait(VE, s_te, te_b)
    for mc in range(2):
        vcopy(VE, b_sb[:, mc * 256:(mc + 1) * 256], PS[mc][:, 0:256])
    v_b = p.mark(VE, s_ve)

    p.wait(TE, s_ve, v_b)
    for mc8 in range(8):
        psn = PS[2 + (mc8 % 2)]
        for cc in range(2):
            mm(psn[:, 0:256], wdq_sb[:, cc * D + mc8 * 128: cc * D + mc8 * 128 + 128],
               b_sb[:, cc * 256:(cc + 1) * 256], cc == 0, cc == 1)
        te_k = p.mark(TE, s_te)
        p.wait(SE, s_te, te_k)
        scopy(keff_sb[:, mc8 * 256:(mc8 + 1) * 256], psn[:, 0:256], scale=0.125)
        se_k = p.mark(SE, s_se)
        if mc8 >= 1:
            p.wait(TE, s_se, se_k - 1)
    se_keff = s_se.n

    for mc in range(2):
        for ns in range(2):
            for dc in range(8):
                mm(PS[4 + ns][:, :], wuv_sb[:, dc * 256 + mc * 128: dc * 256 + mc * 128 + 128],
                   woT_sb[:, dc * D + ns * 512: dc * D + ns * 512 + 512], dc == 0, dc == 7)
        te_v = p.mark(TE, s_te)
        p.wait(VE, s_te, te_v)
        for ns in range(2):
            vcopy(VE, ve_sb[:, mc * D + ns * 512: mc * D + ns * 512 + 512],
                  PS[4 + ns][:, :])
        v_v = p.mark(VE, s_ve)
        p.wait(TE, s_ve, v_v)

    p.wait(TE, s_se, se_keff)
    for h in range(NH):
        pr = (h % 2) * 64
        dc = h // 2
        for mc in range(2):
            mm(PS[6][:, 0:256],
               wuq_sb[pr:pr + 64, dc * 256 + mc * 128: dc * 256 + mc * 128 + 128],
               keff_sb[pr:pr + 64, dc * 256: dc * 256 + 256], True, True)
            te_m = p.mark(TE, s_te)
            p.wait(SE, s_te, te_m)
            scopy(m_sb[:, h * 512 + mc * 256: h * 512 + mc * 256 + 256], PS[6][:, 0:256])
            se_m = p.mark(SE, s_se)
            p.wait(TE, s_se, se_m)

    te_a_done = p.mark(TE, s_te)
    barrier()

    b_te_done = [0] * T5
    for t5 in range(T5):
        buf = t5 % 2
        xf = xfm_t[:, buf]
        if t5 == 0:
            p.wait(GE, s_te, te_a_done)
        if t5 >= 2:
            p.wait(GE, s_te, b_te_done[t5 - 2])
        w_x = p.dma(GE, xf,
                    x_fmt[t5, :, :].rearrange("(dc p) q -> p dc q", p=128),
                    s_xf[buf])
        p.wait(TE, s_xf[buf], w_x)
        evs = []
        for which, (wsb, dst) in enumerate(((wdqT_sb, q1T), (wdkvT_sb, ckvT))):
            for mc in range(2):
                psn = PS[which * 2 + mc]
                for dc in range(8):
                    mm(psn[:, :], wsb[:, dc * 256 + mc * 128: dc * 256 + mc * 128 + 128],
                       xf[:, dc, :], dc == 0, dc == 7)
                te_p = p.mark(TE, s_te)
                ev = VE if (which * 2 + mc) % 2 == 0 else SE
                p.wait(ev, s_te, te_p)
                dap = dst[:, mc, t5 * 512: t5 * 512 + 512]
                if ev == VE:
                    vcopy(VE, dap, psn[:, :])
                else:
                    scopy(dap, psn[:, :])
                evs.append(ev)
        for sub in range(4):
            tc = t5 * 4 + sub
            for dc in range(8):
                mm(PS[4][:, 0:256], xf[:, dc, sub * 128: sub * 128 + 128],
                   wdkvT_sb[:, dc * 256: dc * 256 + 256], dc == 0, dc == 7)
            te_c = p.mark(TE, s_te)
            p.wait(SE, s_te, te_c)
            scopy(ckvtm[:, tc, :], PS[4][:, 0:256])
            se_c = p.mark(SE, s_se)
            p.wait(TE, s_se, se_c)
        v_e = p.mark(VE, s_ve)
        se_e = p.mark(SE, s_se)
        p.wait(TE, s_ve, v_e)
        p.wait(TE, s_se, se_e)
        b_te_done[t5] = p.mark(TE, s_te)

    barrier()
    p.ckpt("B")

    for h in range(NH):
        ql_se = {}
        ql_list = [(mc, t5) for mc in range(2) for t5 in range(T5)]
        for qi, (mc, t5) in enumerate(ql_list):
            psn = PS[6 + qi % 2]
            if qi >= 2:
                p.wait(TE, s_se, ql_se[qi - 2])
            for cc in range(2):
                mm(psn[:, :],
                   m_sb[:, h * 512 + cc * 256 + mc * 128: h * 512 + cc * 256 + mc * 128 + 128],
                   q1T[:, cc, t5 * 512: t5 * 512 + 512], cc == 0, cc == 1)
            te_q = p.mark(TE, s_te)
            p.wait(SE, s_te, te_q)
            scopy(qlatT[:, mc, t5 * 512: t5 * 512 + 512], psn[:, :])
            ql_se[qi] = p.mark(SE, s_se)
        p.wait(TE, s_se, ql_se[len(ql_list) - 1])

        for qt in range(T // QT):
            nk = (qt + 1) * NKQ
            cpair = (PS[2], PS[3]) if qt % 2 == 0 else (PS[5], PS[6])
            dn_ps = PS[4][0:1, :] if qt % 2 == 0 else PS[7][0:1, :]
            se_exp = [0] * nk

            def emit_S(kc, h=h, qt=qt, nk=nk, se_exp=se_exp):
                sps = PS[kc % 2]
                if kc >= 2:
                    p.wait(TE, s_se, se_exp[kc - 2])
                for lc in range(2):
                    mm(sps[:, :], ckvT[:, lc, kc * 128: kc * 128 + 128],
                       qlatT[:, lc, qt * QT: qt * QT + QT], lc == 0, lc == 1)
                te_s = p.mark(TE, s_te)
                diag_o = kc - qt * NKQ
                if diag_o >= 0:
                    p.wait(VE, s_te, te_s)
                    p.on(VE, lambda e, o=sps[:, :], b=mask4[:, diag_o, :]:
                         e.tensor_tensor(o, o, b, OP.add))
                    v_m = p.mark(VE, s_ve)
                    p.wait(SE, s_ve, v_m)
                else:
                    p.wait(SE, s_te, te_s)
                pt = pT[:, kc % 3]
                p.on(SE, lambda e, o=pt, i=sps[:, :]:
                     e.activation(o, i, AF.Exp, bias=0.0, scale=1.0))
                se_exp[kc] = p.mark(SE, s_se)

            emit_S(0)
            for kc in range(nk):
                if kc + 1 < nk:
                    emit_S(kc + 1)
                p.wait(TE, s_se, se_exp[kc])
                pt = pT[:, kc % 3]
                for lc in range(2):
                    mm(cpair[lc][:, :], ckvtm[:, kc, lc * 128: lc * 128 + 128],
                       pt, kc == 0, kc == nk - 1)
                mm(dn_ps, ones_col[:, :], pt, kc == 0, kc == nk - 1)
            te_pv = p.mark(TE, s_te)
            p.wait(VE, s_te, te_pv)
            vcopy(VE, ctxT[:, 0], cpair[0][:, :])
            scopy2 = p.wait(SE, s_te, te_pv)
            scopy(ctxT[:, 1], cpair[1][:, :])
            se_cx = p.mark(SE, s_se)
            p.on(VE, lambda e, o=dn_row[0:1, :], i=dn_ps: e.reciprocal(o, i))
            v_cx = p.mark(VE, s_ve)
            p.wait(TE, s_ve, v_cx)
            p.wait(TE, s_se, se_cx)
            for tcq in range(NKQ):
                trans(PS[kc % 2][:, tcq: tcq + 1],
                      dn_row[0:1, tcq * 128: tcq * 128 + 128])
            te_tr = p.mark(TE, s_te)
            p.wait(VE, s_te, te_tr)
            for tcq in range(NKQ):
                vcopy(VE, recip_pp[:, tcq: tcq + 1], PS[kc % 2][:, tcq: tcq + 1])
            v_rc = p.mark(VE, s_ve)
            p.wait(TE, s_ve, v_rc)
            for tcq in range(NKQ):
                for lc in range(2):
                    mm(PS[(kc + 1) % 2][:, 0:64],
                       ctxT[:, lc, tcq * 128: tcq * 128 + 128],
                       ve_sb[:, lc * D + h * 64: lc * D + h * 64 + 64],
                       lc == 0, lc == 1)
                te_y = p.mark(TE, s_te)
                ev = VE if tcq % 2 == 0 else SE
                p.wait(ev, s_te, te_y)
                tcg = qt * NKQ + tcq
                if ev == VE:
                    p.on(VE, lambda e, o=y_sb[:, tcg, h * 64: h * 64 + 64],
                         i=PS[(kc + 1) % 2][:, 0:64], s=recip_pp[:, tcq: tcq + 1]:
                         e.tensor_scalar(o, i, s, None, OP.mult))
                    v_y = p.mark(VE, s_ve)
                    p.wait(TE, s_ve, v_y)
                else:
                    p.on(SE, lambda e, o=y_sb[:, tcg, h * 64: h * 64 + 64],
                         i=PS[(kc + 1) % 2][:, 0:64], s=recip_pp[:, tcq: tcq + 1]:
                         e.activation(o, i, AF.Copy, bias=0.0, scale=s))
                    se_y = p.mark(SE, s_se)
                    p.wait(TE, s_se, se_y)
    te_flash = p.mark(TE, s_te)
    barrier()
    p.ckpt("C")

    load_fm(sgT_sb.rearrange("p a b -> p (a b)"), sgT, IS, 8, s_shw)
    load_fm(suT_sb.rearrange("p a b -> p (a b)"), suT, IS, 8, s_shw)

    def emit_ln(src_f32_fn, tc, out_f32, out_bf16, extra_in=None):
        s1 = st_t[:, 0:1]; mean = st_t[:, 1:2]; s2 = st_t[:, 2:3]
        tt = st_t[:, 3:4]; rstd = st_t[:, 4:5]
        src_f32_fn(out_f32)
        p.on(VE, lambda e, o=s1, i=out_f32: e.reduce_sum(o, i, AX.X))
        p.on(VE, lambda e, o=mean, i=s1: e.tensor_scalar(o, i, 1.0 / D, None, OP.mult))
        p.on(VE, lambda e, o=lnjunk[:, :], i=out_f32, s=mean, b=out_f32, a=s2:
             e.scalar_tensor_tensor(o, i, s, b, OP.subtract, OP.mult, accum_out=a))
        v_st = p.mark(VE, s_ve)
        p.wait(SE, s_ve, v_st)
        p.on(SE, lambda e, o=tt, i=s2:
             e.activation(o, i, AF.Ln, bias=1e-5, scale=1.0 / D))
        p.on(SE, lambda e, o=rstd, i=tt:
             e.activation(o, i, AF.Exp, bias=0.0, scale=-0.5))
        se_r = p.mark(SE, s_se)
        p.wait(VE, s_se, se_r)
        p.on(VE, lambda e, o=out_f32, i=out_f32, s=mean, s2_=rstd:
             e.tensor_scalar(o, i, s, s2_, OP.subtract, OP.mult))
        if out_bf16 is not None:
            vcopy(VE, out_bf16, out_f32)

    d_ve_done = [0] * TC
    d_sy_done = [0] * TC
    for tc in range(TC):
        buf = tc % 2
        xt = xt_t[:, buf]
        x1f = x1f_t[:, buf]
        x1b = x1b_t[:, buf]
        if tc >= 2:
            p.wait(SY, s_ve, d_ve_done[tc - 2])
        w_xt = p.dma(SY, xt, x_tm[tc * 128:(tc + 1) * 128, :], s_xt[buf])
        p.wait(VE, s_xt[buf], w_xt)
        if tc >= 2:
            p.wait(VE, s_x1s[buf], d_sy_done[tc - 2])
        emit_ln(lambda o, tc_=tc, xt_=xt: p.on(
            VE, lambda e, oo=o, a=xt_, b=y_sb[:, tc_, :]:
            e.tensor_tensor(oo, a, b, OP.add)), tc, x1f, x1b)
        d_ve_done[tc] = p.mark(VE, s_ve)
        p.wait(SY, s_ve, d_ve_done[tc])
        p.dma(SY, x1f_dram[tc * 128:(tc + 1) * 128, :], x1f, s_x1s[buf])
        d_sy_done[tc] = p.dma(SY, x1bf_dram[tc * 128:(tc + 1) * 128, :], x1b,
                              s_x1s[buf])
    x1s_totals = [s_x1s[0].n, s_x1s[1].n]
    barrier()
    p.ckpt("D")
    load_fm(sdT_sb.rearrange("p a b -> p (a b)"), sdT, D, ISC, s_shw)
    w_shw = s_shw.n

    e_te_done = [0] * T5
    for t5 in range(T5):
        buf = t5 % 2
        xg5 = x1g_t[:, buf]
        if t5 >= 2:
            p.wait(SY, s_te, e_te_done[t5 - 2])
        if t5 == 0:
            p.wait(SY, s_x1s[0], x1s_totals[0])
            p.wait(SY, s_x1s[1], x1s_totals[1])
        w_tp = 0
        for dc in range(8):
            w_tp = p.dmat(SY, xg5[:, dc, :],
                          x1bf_dram[t5 * 512:(t5 + 1) * 512, dc * 128:(dc + 1) * 128],
                          s_tp[buf])
        p.wait(TE, s_tp[buf], w_tp)
        for sub in range(4):
            tc = t5 * 4 + sub
            for dc in range(8):
                mm(PS[0][:, 0:16], xg5[:, dc, sub * 128: sub * 128 + 128],
                   gwT_sb[:, dc * 16:(dc + 1) * 16], dc == 0, dc == 7)
            te_l = p.mark(TE, s_te)
            p.wait(VE, s_te, te_l)
            p.on(VE, lambda e, o=gat_s[:, 0:1], i=PS[0][:, 0:16]:
                 e.reduce_max(o, i, AX.X))
            p.on(VE, lambda e, o=logit_t[:, :], i=PS[0][:, 0:16], s=gat_s[:, 0:1]:
                 e.tensor_scalar(o, i, s, None, OP.subtract))
            v_l = p.mark(VE, s_ve)
            p.wait(SE, s_ve, v_l)
            p.on(SE, lambda e, o=sc_t[:, :], i=logit_t[:, :]:
                 e.activation(o, i, AF.Exp, bias=0.0, scale=1.0))
            se_l = p.mark(SE, s_se)
            p.wait(VE, s_se, se_l)
            p.on(VE, lambda e, o=gat_s[:, 1:2], i=sc_t[:, :]: e.reduce_sum(o, i, AX.X))
            p.on(VE, lambda e, o=gat_s[:, 2:3], i=gat_s[:, 1:2]: e.reciprocal(o, i))
            p.on(VE, lambda e, o=sc_t[:, :], i=sc_t[:, :], s=gat_s[:, 2:3]:
                 e.tensor_scalar(o, i, s, None, OP.mult))
            v_s = p.mark(VE, s_ve)
            p.wait(TE, s_ve, v_s)
            trans(PS[1][0:16, 0:128], sc_t[:, :])
            te_sc = p.mark(TE, s_te)
            p.wait(VE, s_te, te_sc)
            vcopy(VE, scT[0:16, tc * 128:(tc + 1) * 128], PS[1][0:16, 0:128])
            v_sc = p.mark(VE, s_ve)
            p.wait(TE, s_ve, v_sc)
        e_te_done[t5] = p.mark(TE, s_te)
    v_gate = s_ve.n
    p.wait(SY, s_ve, v_gate)
    w_ccin = p.dma(SY, cc_in[:, :], scT[0:16, :], s_ccb)
    p.wait(GE, s_ccb, w_ccin)
    v_cc = s_cc.inc()
    p._dma_raw(GE, lambda e: e.collective_compute(
        "AllGather", OP.bypass, replica_groups=[list(range(NR))],
        ins=[cc_in[:, :]], outs=[cc_out[:, :]]).then_inc(s_cc.h, 1))
    p.wait(SY, s_cc, v_cc)
    cc_v = cc_out.rearrange("(r e) j -> r e j", e=16)
    for e_ in range(E):
        p.dma(SY, gsc[e_ * (NR): (e_ + 1) * NR, :], cc_v[:, e_, :], s_ccb)
    w_gsc = s_ccb.n

    barrier()
    p.ckpt("E")

    for e_ in range(E):
        p.dma(SY, sloc[:, e_, :],
              cc_in[e_: e_ + 1, :].rearrange("o (q s) -> o q s", q=16), s_ccb)
    w_route = s_ccb.n
    p.wait(VE, s_ccb, w_route)
    p.on(VE, lambda e: e.memset(lo_v[:, :], 0.0))
    p.on(VE, lambda e: e.memset(hi_v[:, :], 1.0))
    p.on(VE, lambda e: e.memset(tau_pp[:, :], 0.5))
    p.on(VE, lambda e: e.memset(mid_v[:, 0:1], 0.5))
    p.mark(VE, s_ve)

    CSEL = float(cfg.CSEL)
    tau_ready = []

    def emit_bisect_iter(it):
        if it > 0:
            p.wait(VE, s_ve, tau_ready[it - 1])
        p.on(VE, lambda e, o=junk_bf[:, :], i=gsc[:, :], s=tau_pp[:, :], a=cnt_pp[:, :]:
             e.tensor_scalar(o, i, s, None, OP.is_ge, OP.add, accum_out=a))
        v_c = p.mark(VE, s_ve)
        p.wait(TE, s_ve, v_c)
        mm(PS[7][0:16, 0:1], blk[:, :], cnt_pp[:, :], True, True)
        te_s_ = p.mark(TE, s_te)
        p.wait(VE, s_te, te_s_)
        vcopy(VE, cnt_e[:, :], PS[7][0:16, 0:1])
        ge_ = mid_v[:, 1:2]; t0 = mid_v[:, 2:3]; t1 = mid_v[:, 3:4]; mid = mid_v[:, 0:1]
        p.on(VE, lambda e, o=ge_, i=cnt_e[:, :]:
             e.tensor_scalar(o, i, CSEL, None, OP.is_ge))
        p.on(VE, lambda e, o=t0, a=mid, b=lo_v[:, :]: e.tensor_tensor(o, a, b, OP.subtract))
        p.on(VE, lambda e, o=t0, a=t0, b=ge_: e.tensor_tensor(o, a, b, OP.mult))
        p.on(VE, lambda e, o=lo_v[:, :], a=lo_v[:, :], b=t0: e.tensor_tensor(o, a, b, OP.add))
        p.on(VE, lambda e, o=t1, a=mid, b=hi_v[:, :]: e.tensor_tensor(o, a, b, OP.subtract))
        p.on(VE, lambda e, o=t0, i=ge_: e.tensor_scalar(o, i, -1.0, 1.0, OP.mult, OP.add))
        p.on(VE, lambda e, o=t1, a=t1, b=t0: e.tensor_tensor(o, a, b, OP.mult))
        p.on(VE, lambda e, o=hi_v[:, :], a=hi_v[:, :], b=t1: e.tensor_tensor(o, a, b, OP.add))
        p.on(VE, lambda e, o=mid, a=lo_v[:, :], b=hi_v[:, :]: e.tensor_tensor(o, a, b, OP.add))
        p.on(VE, lambda e, o=mid, i=mid: e.tensor_scalar(o, i, 0.5, None, OP.mult))
        v_u = p.mark(VE, s_ve)
        p.wait(TE, s_ve, v_u)
        mm(PS[7][:, 0:1], blkT[:, :], mid, True, True)
        te_b_ = p.mark(TE, s_te)
        p.wait(VE, s_te, te_b_)
        vcopy(VE, tau_pp[:, :], PS[7][:, 0:1])
        tau_ready.append(p.mark(VE, s_ve))

    f_te_done = [0] * T5
    sh_hist = []

    def emit_shared_t5(t5):
        buf = t5 % 2
        xs5 = x1s_t[:, buf]
        if t5 == 0:
            p.wait(SY, s_ccb, w_route)
        if t5 >= 2:
            p.wait(SY, s_te, f_te_done[t5 - 2])
        w_tp = 0
        for dc in range(8):
            w_tp = p.dmat(SY, xs5[:, dc, :],
                          x1bf_dram[t5 * 512:(t5 + 1) * 512, dc * 128:(dc + 1) * 128],
                          s_tp[buf])
        p.wait(TE, s_tp[buf], w_tp)
        if t5 == 0:
            p.wait(TE, s_shw, w_shw)
        v_hb = {}
        for ic in range(ISC):
            pr = (PS[0], PS[1]) if ic % 2 == 0 else (PS[4], PS[5])
            gt = gtmp[:, ic % 2, :]
            if ic >= 2:
                p.wait(TE, s_ve, v_hb[ic - 2])
            for which, wsb in ((0, sgT_sb), (1, suT_sb)):
                for dc in range(8):
                    mm(pr[which][:, :], wsb[:, dc, ic * 128: ic * 128 + 128],
                       xs5[:, dc, :], dc == 0, dc == 7)
            te_g = p.mark(TE, s_te)
            p.wait(SE, s_te, te_g)
            p.on(SE, lambda e, o=gt, i=pr[0][:, :]:
                 e.activation(o, i, AF.Gelu, bias=0.0, scale=1.0))
            se_g = p.mark(SE, s_se)
            p.wait(VE, s_se, se_g)
            p.on(VE, lambda e, o=hs_sb[:, ic, :], a=gt, b=pr[1][:, :]:
                 e.tensor_tensor(o, a, b, OP.mult))
            v_hb[ic] = p.mark(VE, s_ve)
        p.wait(TE, s_ve, v_hb[ISC - 1])
        p.wait(TE, s_ve, v_hb[ISC - 2])
        for tcq in range(4):
            for ns in range(2):
                for ic in range(ISC):
                    mm(PS[2 + ns][:, :], hs_sb[:, ic, tcq * 128: tcq * 128 + 128],
                       sdT_sb[:, ic, ns * 512: ns * 512 + 512], ic == 0, ic == ISC - 1)
            te_d = p.mark(TE, s_te)
            p.wait(VE, s_te, te_d)
            sbuf_ = tcq % 2
            sh = sh_t[:, sbuf_]
            if len(sh_hist) >= 2:
                slot, vv = sh_hist[-2]
                p.wait(VE, s_msh[slot], vv)
            for ns in range(2):
                vcopy(VE, sh[:, ns * 512:(ns + 1) * 512], PS[2 + ns][:, :])
            v_sh = p.mark(VE, s_ve)
            p.wait(TE, s_ve, v_sh)
            p.wait(SY, s_ve, v_sh)
            tcg = t5 * 4 + tcq
            slot = len(sh_hist) % 2
            sh_hist.append(
                (slot, p.dma(SY, moe_dram[tcg * 128:(tcg + 1) * 128, :], sh,
                             s_msh[slot])))
        f_te_done[t5] = p.mark(TE, s_te)

    for u in range(max(cfg.BIS, T5)):
        if u < cfg.BIS:
            emit_bisect_iter(u)
        if u < T5:
            emit_shared_t5(u)
    msh_totals = [s_msh[0].n, s_msh[1].n]

    p.wait(TE, s_ve, s_ve.n)
    trans(PS[7][0:1, 0:16], lo_v[:, :])
    te_lr = p.mark(TE, s_te)
    p.wait(VE, s_te, te_lr)
    vcopy(VE, lo_row[0:1, :], PS[7][0:1, 0:16])
    v_lr = p.mark(VE, s_ve)
    p.wait(TE, s_ve, v_lr)
    mm(PS[7][0:16, 0:16], ones_1_16[0:1, :], lo_row[0:1, :], True, True)
    te_t16 = p.mark(TE, s_te)
    p.wait(VE, s_te, te_t16)
    vcopy(VE, tau16[:, :], PS[7][0:16, 0:16])
    v_tau = p.mark(VE, s_ve)

    barrier()
    p.ckpt("F")

    p._dma_raw(GE, lambda e: e.load_library(library_config.sparse_gather))
    p.ckpt("Ga")
    sg_done = [0] * E
    cast_done = [0] * E
    for e_ in range(E):
        sl = e_ % 2
        if e_ >= 2:
            p.wait(VE, s_ge, sg_done[e_ - 2])
        p.on(VE, lambda e, o=mids[:, sl, :], i=sloc[:, e_, :],
             s=tau16[:, e_: e_ + 1]: e.tensor_scalar(o, i, s, None, OP.is_ge))
        p.on(VE, lambda e, o=mids[:, sl, :], a=mids[:, sl, :], b=idsp1[:, :]:
             e.tensor_tensor(o, a, b, OP.mult))
        p.on(VE, lambda e, o=mids[:, sl, :], i=mids[:, sl, :]:
             e.tensor_scalar(o, i, -1.0, None, OP.add))
        v_m = p.mark(VE, s_ve)
        p.wait(GE, s_ve, v_m)
        if e_ >= 2:
            p.wait(GE, s_ve, cast_done[e_ - 2])
        p.on(GE, lambda e, o=comp[:, sl, :], i=mids[:, sl, :],
             n=nf_u[0:1, e_: e_ + 1]: e.sparse_gather(o, i, num_found=n))
        sg_done[e_] = p.mark(GE, s_ge)
        p.wait(VE, s_ge, sg_done[e_])
        vcopy(VE, idx16[0:16, e_, :], comp[:, sl, 0:NIDX])
        cast_done[e_] = p.mark(VE, s_ve)
        if e_ == 0:
            p.ckpt("Gb")
    p.ckpt("Gc")
    p._dma_raw(GE, lambda e: e.load_library(library_config.mlp))
    p.on(GE, lambda e: e.nop())
    ge_mlp = s_ge.n
    p.ckpt("Gd")
    p.wait(VE, s_ge, sg_done[E - 1])
    vcopy(VE, nf_f[:, :], nf_u[:, :])
    p.on(VE, lambda e, o=nf_f[:, :], i=nf_f[:, :]:
         e.tensor_scalar(o, i, float(CAP), None, OP.min))
    vcopy(VE, nf_i[:, :], nf_f[:, :])
    v_nf = p.mark(VE, s_ve)
    p.wait(TE, s_ve, v_nf)
    mm(PS[7][0:16, 0:16], ones_1_16[0:1, :], nf_f[0:1, :], True, True)
    te_nf = p.mark(TE, s_te)
    p.wait(VE, s_te, te_nf)
    vcopy(VE, nf16[:, :], PS[7][0:16, 0:16])
    for e_ in range(E):
        p.on(VE, lambda e, o=mwin[:, :], i=rk1[:, :], s=nf16[:, e_: e_ + 1]:
             e.tensor_scalar(o, i, s, None, OP.is_le))
        p.on(VE, lambda e, o=i16tmp[:, :], a=idx16[0:16, e_, :],
             b=mwin[:, :]: e.tensor_tensor(o, a, b, OP.mult))
        p.on(VE, lambda e, o=idx16[0:16, e_, :], a=mwin[:, :],
             b=i16tmp[:, :]:
             e.scalar_tensor_tensor(o, a, -1.0, b, OP.add, OP.add))
    v_idx = p.mark(VE, s_ve)
    p.ckpt("Ge")
    p.wait(SY, s_tp[0], s_tp[0].n)
    p.wait(SY, s_tp[1], s_tp[1].n)
    p.wait(SY, s_ve, v_idx)
    i16f = idx16.rearrange("p a b -> p (a b)")
    w_r1 = p.dma(SY, i16f[16:32, :], i16f[0:16, :], s_idx)
    p.wait(SY, s_idx, w_r1)
    w_r2 = p.dma(SY, i16f[32:64, :], i16f[0:32, :], s_idx)
    p.wait(SY, s_idx, w_r2)
    w_idxrep = p.dma(SY, i16f[64:128, :], i16f[0:64, :], s_idx)
    p.wait(SY, s_idx, w_idxrep)
    p.ckpt("Gf")

    def fetch_wgwu(e_, isl, buf, wait_te=None):
        if wait_te is not None:
            p.wait(GE, s_te, wait_te)
        p.dma(GE, wg_buf[:, buf].rearrange("p a b -> p (a b)"),
              wgT_p[e_, isl].rearrange("p a b -> p (a b)"), s_wg[buf])
        p.dma(GE, wu_buf[:, buf].rearrange("p a b -> p (a b)"),
              wuT_p[e_, isl].rearrange("p a b -> p (a b)"), s_wg[buf])
        return (buf, s_wg[buf].n)

    def fetch_wd(e_):
        for ws in range(I // 512):
            p.dma(GE, wd_buf[:, ws * 4:(ws + 1) * 4].rearrange("p a b -> p (a b)"),
                  wdT_p[e_, ws].rearrange("p a b -> p (a b)"), s_wd)
        return s_wd.n

    p.ckpt("G0")
    wgwu_ready = {(0, 0): fetch_wgwu(0, 0, 0)}
    wd_ready = [None] * E
    wd_ready[0] = fetch_wd(0)

    NSPL = [(0, 512), (512, CAP - 512)] if CAP > 512 else [(0, CAP)]
    gat_ready = [None] * E
    reg_n = [None] * E

    def emit_gather(e_, wait_te=None):
        if wait_te is not None:
            p.wait(GE, s_te, wait_te)
        p.wait(GE, s_ge, ge_mlp)
        p.wait(GE, s_ve, v_idx)
        p.wait(GE, s_idx, w_idxrep)

        def ld(eng, ee=e_):
            eng.load(nreg[ee % 2], nf_i[0:1, ee: ee + 1])
            reg_n[ee] = nreg[ee % 2]
        p.on(GE, ld)
        buf = e_ % 2
        gat_ready[e_] = s_gt[buf].inc(16)
        p._dma_raw(GE, lambda e, b=buf, ee=e_:
                   e.dma_gather(xg_sb[:, b], x1bf_dram[:, :],
                                i16f[:, ee * NIDX: (ee + 1) * NIDX],
                                CAP, reg_n[ee], D,
                                transpose=True).then_inc(s_gt[b].h, 16))

    p.wait(GE, s_x1s[0], x1s_totals[0])
    p.wait(GE, s_x1s[1], x1s_totals[1])
    emit_gather(0)
    p.ckpt("G")

    te_after_up = [0] * E
    te_after_down = [0] * E
    for e_ in range(E):
        xg = xg_sb[:, e_ % 2]
        p.wait(TE, s_gt[e_ % 2], gat_ready[e_])
        v_ub = {}
        bi = 0
        for isl in range(nIS):
            buf = isl % 2
            nxt = (e_, isl + 1) if isl + 1 < nIS else \
                ((e_ + 1, 0) if e_ + 1 < E else None)
            wbuf_, wv_ = wgwu_ready[(e_, isl)]
            p.wait(TE, s_wg[wbuf_], wv_)
            for mc in range(IB // 128):
                img = isl * (IB // 128) + mc
                for (n0, nw) in NSPL:
                    pr = (PS[0], PS[1]) if bi % 2 == 0 else (PS[2], PS[3])
                    mg = moegelu[:, bi % 2, 0:nw]
                    if bi >= 2:
                        p.wait(TE, s_ve, v_ub[bi - 2])
                    for which, wb in ((0, wg_buf), (1, wu_buf)):
                        for dc in range(8):
                            mm(pr[which][:, 0:nw],
                               wb[:, buf, dc, mc * 128: mc * 128 + 128],
                               xg[:, dc, n0: n0 + nw], dc == 0, dc == 7)
                    te_u = p.mark(TE, s_te)
                    p.wait(SE, s_te, te_u)
                    p.on(SE, lambda e, o=mg, i=pr[0][:, 0:nw]:
                         e.activation(o, i, AF.Gelu, bias=0.0, scale=1.0))
                    se_u = p.mark(SE, s_se)
                    p.wait(VE, s_se, se_u)
                    p.on(VE, lambda e, o=h_sb[:, img, n0: n0 + nw],
                         a=mg, b=pr[1][:, 0:nw]:
                         e.tensor_tensor(o, a, b, OP.mult))
                    v_ub[bi] = p.mark(VE, s_ve)
                    bi += 1
            if nxt is not None and nxt not in wgwu_ready:
                wgwu_ready[nxt] = fetch_wgwu(nxt[0], nxt[1], (buf + 1) % 2,
                                             wait_te=s_te.n)
        p.wait(TE, s_ve, v_ub[bi - 1])
        p.wait(TE, s_ve, v_ub[bi - 2])
        te_after_up[e_] = p.mark(TE, s_te)
        if e_ + 1 < E:
            emit_gather(e_ + 1, wait_te=(te_after_down[e_ - 1] if e_ >= 1 else None))
        p.wait(TE, s_wd, wd_ready[e_])
        for tcq in range(NCAP):
            for dc in range(8):
                mm(PS[4][:, 0:16], xg[:, dc, tcq * 128: tcq * 128 + 128],
                   gwT_sb[:, dc * 16:(dc + 1) * 16], dc == 0, dc == 7)
            te_gl = p.mark(TE, s_te)
            p.wait(VE, s_te, te_gl)
            p.on(VE, lambda e, o=gat_s[:, 0:1], i=PS[4][:, 0:16]:
                 e.reduce_max(o, i, AX.X))
            p.on(VE, lambda e, o=gat_t[:, :], i=PS[4][:, 0:16], s=gat_s[:, 0:1]:
                 e.tensor_scalar(o, i, s, None, OP.subtract))
            v_gl = p.mark(VE, s_ve)
            p.wait(SE, s_ve, v_gl)
            p.on(SE, lambda e, o=gat_t[:, :], i=gat_t[:, :]:
                 e.activation(o, i, AF.Exp, bias=0.0, scale=1.0))
            se_gl = p.mark(SE, s_se)
            p.wait(VE, s_se, se_gl)
            p.on(VE, lambda e, o=gat_s[:, 1:2], i=gat_t[:, :]: e.reduce_sum(o, i, AX.X))
            p.on(VE, lambda e, o=gat_s[:, 2:3], i=gat_s[:, 1:2]: e.reciprocal(o, i))
            p.on(VE, lambda e, o=gat_s[:, 3:4], a=gat_t[:, e_: e_ + 1], b=gat_s[:, 2:3]:
                 e.tensor_tensor(o, a, b, OP.mult))
            v_gt = p.mark(VE, s_ve)
            for ns in range(2):
                for ic in range(IC):
                    mm(PS[5 + ns][:, :], h_sb[:, ic, tcq * 128: tcq * 128 + 128],
                       wd_buf[:, ic, ns * 512: ns * 512 + 512], ic == 0, ic == IC - 1)
            te_d = p.mark(TE, s_te)
            p.wait(VE, s_te, te_d)
            p.on(VE, lambda e, o=eo_sb[:, tcq, 0:512],
                 i=PS[5][:, :], s=gat_s[:, 3:4]:
                 e.tensor_scalar(o, i, s, None, OP.mult))
            v_eo = p.mark(VE, s_ve)
            p.wait(SE, s_te, te_d)
            p.wait(SE, s_ve, v_gt)
            p.on(SE, lambda e, o=eo_sb[:, tcq, 512:1024],
                 i=PS[6][:, :], s=gat_s[:, 3:4]:
                 e.activation(o, i, AF.Copy, bias=0.0, scale=s))
            se_eo = p.mark(SE, s_se)
            p.wait(TE, s_ve, v_eo)
            p.wait(TE, s_se, se_eo)
        te_after_down[e_] = p.mark(TE, s_te)
        v_eo_done = s_ve.n
        se_eo_done = s_se.n
        if e_ + 1 < E:
            p.wait(GE, s_ve, v_eo_done)
            wd_ready[e_ + 1] = fetch_wd(e_ + 1)
        p.wait(GE, s_ve, v_eo_done)
        p.wait(GE, s_se, se_eo_done)
        if e_ == 0:
            p.wait(GE, s_msh[0], msh_totals[0])
            p.wait(GE, s_msh[1], msh_totals[1])
        v_sc_ = s_sca.inc(16)
        p._dma_raw(GE, lambda e, ee=e_:
                   e.dma_scatter_add(moe_dram[:, :], eo_sb,
                                     i16f[:, ee * NIDX: (ee + 1) * NIDX],
                                     CAP, reg_n[ee], D).then_inc(s_sca.h, 16))
        p.wait(GE, s_sca, v_sc_)
    scatters_done = s_sca.n
    barrier()
    p.ckpt("H")

    p.wait(SY, s_sca, scatters_done)
    i_ve_done = [0] * TC
    ou_done = [0] * TC
    for tc in range(TC):
        buf = tc % 2
        mo = mo_t[:, buf]
        xf2 = xf2_t[:, buf]
        ot = o_t[:, buf]
        if tc >= 2:
            p.wait(SY, s_ve, i_ve_done[tc - 2])
        p.dma(SY, mo, moe_dram[tc * 128:(tc + 1) * 128, :], s_rb[buf])
        w_rb = p.dma(SY, xf2, x1f_dram[tc * 128:(tc + 1) * 128, :], s_rb[buf])
        p.wait(VE, s_rb[buf], w_rb)
        if tc >= 2:
            p.wait(VE, s_ou[buf], ou_done[tc - 2])
        emit_ln(lambda o, m_=mo, x_=xf2: p.on(
            VE, lambda e, oo=o, a=x_, b=m_: e.tensor_tensor(oo, a, b, OP.add)),
            tc, ot, None)
        i_ve_done[tc] = p.mark(VE, s_ve)
        p.wait(SY, s_ve, i_ve_done[tc])
        ou_done[tc] = p.dma(SY, out[tc * 128:(tc + 1) * 128, :], ot, s_ou[buf])
    p.wait(SY, s_ou[0], s_ou[0].n)
    p.wait(SY, s_ou[1], s_ou[1].n)

    if cfg.stop:
        n = p.ckpts[cfg.stop]
        for e in ENGINES:
            p.ops[e] = p.ops[e][:n[e]]
        srcd = x1f_dram if cfg.stop in ("D", "E", "F", "G", "G0", "Ga", "Gb", "Gc", "Gd", "Ge", "Gf") else \
            (moe_dram if cfg.stop == "H" else x1bf_dram)
        dbg_done = [0] * TC
        for tc in range(TC):
            buf = tc % 2
            ot = o_t[:, buf]
            if tc >= 2:
                p.wait(SY, s_dbo[buf], dbg_done[tc - 2])
            if cfg.stop in ("B", "C"):
                p.wait(SY, s_dbg[buf], s_dbg[buf].n)
                w = p.dma(SY, ot[0:128, 0:1], x_tm[tc * 128:(tc + 1) * 128, 0:1],
                          s_dbg[buf])
            else:
                w = p.dma(SY, ot, srcd[tc * 128:(tc + 1) * 128, :], s_dbg[buf])
            p.wait(SY, s_dbg[buf], w)
            dbg_done[tc] = p.dma(SY, out[tc * 128:(tc + 1) * 128, :], ot,
                                 s_dbo[buf])
        p.wait(SY, s_dbo[0], s_dbo[0].n)
        p.wait(SY, s_dbo[1], s_dbo[1].n)

    p.run()
    lower_extended_insts(nc)
    return nc


def kernel(**inputs):
    cfg = Cfg()
    com, cores = host_prep(cfg, inputs)
    nc = build(cfg)
    in_maps = []
    for b in range(cfg.n_cores):
        m = dict(com)
        m.update(cores[b])
        in_maps.append(m)
    res = run_bass_kernel_spmd(nc, in_maps, core_ids=list(range(cfg.n_cores)))
    outs = [np.asarray(r["out"]) for r in res.results]
    return np.stack(outs, axis=0)


# version 05
# speedup vs baseline: 1.6772x; 1.1607x over previous
import numpy as np
import concourse.bass as bass
import concourse.mybir as mybir
from concourse.bass_utils import run_bass_kernel_spmd
from concourse import library_config
from concourse.library_overlay import lower_extended_insts

F32 = mybir.dt.float32
BF16 = mybir.dt.bfloat16
I16 = mybir.dt.int16
I32 = mybir.dt.int32
U32 = mybir.dt.uint32
AX = mybir.AxisListType
OP = mybir.AluOpType
AF = mybir.ActivationFunctionType

TE, VE, SE, GE, SY = "tensor", "vector", "scalar", "gpsimd", "sync"
ENGINES = (TE, VE, SE, GE, SY)

PH = {n: i for i, n in enumerate("ABCDEFGHI")}


class Cfg:
    def __init__(self, T=4096, I=4096, IS=2048, CAP=640, BIS=36, n_cores=8):
        self.T = T
        self.D = 1024
        self.NH = 16
        self.HS = 64
        self.KVL = 256
        self.E = 16
        self.I = I
        self.IS = IS
        self.CAP = CAP
        self.BIS = BIS
        self.n_cores = n_cores
        self.CSEL = n_cores * T * 2 // 16
        self.QT = 512
        self.IB = min(256, I)
        import os
        self.stop = os.environ.get("KSTOP", "")
        assert T % 512 == 0 and I % self.IB == 0 and CAP % 128 == 0


class Sem:
    def __init__(self, nc, name):
        self.h = nc.semaphore(name).__enter__()
        self.n = 0

    def inc(self, k=1):
        self.n += k
        return self.n


class Arena:

    def __init__(self, nc, name, dtype, esz):
        self.nc, self.name, self.dtype, self.esz = nc, name, dtype, esz
        self.reqs = []
        self.views = {}

    def alloc(self, name, shape, ph0, ph1):
        self.reqs.append((name, shape, PH[ph0], PH[ph1]))
        return name

    def finalize(self):
        placed = []
        for name, shape, p0, p1 in sorted(self.reqs, key=lambda r: -np.prod(r[1])):
            cols = int(np.prod(shape[1:]))
            cands = sorted({0} | {off + c for off, c, q0, q1, *_ in placed
                                  if not (q1 < p0 or p1 < q0)})
            for off in cands:
                ok = True
                for o2, c2, q0, q1, *_ in placed:
                    if q1 < p0 or p1 < q0:
                        continue
                    if not (off + cols <= o2 or o2 + c2 <= off):
                        ok = False
                        break
                if ok:
                    placed.append((off, cols, p0, p1, name, shape))
                    break
        total = max(o + c for o, c, *_ in placed) if placed else 0
        import os
        if os.environ.get("ARENA_DEBUG"):
            print(f"arena {self.name}: total {total} cols = {total*self.esz} B/part")
            for off, cols, p0, p1, name, shape in sorted(placed):
                print(f"  {off:7d}..{off+cols:7d} ph{p0}-{p1} {name} {shape}")
        base = self.nc.alloc_sbuf_tensor(self.name, [128, total], self.dtype).ap()
        for off, cols, p0, p1, name, shape in placed:
            v = base[0:shape[0], off:off + cols]
            if len(shape) > 2:
                pat = "p (" + " ".join(f"a{i}" for i in range(len(shape) - 1)) + ") -> p " \
                      + " ".join(f"a{i}" for i in range(len(shape) - 1))
                kw = {f"a{i}": shape[i + 1] for i in range(len(shape) - 1)}
                v = v.rearrange(pat, **kw)
            self.views[name] = v
        self.total_bytes = total * self.esz
        return self.views


class Prog:
    def __init__(self, nc):
        self.nc = nc
        self.ops = {e: [] for e in ENGINES}
        self.last = {e: None for e in ENGINES}
        self.n_real = {e: 0 for e in ENGINES}
        self.mark_at = {e: -1 for e in ENGINES}
        self.mark_vals = {e: {} for e in ENGINES}
        self.auto = {}

    def on(self, eng, fn):
        a = self.auto.get(eng)
        if a is not None:
            prev = a.n
            self.ops[eng].append(lambda e, s=a.h, vv=prev: e.wait_ge(s, vv))
            v = a.inc()

            def wrapped(e, f=fn, eng_=eng, s=a.h):
                r = f(e)
                if r is not None:
                    self.last[eng_] = r
                    r.then_inc(s, 1)
                else:
                    e.nop().then_inc(s, 1)
            self.ops[eng].append(wrapped)
            self.n_real[eng] += 1
            return
        def wrapped(e, f=fn, eng_=eng):
            r = f(e)
            if r is not None:
                self.last[eng_] = r
        self.ops[eng].append(wrapped)
        self.n_real[eng] += 1

    def wait(self, eng, sem, val=None):
        v = sem.n if val is None else val
        self.ops[eng].append(lambda e, s=sem.h, vv=v: e.wait_ge(s, vv))

    def mark(self, eng, sem):
        a = self.auto.get(eng)
        if a is not None:
            assert sem is a, (eng, "marks on auto engines use the auto sem")
            return a.n
        if self.mark_at[eng] == self.n_real[eng]:
            if sem in self.mark_vals[eng]:
                return self.mark_vals[eng][sem]
            o_sem, o_v = next(iter(self.mark_vals[eng].items()))
            self.wait(eng, o_sem, o_v)
            v = sem.inc()
            self.ops[eng].append(lambda e, s=sem.h: e.nop().then_inc(s, 1))
            self.mark_vals[eng][sem] = v
            return v
        v = sem.inc()
        self.ops[eng].append(lambda e, s=sem.h, eng_=eng:
                             self.last[eng_].then_inc(s, 1))
        self.mark_at[eng] = self.n_real[eng]
        self.mark_vals[eng] = {sem: v}
        return v

    def ckpt(self, tag):
        if not hasattr(self, "ckpts"):
            self.ckpts = {}
        self.ckpts[tag] = {e: len(self.ops[e]) for e in ENGINES}

    def _dma_raw(self, eng, fn):
        def wrapped(e, f=fn, eng_=eng):
            r = f(e)
            if r is not None:
                self.last[eng_] = r
        self.ops[eng].append(wrapped)
        self.n_real[eng] += 1

    def dma(self, eng, out_ap, in_ap, sem, **kw):
        v = sem.inc(16)
        self._dma_raw(eng, lambda e, o=out_ap, i=in_ap, s=sem.h, k=dict(kw):
                      e.dma_start(out=o, in_=i, **k).then_inc(s, 16))
        return v

    def dmat(self, eng, out_ap, in_ap, sem):
        v = sem.inc(16)
        self._dma_raw(eng, lambda e, o=out_ap, i=in_ap, s=sem.h:
                      e.dma_start_transpose(o, i).then_inc(s, 16))
        return v

    def run(self):
        nc = self.nc
        with nc.Block() as block:
            @block.tensor
            def _(eng):
                for f in self.ops[TE]:
                    f(eng)

            @block.vector
            def _(eng):
                for f in self.ops[VE]:
                    f(eng)

            @block.scalar
            def _(eng):
                for f in self.ops[SE]:
                    f(eng)

            @block.gpsimd
            def _(eng):
                for f in self.ops[GE]:
                    f(eng)

            @block.sync
            def _(eng):
                for f in self.ops[SY]:
                    f(eng)


def host_prep(cfg, inputs):
    T, D, I, IS, E, IB = cfg.T, cfg.D, cfg.I, cfg.IS, cfg.E, cfg.IB
    f = np.ascontiguousarray
    x = np.asarray(inputs["L0_attn_x"], dtype=np.float32)
    com = {}
    com["wdqT"] = f(np.asarray(inputs["L0_attn_W_dq"], np.float32).T)
    com["wdq"] = f(np.asarray(inputs["L0_attn_W_dq"], np.float32))
    com["wuq"] = f(np.asarray(inputs["L0_attn_W_uq"], np.float32))
    com["wdkvT"] = f(np.asarray(inputs["L0_attn_W_dkv"], np.float32).T)
    com["wuk"] = f(np.asarray(inputs["L0_attn_W_uk"], np.float32))
    com["wuv"] = f(np.asarray(inputs["L0_attn_W_uv"], np.float32))
    com["woT"] = f(np.asarray(inputs["L0_attn_W_o"], np.float32).T)
    com["gwT"] = f(np.asarray(inputs["L0_ffn_gate_w"], np.float32).T)
    com["sgT"] = f(np.asarray(inputs["L0_ffn_Sg"], np.float32).T)
    com["suT"] = f(np.asarray(inputs["L0_ffn_Su"], np.float32).T)
    com["sdT"] = f(np.asarray(inputs["L0_ffn_Sd"], np.float32).T)
    wg = np.asarray(inputs["L0_ffn_Wg"], np.float32)
    wu = np.asarray(inputs["L0_ffn_Wu"], np.float32)
    wd = np.asarray(inputs["L0_ffn_Wd"], np.float32)
    nIS = I // IB
    com["wgT_p"] = f(wg.reshape(E, nIS, IB, 8, 128).transpose(0, 1, 4, 3, 2))
    com["wuT_p"] = f(wu.reshape(E, nIS, IB, 8, 128).transpose(0, 1, 4, 3, 2))
    nWS = I // 512
    com["wdT_p"] = f(wd.transpose(0, 2, 1).reshape(E, nWS, 4, 128, D)
                     .transpose(0, 1, 3, 2, 4))
    com["c_ident"] = np.eye(128, dtype=np.float32)
    m4 = np.zeros((4, 128, cfg.QT), dtype=np.float32)
    r = np.arange(128)[:, None]
    c = np.arange(cfg.QT)[None, :]
    for o in range(4):
        m4[o][(o * 128 + r) > c] = -1e30
    com["c_mask4"] = m4
    com["c_ones_col"] = np.ones((128, 1), dtype=np.float32)
    SX = T // 16
    ids = np.arange(16)[:, None] * SX + np.arange(SX)[None, :]
    com["c_idsp1"] = (ids + 1).astype(np.float32)
    NIDX = cfg.CAP // 16
    rk = np.arange(NIDX)[None, :] * 16 + np.arange(16)[:, None]
    com["c_rk1"] = (rk + 1).astype(np.float32)
    blkm = (np.arange(128)[:, None] // 8 == np.arange(16)[None, :]).astype(np.float32)
    com["c_blk"] = f(blkm)
    com["c_blkT"] = f(blkm.T)
    com["c_ones_1_16"] = np.ones((1, 16), dtype=np.float32)
    cores = []
    for b in range(cfg.n_cores):
        xb = f(x[b])
        cores.append({
            "x_tm": xb,
            "x_fmt": f(xb.T.reshape(D, T // 512, 512).transpose(1, 0, 2)),
        })
    return com, cores


def build(cfg):
    T, D, I, IS, E, CAP, IB, QT = (cfg.T, cfg.D, cfg.I, cfg.IS, cfg.E,
                                   cfg.CAP, cfg.IB, cfg.QT)
    NH = cfg.NH
    TC = T // 128
    T5 = T // 512
    nIS = I // IB
    IC = I // 128
    ISC = IS // 128
    NCAP = CAP // 128
    SX = T // 16
    NR = cfg.n_cores
    NKQ = QT // 128

    nc = bass.Bass("TRN2", target_bir_lowering=False, debug=False)

    def din(name, shape):
        return nc.dram_tensor(name, list(shape), F32, kind="ExternalInput").ap()

    x_tm = din("x_tm", (T, D))
    x_fmt = din("x_fmt", (T5, D, 512))
    wdqT = din("wdqT", (D, 256)); wdq = din("wdq", (256, D))
    wuq = din("wuq", (D, 256)); wdkvT = din("wdkvT", (D, 256))
    wuk = din("wuk", (D, 256)); wuv = din("wuv", (D, 256))
    woT = din("woT", (D, D)); gwT = din("gwT", (D, 16))
    sgT = din("sgT", (D, IS)); suT = din("suT", (D, IS)); sdT = din("sdT", (IS, D))
    wgT_p = din("wgT_p", (E, nIS, 128, 8, IB))
    wuT_p = din("wuT_p", (E, nIS, 128, 8, IB))
    wdT_p = din("wdT_p", (E, I // 512, 128, 4, D))
    c_ident = din("c_ident", (128, 128))
    c_mask4 = din("c_mask4", (4, 128, QT))
    c_ones_col = din("c_ones_col", (128, 1))
    c_idsp1 = din("c_idsp1", (16, SX))
    c_rk1 = din("c_rk1", (16, CAP // 16))
    c_blk = din("c_blk", (128, 16))
    c_blkT = din("c_blkT", (16, 128))
    c_ones_1_16 = din("c_ones_1_16", (1, 16))

    out = nc.dram_tensor("out", [T, D], F32, kind="ExternalOutput").ap()
    x1bf_dram = nc.dram_tensor("x1bf_dram", [T, D], BF16, kind="Internal").ap()
    x1f_dram = nc.dram_tensor("x1f_dram", [T, D], F32, kind="Internal").ap()
    moe_dram = nc.dram_tensor("moe_dram", [T, D], F32, kind="Internal").ap()
    cc_in = nc.dram_tensor("cc_in", [16, T], F32, kind="Internal").ap()
    cc_out = nc.dram_tensor("cc_out", [16 * NR, T], F32, kind="Internal").ap()

    a16 = Arena(nc, "a16", BF16, 2)
    a32 = Arena(nc, "a32", F32, 4)

    def A16(name, shape, ph0, ph1):
        return a16.alloc(name, shape, ph0, ph1)

    def A32(name, shape, ph0, ph1):
        return a32.alloc(name, shape, ph0, ph1)

    A16("ident16", (128, 128), "A", "I")
    A16("ones_col", (128, 1), "A", "I")
    A16("gwT_sb", (128, 8 * 16), "A", "I")
    A16("wdqT_sb", (128, 8 * 256), "A", "B")
    A16("wdkvT_sb", (128, 8 * 256), "A", "B")
    A16("wuq_sb", (128, 8 * 256), "A", "A")
    A16("wuk_sb", (128, 8 * 256), "A", "A")
    A16("wuv_sb", (128, 8 * 256), "A", "A")
    A16("wdq_sb", (128, 2 * D), "A", "A")
    A16("woT_sb", (128, 8 * D), "A", "A")
    A16("b_sb", (128, 2 * 256), "A", "A")
    A16("keff_sb", (128, 8 * 256), "A", "A")
    A16("m_sb", (128, NH * 2 * 256), "A", "C")
    A16("ve_sb", (128, 2 * D), "A", "C")
    A16("xfm_t", (128, 2, 8, 512), "B", "B")
    A16("q1T", (128, 2, T), "B", "C")
    A16("ckvT", (128, 2, T), "B", "C")
    A16("ckvtm", (128, TC, 256), "B", "C")
    A16("qlatT", (128, 2, T), "C", "C")
    A16("pT", (128, 3, QT), "C", "C")
    A16("ctxT", (128, 2, QT), "C", "C")
    A16("y_sb", (128, TC, D), "C", "D")
    A16("x1b_t", (128, 2, D), "D", "D")
    A16("x1g_t", (128, 2, 8, 512), "E", "E")
    A16("sgT_sb", (128, 8, IS), "D", "F")
    A16("suT_sb", (128, 8, IS), "D", "F")
    A16("sdT_sb", (128, ISC, D), "E", "F")
    A16("x1s_t", (128, 2, 8, 512), "F", "F")
    A16("hs_sb", (128, ISC, 512), "F", "F")
    A16("junk_bf", (128, T), "F", "F")
    A16("wg_buf", (128, 3, 8, IB), "G", "I")
    A16("wu_buf", (128, 3, 8, IB), "G", "I")
    A16("wd_buf", (128, IC, D), "G", "I")
    A16("xg_sb", (128, 2, 8, CAP), "G", "I")
    A16("h_sb", (128, IC, CAP), "G", "I")

    A32("identf", (128, 128), "A", "I")
    A32("mask4", (128, 4, QT), "A", "C")
    A32("blk", (128, 16), "A", "I")
    A32("blkT", (16, 128), "A", "I")
    A32("ones_1_16", (1, 16), "A", "I")
    A32("idsp1", (16, SX), "A", "I")
    A32("rk1", (16, CAP // 16), "A", "I")
    A32("nf16", (16, 16), "G", "H")
    A32("mwin", (16, CAP // 16), "G", "G")
    A32("dn_row", (1, QT), "C", "C")
    A32("recip_pp", (128, 4), "C", "C")
    A32("xt_t", (128, 2, D), "D", "D")
    A32("x1f_t", (128, 2, D), "D", "D")
    A32("lnjunk", (128, D), "D", "I")
    A32("st_t", (128, 16), "D", "I")
    A32("logit_t", (128, 16), "E", "E")
    A32("sc_t", (128, 16), "E", "E")
    A32("scT", (16, T), "E", "E")
    A32("gsc", (128, T), "E", "F")
    A32("sloc", (16, E, SX), "F", "G")
    A32("cnt_pp", (128, 1), "F", "F")
    A32("cnt_e", (16, 1), "F", "F")
    A32("lo_v", (16, 1), "F", "G")
    A32("hi_v", (16, 1), "F", "F")
    A32("mid_v", (16, 4), "F", "F")
    A32("tau_pp", (128, 1), "F", "F")
    A32("lo_row", (1, 16), "F", "G")
    A32("tau16", (16, 16), "F", "G")
    A32("mids", (16, 2, SX), "G", "G")
    A32("comp", (16, 2, SX), "G", "G")
    A32("nf_f", (1, 16), "G", "H")
    A32("gtmp", (128, 2, 512), "F", "F")
    A32("sh_t", (128, 2, D), "F", "F")
    A32("gat_t", (128, 16), "H", "H")
    A32("gat_s", (128, 8), "E", "H")
    A32("moegelu", (128, 2, CAP), "H", "H")
    A32("eo_sb", (128, NCAP, D), "H", "H")
    A32("mo_t", (128, 2, D), "I", "I")
    A32("xf2_t", (128, 2, D), "I", "I")
    A32("o_t", (128, 2, D), "I", "I")

    V16 = a16.finalize()
    V32 = a32.finalize()
    V = {}
    V.update(V16)
    V.update(V32)
    assert a16.total_bytes + a32.total_bytes < 205 * 1024, \
        (a16.total_bytes // 1024, a32.total_bytes // 1024)

    NIDX = CAP // 16
    i16tmp = nc.alloc_sbuf_tensor("i16tmp", [16, NIDX], I16).ap()
    idx16 = nc.alloc_sbuf_tensor("idx16", [128, E * NIDX], I16).ap()
    idx16 = idx16.rearrange("p (a b) -> p a b", a=E)
    nf_u = nc.alloc_sbuf_tensor("nf_u", [1, 16], U32).ap()
    nf_i = nc.alloc_sbuf_tensor("nf_i", [1, 16], I32).ap()
    c_eps = nc.alloc_sbuf_tensor("c_eps", [128, 1], F32).ap()
    nc.const_aps.register(F32, 1e-5, c_eps)

    (ident16, ones_col, gwT_sb, wdqT_sb, wdkvT_sb, wuq_sb, wuk_sb, wuv_sb,
     wdq_sb, woT_sb, b_sb, keff_sb, m_sb, ve_sb, xfm_t, q1T, ckvT, ckvtm,
     qlatT, pT, ctxT, y_sb, x1b_t, x1g_t, sgT_sb, suT_sb, sdT_sb, x1s_t,
     hs_sb, junk_bf, wg_buf, wu_buf, wd_buf, xg_sb, h_sb) = (
        V[k] for k in (
            "ident16", "ones_col", "gwT_sb", "wdqT_sb", "wdkvT_sb", "wuq_sb",
            "wuk_sb", "wuv_sb", "wdq_sb", "woT_sb", "b_sb", "keff_sb", "m_sb",
            "ve_sb", "xfm_t", "q1T", "ckvT", "ckvtm", "qlatT", "pT", "ctxT",
            "y_sb", "x1b_t", "x1g_t", "sgT_sb", "suT_sb", "sdT_sb", "x1s_t",
            "hs_sb", "junk_bf", "wg_buf", "wu_buf", "wd_buf",
            "xg_sb", "h_sb"))
    (identf, mask4, blk, blkT, ones_1_16, idsp1, dn_row, recip_pp, xt_t,
     x1f_t, lnjunk, st_t, logit_t, sc_t, scT, gsc, sloc, cnt_pp, cnt_e, lo_v,
     hi_v, mid_v, tau_pp, lo_row, tau16, mids, comp, nf_f, gtmp, sh_t, gat_t,
     gat_s, moegelu, eo_sb, mo_t, xf2_t, o_t, rk1, nf16, mwin) = (
        V[k] for k in (
            "identf", "mask4", "blk", "blkT", "ones_1_16", "idsp1", "dn_row",
            "recip_pp", "xt_t", "x1f_t", "lnjunk", "st_t", "logit_t", "sc_t",
            "scT", "gsc", "sloc", "cnt_pp", "cnt_e", "lo_v", "hi_v", "mid_v",
            "tau_pp", "lo_row", "tau16", "mids", "comp", "nf_f", "gtmp",
            "sh_t", "gat_t", "gat_s", "moegelu", "eo_sb", "mo_t", "xf2_t",
            "o_t", "rk1", "nf16", "mwin"))

    PS = [nc.alloc_psum_tensor(f"ps{i}", [128, 512], F32).ap() for i in range(8)]

    p = Prog(nc)
    dma_sems = {SY: [], GE: []}

    def dsem(eng, name):
        s = Sem(nc, name)
        dma_sems[eng].append(s)
        return s

    s_cst = dsem(SY, "s_cst")
    s_aw = dsem(GE, "s_aw")
    s_xf = [dsem(GE, "s_xf0"), dsem(GE, "s_xf1")]
    s_shw = dsem(GE, "s_shw")
    s_xt = [dsem(SY, "s_xt0"), dsem(SY, "s_xt1")]
    s_x1s = [dsem(SY, "s_x1s0"), dsem(SY, "s_x1s1")]
    s_tp = [dsem(SY, "s_tp0"), dsem(SY, "s_tp1")]
    s_ccb = dsem(SY, "s_ccb")
    s_msh = [dsem(SY, "s_msh0"), dsem(SY, "s_msh1")]
    s_idx = dsem(SY, "s_idx")
    s_wg = [dsem(GE, "s_wg0"), dsem(GE, "s_wg1"), dsem(GE, "s_wg2")]
    s_wd = dsem(GE, "s_wd")
    s_gt = [dsem(GE, "s_gt0"), dsem(GE, "s_gt1")]
    s_sca = dsem(GE, "s_sca")
    s_rb = [dsem(SY, "s_rb0"), dsem(SY, "s_rb1")]
    s_dbg = [dsem(SY, "s_dbg0"), dsem(SY, "s_dbg1")]
    s_dbo = [dsem(SY, "s_dbo0"), dsem(SY, "s_dbo1")]
    s_ou = [dsem(SY, "s_ou0"), dsem(SY, "s_ou1")]
    s_te = Sem(nc, "s_te")
    s_ve = Sem(nc, "s_ve")
    s_se = Sem(nc, "s_se")
    s_ge = Sem(nc, "s_ge")
    s_cc = Sem(nc, "s_cc")
    s_b = Sem(nc, "s_b")
    p.auto = {VE: s_ve, SE: s_se, GE: s_ge}

    def barrier():
        for s in dma_sems[SY]:
            p.wait(SY, s, s.n)
        for s in dma_sems[GE]:
            p.wait(GE, s, s.n)
        p.wait(GE, s_ge, s_ge.n)
        targets = [(s_te, p.mark(TE, s_te)), (s_ve, s_ve.n), (s_se, s_se.n)]
        for eng, s in ((GE, s_ge), (SY, s_b)):
            v = s.inc()
            p.ops[eng].append(lambda e, ss=s.h: e.nop().then_inc(ss, 1))
            targets.append((s, v))
        for eng in ENGINES:
            for s, v in targets:
                p.wait(eng, s, v)

    def mm(ps_ap, lhsT_ap, rhs_ap, start, stop):
        p.on(TE, lambda e, o=ps_ap, l=lhsT_ap, r=rhs_ap, st=start, sp=stop:
             e.matmul(o, l, r, start=st, stop=sp, skip_group_check=True))

    def trans(ps_ap, in_ap):
        np_ = in_ap.shape[0]
        idt = ident16 if in_ap.dtype == BF16 else identf
        p.on(TE, lambda e, o=ps_ap, i=in_ap, d=idt[0:np_, 0:np_]:
             e.transpose(o, i, d))

    def vcopy(eng, out_ap, in_ap):
        p.on(eng, lambda e, o=out_ap, i=in_ap: e.tensor_copy(o, i))

    def scopy(out_ap, in_ap, scale=1.0):
        p.on(SE, lambda e, o=out_ap, i=in_ap, s=scale:
             e.activation(o, i, AF.Copy, bias=0.0, scale=s))

    p.dma(SY, identf[:, :], c_ident[:, :], s_cst)
    for o_ in range(4):
        p.dma(SY, mask4[:, o_, :], c_mask4[o_, :, :], s_cst)
    p.dma(SY, idsp1[:, :], c_idsp1[:, :], s_cst)
    p.dma(SY, rk1[:, :], c_rk1[:, :], s_cst)
    p.dma(SY, blk[:, :], c_blk[:, :], s_cst)
    p.dma(SY, blkT[:, :], c_blkT[:, :], s_cst)
    p.dma(SY, ones_1_16[:, :], c_ones_1_16[:, :], s_cst)
    w_consts = s_cst.n

    def load_fm(dst, src, ncols, nch, sem):
        v = 0
        for ch in range(nch):
            v = p.dma(GE, dst[:, ch * ncols:(ch + 1) * ncols],
                      src[ch * 128:(ch + 1) * 128, :], sem)
        return v

    nreg = [None, None]

    def _alloc_regs(e):
        nreg[0] = e.alloc_register("nidx0")
        nreg[1] = e.alloc_register("nidx1")
    p.on(GE, _alloc_regs)

    p.dma(GE, ones_col[:, :], c_ones_col[:, :], s_aw)
    load_fm(wdqT_sb, wdqT, 256, 8, s_aw)
    load_fm(wdkvT_sb, wdkvT, 256, 8, s_aw)
    load_fm(wuq_sb, wuq, 256, 8, s_aw)
    load_fm(wuk_sb, wuk, 256, 8, s_aw)
    load_fm(wuv_sb, wuv, 256, 8, s_aw)
    load_fm(wdq_sb, wdq, D, 2, s_aw)
    load_fm(woT_sb, woT, D, 8, s_aw)
    w_attn = load_fm(gwT_sb, gwT, 16, 8, s_aw)

    p.on(VE, lambda e: e.memset(c_eps[:, :], 1e-5))
    p.wait(VE, s_cst, w_consts)
    vcopy(VE, ident16[:, :], identf[:, :])
    v_ident = p.mark(VE, s_ve)

    p.wait(TE, s_aw, w_attn)
    p.wait(TE, s_ve, v_ident)
    for mc in range(2):
        for dc in range(8):
            mm(PS[mc][:, 0:256], wuq_sb[:, dc * 256 + mc * 128: dc * 256 + mc * 128 + 128],
               wuk_sb[:, dc * 256: dc * 256 + 256], dc == 0, dc == 7)
    te_b = p.mark(TE, s_te)
    p.wait(VE, s_te, te_b)
    for mc in range(2):
        vcopy(VE, b_sb[:, mc * 256:(mc + 1) * 256], PS[mc][:, 0:256])
    v_b = p.mark(VE, s_ve)

    p.wait(TE, s_ve, v_b)
    for mc8 in range(8):
        psn = PS[2 + (mc8 % 2)]
        for cc in range(2):
            mm(psn[:, 0:256], wdq_sb[:, cc * D + mc8 * 128: cc * D + mc8 * 128 + 128],
               b_sb[:, cc * 256:(cc + 1) * 256], cc == 0, cc == 1)
        te_k = p.mark(TE, s_te)
        p.wait(SE, s_te, te_k)
        scopy(keff_sb[:, mc8 * 256:(mc8 + 1) * 256], psn[:, 0:256], scale=0.125)
        se_k = p.mark(SE, s_se)
        if mc8 >= 1:
            p.wait(TE, s_se, se_k - 1)
    se_keff = s_se.n

    for mc in range(2):
        for ns in range(2):
            for dc in range(8):
                mm(PS[4 + ns][:, :], wuv_sb[:, dc * 256 + mc * 128: dc * 256 + mc * 128 + 128],
                   woT_sb[:, dc * D + ns * 512: dc * D + ns * 512 + 512], dc == 0, dc == 7)
        te_v = p.mark(TE, s_te)
        p.wait(VE, s_te, te_v)
        for ns in range(2):
            vcopy(VE, ve_sb[:, mc * D + ns * 512: mc * D + ns * 512 + 512],
                  PS[4 + ns][:, :])
        v_v = p.mark(VE, s_ve)
        p.wait(TE, s_ve, v_v)

    p.wait(TE, s_se, se_keff)
    for h in range(NH):
        pr = (h % 2) * 64
        dc = h // 2
        for mc in range(2):
            mm(PS[6][:, 0:256],
               wuq_sb[pr:pr + 64, dc * 256 + mc * 128: dc * 256 + mc * 128 + 128],
               keff_sb[pr:pr + 64, dc * 256: dc * 256 + 256], True, True)
            te_m = p.mark(TE, s_te)
            p.wait(SE, s_te, te_m)
            scopy(m_sb[:, h * 512 + mc * 256: h * 512 + mc * 256 + 256], PS[6][:, 0:256])
            se_m = p.mark(SE, s_se)
            p.wait(TE, s_se, se_m)

    te_a_done = p.mark(TE, s_te)
    barrier()

    b_te_done = [0] * T5
    for t5 in range(T5):
        buf = t5 % 2
        xf = xfm_t[:, buf]
        if t5 == 0:
            p.wait(GE, s_te, te_a_done)
        if t5 >= 2:
            p.wait(GE, s_te, b_te_done[t5 - 2])
        w_x = p.dma(GE, xf,
                    x_fmt[t5, :, :].rearrange("(dc p) q -> p dc q", p=128),
                    s_xf[buf])
        p.wait(TE, s_xf[buf], w_x)
        evs = []
        for which, (wsb, dst) in enumerate(((wdqT_sb, q1T), (wdkvT_sb, ckvT))):
            for mc in range(2):
                psn = PS[which * 2 + mc]
                for dc in range(8):
                    mm(psn[:, :], wsb[:, dc * 256 + mc * 128: dc * 256 + mc * 128 + 128],
                       xf[:, dc, :], dc == 0, dc == 7)
                te_p = p.mark(TE, s_te)
                ev = VE if (which * 2 + mc) % 2 == 0 else SE
                p.wait(ev, s_te, te_p)
                dap = dst[:, mc, t5 * 512: t5 * 512 + 512]
                if ev == VE:
                    vcopy(VE, dap, psn[:, :])
                else:
                    scopy(dap, psn[:, :])
                evs.append(ev)
        for sub in range(4):
            tc = t5 * 4 + sub
            for dc in range(8):
                mm(PS[4][:, 0:256], xf[:, dc, sub * 128: sub * 128 + 128],
                   wdkvT_sb[:, dc * 256: dc * 256 + 256], dc == 0, dc == 7)
            te_c = p.mark(TE, s_te)
            p.wait(SE, s_te, te_c)
            scopy(ckvtm[:, tc, :], PS[4][:, 0:256])
            se_c = p.mark(SE, s_se)
            p.wait(TE, s_se, se_c)
        v_e = p.mark(VE, s_ve)
        se_e = p.mark(SE, s_se)
        p.wait(TE, s_ve, v_e)
        p.wait(TE, s_se, se_e)
        b_te_done[t5] = p.mark(TE, s_te)

    barrier()
    p.ckpt("B")

    for h in range(NH):
        ql_se = {}
        ql_list = [(mc, t5) for mc in range(2) for t5 in range(T5)]
        for qi, (mc, t5) in enumerate(ql_list):
            psn = PS[6 + qi % 2]
            if qi >= 2:
                p.wait(TE, s_se, ql_se[qi - 2])
            for cc in range(2):
                mm(psn[:, :],
                   m_sb[:, h * 512 + cc * 256 + mc * 128: h * 512 + cc * 256 + mc * 128 + 128],
                   q1T[:, cc, t5 * 512: t5 * 512 + 512], cc == 0, cc == 1)
            te_q = p.mark(TE, s_te)
            p.wait(SE, s_te, te_q)
            scopy(qlatT[:, mc, t5 * 512: t5 * 512 + 512], psn[:, :])
            ql_se[qi] = p.mark(SE, s_se)
        p.wait(TE, s_se, ql_se[len(ql_list) - 1])

        for qt in range(T // QT):
            nk = (qt + 1) * NKQ
            cpair = (PS[2], PS[3]) if qt % 2 == 0 else (PS[5], PS[6])
            dn_ps = PS[4][0:1, :] if qt % 2 == 0 else PS[7][0:1, :]
            se_exp = [0] * nk

            def emit_S(kc, h=h, qt=qt, nk=nk, se_exp=se_exp):
                sps = PS[kc % 2]
                if kc >= 2:
                    p.wait(TE, s_se, se_exp[kc - 2])
                for lc in range(2):
                    mm(sps[:, :], ckvT[:, lc, kc * 128: kc * 128 + 128],
                       qlatT[:, lc, qt * QT: qt * QT + QT], lc == 0, lc == 1)
                te_s = p.mark(TE, s_te)
                diag_o = kc - qt * NKQ
                if diag_o >= 0:
                    p.wait(VE, s_te, te_s)
                    p.on(VE, lambda e, o=sps[:, :], b=mask4[:, diag_o, :]:
                         e.tensor_tensor(o, o, b, OP.add))
                    v_m = p.mark(VE, s_ve)
                    p.wait(SE, s_ve, v_m)
                else:
                    p.wait(SE, s_te, te_s)
                pt = pT[:, kc % 3]
                p.on(SE, lambda e, o=pt, i=sps[:, :]:
                     e.activation(o, i, AF.Exp, bias=0.0, scale=1.0))
                se_exp[kc] = p.mark(SE, s_se)

            emit_S(0)
            for kc in range(nk):
                if kc + 1 < nk:
                    emit_S(kc + 1)
                p.wait(TE, s_se, se_exp[kc])
                pt = pT[:, kc % 3]
                for lc in range(2):
                    mm(cpair[lc][:, :], ckvtm[:, kc, lc * 128: lc * 128 + 128],
                       pt, kc == 0, kc == nk - 1)
                mm(dn_ps, ones_col[:, :], pt, kc == 0, kc == nk - 1)
            te_pv = p.mark(TE, s_te)
            p.wait(VE, s_te, te_pv)
            vcopy(VE, ctxT[:, 0], cpair[0][:, :])
            scopy2 = p.wait(SE, s_te, te_pv)
            scopy(ctxT[:, 1], cpair[1][:, :])
            se_cx = p.mark(SE, s_se)
            p.on(VE, lambda e, o=dn_row[0:1, :], i=dn_ps: e.reciprocal(o, i))
            v_cx = p.mark(VE, s_ve)
            p.wait(TE, s_ve, v_cx)
            p.wait(TE, s_se, se_cx)
            for tcq in range(NKQ):
                trans(PS[kc % 2][:, tcq: tcq + 1],
                      dn_row[0:1, tcq * 128: tcq * 128 + 128])
            te_tr = p.mark(TE, s_te)
            p.wait(VE, s_te, te_tr)
            for tcq in range(NKQ):
                vcopy(VE, recip_pp[:, tcq: tcq + 1], PS[kc % 2][:, tcq: tcq + 1])
            v_rc = p.mark(VE, s_ve)
            p.wait(TE, s_ve, v_rc)
            for tcq in range(NKQ):
                for lc in range(2):
                    mm(PS[(kc + 1) % 2][:, 0:64],
                       ctxT[:, lc, tcq * 128: tcq * 128 + 128],
                       ve_sb[:, lc * D + h * 64: lc * D + h * 64 + 64],
                       lc == 0, lc == 1)
                te_y = p.mark(TE, s_te)
                ev = VE if tcq % 2 == 0 else SE
                p.wait(ev, s_te, te_y)
                tcg = qt * NKQ + tcq
                if ev == VE:
                    p.on(VE, lambda e, o=y_sb[:, tcg, h * 64: h * 64 + 64],
                         i=PS[(kc + 1) % 2][:, 0:64], s=recip_pp[:, tcq: tcq + 1]:
                         e.tensor_scalar(o, i, s, None, OP.mult))
                    v_y = p.mark(VE, s_ve)
                    p.wait(TE, s_ve, v_y)
                else:
                    p.on(SE, lambda e, o=y_sb[:, tcg, h * 64: h * 64 + 64],
                         i=PS[(kc + 1) % 2][:, 0:64], s=recip_pp[:, tcq: tcq + 1]:
                         e.activation(o, i, AF.Copy, bias=0.0, scale=s))
                    se_y = p.mark(SE, s_se)
                    p.wait(TE, s_se, se_y)
    te_flash = p.mark(TE, s_te)
    barrier()
    p.ckpt("C")

    load_fm(sgT_sb.rearrange("p a b -> p (a b)"), sgT, IS, 8, s_shw)
    load_fm(suT_sb.rearrange("p a b -> p (a b)"), suT, IS, 8, s_shw)

    def emit_ln(src_f32_fn, tc, out_f32, out_bf16, extra_in=None):
        s1 = st_t[:, 0:1]; mean = st_t[:, 1:2]; s2 = st_t[:, 2:3]
        tt = st_t[:, 3:4]; rstd = st_t[:, 4:5]
        src_f32_fn(out_f32)
        p.on(VE, lambda e, o=s1, i=out_f32: e.reduce_sum(o, i, AX.X))
        p.on(VE, lambda e, o=mean, i=s1: e.tensor_scalar(o, i, 1.0 / D, None, OP.mult))
        p.on(VE, lambda e, o=lnjunk[:, :], i=out_f32, s=mean, b=out_f32, a=s2:
             e.scalar_tensor_tensor(o, i, s, b, OP.subtract, OP.mult, accum_out=a))
        v_st = p.mark(VE, s_ve)
        p.wait(SE, s_ve, v_st)
        p.on(SE, lambda e, o=tt, i=s2:
             e.activation(o, i, AF.Ln, bias=1e-5, scale=1.0 / D))
        p.on(SE, lambda e, o=rstd, i=tt:
             e.activation(o, i, AF.Exp, bias=0.0, scale=-0.5))
        se_r = p.mark(SE, s_se)
        p.wait(VE, s_se, se_r)
        p.on(VE, lambda e, o=out_f32, i=out_f32, s=mean, s2_=rstd:
             e.tensor_scalar(o, i, s, s2_, OP.subtract, OP.mult))
        if out_bf16 is not None:
            vcopy(VE, out_bf16, out_f32)

    d_ve_done = [0] * TC
    d_sy_done = [0] * TC
    for tc in range(TC):
        buf = tc % 2
        xt = xt_t[:, buf]
        x1f = x1f_t[:, buf]
        x1b = x1b_t[:, buf]
        if tc >= 2:
            p.wait(SY, s_ve, d_ve_done[tc - 2])
        w_xt = p.dma(SY, xt, x_tm[tc * 128:(tc + 1) * 128, :], s_xt[buf])
        p.wait(VE, s_xt[buf], w_xt)
        if tc >= 2:
            p.wait(VE, s_x1s[buf], d_sy_done[tc - 2])
        emit_ln(lambda o, tc_=tc, xt_=xt: p.on(
            VE, lambda e, oo=o, a=xt_, b=y_sb[:, tc_, :]:
            e.tensor_tensor(oo, a, b, OP.add)), tc, x1f, x1b)
        d_ve_done[tc] = p.mark(VE, s_ve)
        p.wait(SY, s_ve, d_ve_done[tc])
        p.dma(SY, x1f_dram[tc * 128:(tc + 1) * 128, :], x1f, s_x1s[buf])
        d_sy_done[tc] = p.dma(SY, x1bf_dram[tc * 128:(tc + 1) * 128, :], x1b,
                              s_x1s[buf])
    x1s_totals = [s_x1s[0].n, s_x1s[1].n]
    barrier()
    p.ckpt("D")
    load_fm(sdT_sb.rearrange("p a b -> p (a b)"), sdT, D, ISC, s_shw)
    w_shw = s_shw.n

    e_te_done = [0] * T5
    for t5 in range(T5):
        buf = t5 % 2
        xg5 = x1g_t[:, buf]
        if t5 >= 2:
            p.wait(SY, s_te, e_te_done[t5 - 2])
        if t5 == 0:
            p.wait(SY, s_x1s[0], x1s_totals[0])
            p.wait(SY, s_x1s[1], x1s_totals[1])
        w_tp = 0
        for dc in range(8):
            w_tp = p.dmat(SY, xg5[:, dc, :],
                          x1bf_dram[t5 * 512:(t5 + 1) * 512, dc * 128:(dc + 1) * 128],
                          s_tp[buf])
        p.wait(TE, s_tp[buf], w_tp)
        for sub in range(4):
            tc = t5 * 4 + sub
            for dc in range(8):
                mm(PS[0][:, 0:16], xg5[:, dc, sub * 128: sub * 128 + 128],
                   gwT_sb[:, dc * 16:(dc + 1) * 16], dc == 0, dc == 7)
            te_l = p.mark(TE, s_te)
            p.wait(VE, s_te, te_l)
            p.on(VE, lambda e, o=gat_s[:, 0:1], i=PS[0][:, 0:16]:
                 e.reduce_max(o, i, AX.X))
            p.on(VE, lambda e, o=logit_t[:, :], i=PS[0][:, 0:16], s=gat_s[:, 0:1]:
                 e.tensor_scalar(o, i, s, None, OP.subtract))
            v_l = p.mark(VE, s_ve)
            p.wait(SE, s_ve, v_l)
            p.on(SE, lambda e, o=sc_t[:, :], i=logit_t[:, :]:
                 e.activation(o, i, AF.Exp, bias=0.0, scale=1.0))
            se_l = p.mark(SE, s_se)
            p.wait(VE, s_se, se_l)
            p.on(VE, lambda e, o=gat_s[:, 1:2], i=sc_t[:, :]: e.reduce_sum(o, i, AX.X))
            p.on(VE, lambda e, o=gat_s[:, 2:3], i=gat_s[:, 1:2]: e.reciprocal(o, i))
            p.on(VE, lambda e, o=sc_t[:, :], i=sc_t[:, :], s=gat_s[:, 2:3]:
                 e.tensor_scalar(o, i, s, None, OP.mult))
            v_s = p.mark(VE, s_ve)
            p.wait(TE, s_ve, v_s)
            trans(PS[1][0:16, 0:128], sc_t[:, :])
            te_sc = p.mark(TE, s_te)
            p.wait(VE, s_te, te_sc)
            vcopy(VE, scT[0:16, tc * 128:(tc + 1) * 128], PS[1][0:16, 0:128])
            v_sc = p.mark(VE, s_ve)
            p.wait(TE, s_ve, v_sc)
        e_te_done[t5] = p.mark(TE, s_te)
    v_gate = s_ve.n
    p.wait(SY, s_ve, v_gate)
    w_ccin = p.dma(SY, cc_in[:, :], scT[0:16, :], s_ccb)
    p.wait(GE, s_ccb, w_ccin)
    v_cc = s_cc.inc()
    p._dma_raw(GE, lambda e: e.collective_compute(
        "AllGather", OP.bypass, replica_groups=[list(range(NR))],
        ins=[cc_in[:, :]], outs=[cc_out[:, :]]).then_inc(s_cc.h, 1))
    p.wait(SY, s_cc, v_cc)
    cc_v = cc_out.rearrange("(r e) j -> r e j", e=16)
    for e_ in range(E):
        p.dma(SY, gsc[e_ * (NR): (e_ + 1) * NR, :], cc_v[:, e_, :], s_ccb)
    w_gsc = s_ccb.n

    barrier()
    p.ckpt("E")

    for e_ in range(E):
        p.dma(SY, sloc[:, e_, :],
              cc_in[e_: e_ + 1, :].rearrange("o (q s) -> o q s", q=16), s_ccb)
    w_route = s_ccb.n
    p.wait(VE, s_ccb, w_route)
    p.on(VE, lambda e: e.memset(lo_v[:, :], 0.0))
    p.on(VE, lambda e: e.memset(hi_v[:, :], 1.0))
    p.on(VE, lambda e: e.memset(tau_pp[:, :], 0.5))
    p.on(VE, lambda e: e.memset(mid_v[:, 0:1], 0.5))
    p.mark(VE, s_ve)

    CSEL = float(cfg.CSEL)
    tau_ready = []

    def emit_bisect_iter(it):
        if it > 0:
            p.wait(VE, s_ve, tau_ready[it - 1])
        p.on(VE, lambda e, o=junk_bf[:, :], i=gsc[:, :], s=tau_pp[:, :], a=cnt_pp[:, :]:
             e.tensor_scalar(o, i, s, None, OP.is_ge, OP.add, accum_out=a))
        v_c = p.mark(VE, s_ve)
        p.wait(TE, s_ve, v_c)
        mm(PS[7][0:16, 0:1], blk[:, :], cnt_pp[:, :], True, True)
        te_s_ = p.mark(TE, s_te)
        p.wait(VE, s_te, te_s_)
        vcopy(VE, cnt_e[:, :], PS[7][0:16, 0:1])
        ge_ = mid_v[:, 1:2]; t0 = mid_v[:, 2:3]; t1 = mid_v[:, 3:4]; mid = mid_v[:, 0:1]
        p.on(VE, lambda e, o=ge_, i=cnt_e[:, :]:
             e.tensor_scalar(o, i, CSEL, None, OP.is_ge))
        p.on(VE, lambda e, o=t0, a=mid, b=lo_v[:, :]: e.tensor_tensor(o, a, b, OP.subtract))
        p.on(VE, lambda e, o=t0, a=t0, b=ge_: e.tensor_tensor(o, a, b, OP.mult))
        p.on(VE, lambda e, o=lo_v[:, :], a=lo_v[:, :], b=t0: e.tensor_tensor(o, a, b, OP.add))
        p.on(VE, lambda e, o=t1, a=mid, b=hi_v[:, :]: e.tensor_tensor(o, a, b, OP.subtract))
        p.on(VE, lambda e, o=t0, i=ge_: e.tensor_scalar(o, i, -1.0, 1.0, OP.mult, OP.add))
        p.on(VE, lambda e, o=t1, a=t1, b=t0: e.tensor_tensor(o, a, b, OP.mult))
        p.on(VE, lambda e, o=hi_v[:, :], a=hi_v[:, :], b=t1: e.tensor_tensor(o, a, b, OP.add))
        p.on(VE, lambda e, o=mid, a=lo_v[:, :], b=hi_v[:, :]: e.tensor_tensor(o, a, b, OP.add))
        p.on(VE, lambda e, o=mid, i=mid: e.tensor_scalar(o, i, 0.5, None, OP.mult))
        v_u = p.mark(VE, s_ve)
        p.wait(TE, s_ve, v_u)
        mm(PS[7][:, 0:1], blkT[:, :], mid, True, True)
        te_b_ = p.mark(TE, s_te)
        p.wait(VE, s_te, te_b_)
        vcopy(VE, tau_pp[:, :], PS[7][:, 0:1])
        tau_ready.append(p.mark(VE, s_ve))

    f_te_done = [0] * T5
    sh_hist = []

    def emit_shared_t5(t5):
        buf = t5 % 2
        xs5 = x1s_t[:, buf]
        if t5 == 0:
            p.wait(SY, s_ccb, w_route)
        if t5 >= 2:
            p.wait(SY, s_te, f_te_done[t5 - 2])
        w_tp = 0
        for dc in range(8):
            w_tp = p.dmat(SY, xs5[:, dc, :],
                          x1bf_dram[t5 * 512:(t5 + 1) * 512, dc * 128:(dc + 1) * 128],
                          s_tp[buf])
        p.wait(TE, s_tp[buf], w_tp)
        if t5 == 0:
            p.wait(TE, s_shw, w_shw)
        v_hb = {}
        for ic in range(ISC):
            pr = (PS[0], PS[1]) if ic % 2 == 0 else (PS[4], PS[5])
            gt = gtmp[:, ic % 2, :]
            if ic >= 2:
                p.wait(TE, s_ve, v_hb[ic - 2])
            for which, wsb in ((0, sgT_sb), (1, suT_sb)):
                for dc in range(8):
                    mm(pr[which][:, :], wsb[:, dc, ic * 128: ic * 128 + 128],
                       xs5[:, dc, :], dc == 0, dc == 7)
            te_g = p.mark(TE, s_te)
            p.wait(SE, s_te, te_g)
            p.on(SE, lambda e, o=gt, i=pr[0][:, :]:
                 e.activation(o, i, AF.Gelu, bias=0.0, scale=1.0))
            se_g = p.mark(SE, s_se)
            p.wait(VE, s_se, se_g)
            p.on(VE, lambda e, o=hs_sb[:, ic, :], a=gt, b=pr[1][:, :]:
                 e.tensor_tensor(o, a, b, OP.mult))
            v_hb[ic] = p.mark(VE, s_ve)
        p.wait(TE, s_ve, v_hb[ISC - 1])
        p.wait(TE, s_ve, v_hb[ISC - 2])
        for tcq in range(4):
            for ns in range(2):
                for ic in range(ISC):
                    mm(PS[2 + ns][:, :], hs_sb[:, ic, tcq * 128: tcq * 128 + 128],
                       sdT_sb[:, ic, ns * 512: ns * 512 + 512], ic == 0, ic == ISC - 1)
            te_d = p.mark(TE, s_te)
            p.wait(VE, s_te, te_d)
            sbuf_ = tcq % 2
            sh = sh_t[:, sbuf_]
            if len(sh_hist) >= 2:
                slot, vv = sh_hist[-2]
                p.wait(VE, s_msh[slot], vv)
            for ns in range(2):
                vcopy(VE, sh[:, ns * 512:(ns + 1) * 512], PS[2 + ns][:, :])
            v_sh = p.mark(VE, s_ve)
            p.wait(TE, s_ve, v_sh)
            p.wait(SY, s_ve, v_sh)
            tcg = t5 * 4 + tcq
            slot = len(sh_hist) % 2
            sh_hist.append(
                (slot, p.dma(SY, moe_dram[tcg * 128:(tcg + 1) * 128, :], sh,
                             s_msh[slot])))
        f_te_done[t5] = p.mark(TE, s_te)

    for u in range(max(cfg.BIS, T5)):
        if u < cfg.BIS:
            emit_bisect_iter(u)
        if u < T5:
            emit_shared_t5(u)
    msh_totals = [s_msh[0].n, s_msh[1].n]

    p.wait(TE, s_ve, s_ve.n)
    trans(PS[7][0:1, 0:16], lo_v[:, :])
    te_lr = p.mark(TE, s_te)
    p.wait(VE, s_te, te_lr)
    vcopy(VE, lo_row[0:1, :], PS[7][0:1, 0:16])
    v_lr = p.mark(VE, s_ve)
    p.wait(TE, s_ve, v_lr)
    mm(PS[7][0:16, 0:16], ones_1_16[0:1, :], lo_row[0:1, :], True, True)
    te_t16 = p.mark(TE, s_te)
    p.wait(VE, s_te, te_t16)
    vcopy(VE, tau16[:, :], PS[7][0:16, 0:16])
    v_tau = p.mark(VE, s_ve)

    barrier()
    p.ckpt("F")

    p._dma_raw(GE, lambda e: e.load_library(library_config.sparse_gather))
    p.ckpt("Ga")
    sg_done = [0] * E
    cast_done = [0] * E
    for e_ in range(E):
        sl = e_ % 2
        if e_ >= 2:
            p.wait(VE, s_ge, sg_done[e_ - 2])
        p.on(VE, lambda e, o=mids[:, sl, :], i=sloc[:, e_, :],
             s=tau16[:, e_: e_ + 1]: e.tensor_scalar(o, i, s, None, OP.is_ge))
        p.on(VE, lambda e, o=mids[:, sl, :], a=mids[:, sl, :], b=idsp1[:, :]:
             e.tensor_tensor(o, a, b, OP.mult))
        p.on(VE, lambda e, o=mids[:, sl, :], i=mids[:, sl, :]:
             e.tensor_scalar(o, i, -1.0, None, OP.add))
        v_m = p.mark(VE, s_ve)
        p.wait(GE, s_ve, v_m)
        if e_ >= 2:
            p.wait(GE, s_ve, cast_done[e_ - 2])
        p.on(GE, lambda e, o=comp[:, sl, :], i=mids[:, sl, :],
             n=nf_u[0:1, e_: e_ + 1]: e.sparse_gather(o, i, num_found=n))
        sg_done[e_] = p.mark(GE, s_ge)
        p.wait(VE, s_ge, sg_done[e_])
        vcopy(VE, idx16[0:16, e_, :], comp[:, sl, 0:NIDX])
        cast_done[e_] = p.mark(VE, s_ve)
        if e_ == 0:
            p.ckpt("Gb")
    p.ckpt("Gc")
    p._dma_raw(GE, lambda e: e.load_library(library_config.mlp))
    p.on(GE, lambda e: e.nop())
    ge_mlp = s_ge.n
    p.ckpt("Gd")
    p.wait(VE, s_ge, sg_done[E - 1])
    vcopy(VE, nf_f[:, :], nf_u[:, :])
    p.on(VE, lambda e, o=nf_f[:, :], i=nf_f[:, :]:
         e.tensor_scalar(o, i, float(CAP), None, OP.min))
    vcopy(VE, nf_i[:, :], nf_f[:, :])
    v_nf = p.mark(VE, s_ve)
    p.wait(TE, s_ve, v_nf)
    mm(PS[7][0:16, 0:16], ones_1_16[0:1, :], nf_f[0:1, :], True, True)
    te_nf = p.mark(TE, s_te)
    p.wait(VE, s_te, te_nf)
    vcopy(VE, nf16[:, :], PS[7][0:16, 0:16])
    for e_ in range(E):
        p.on(VE, lambda e, o=mwin[:, :], i=rk1[:, :], s=nf16[:, e_: e_ + 1]:
             e.tensor_scalar(o, i, s, None, OP.is_le))
        p.on(VE, lambda e, o=i16tmp[:, :], a=idx16[0:16, e_, :],
             b=mwin[:, :]: e.tensor_tensor(o, a, b, OP.mult))
        p.on(VE, lambda e, o=idx16[0:16, e_, :], a=mwin[:, :],
             b=i16tmp[:, :]:
             e.scalar_tensor_tensor(o, a, -1.0, b, OP.add, OP.add))
    v_idx = p.mark(VE, s_ve)
    p.ckpt("Ge")
    p.wait(SY, s_tp[0], s_tp[0].n)
    p.wait(SY, s_tp[1], s_tp[1].n)
    p.wait(SY, s_ve, v_idx)
    i16f = idx16.rearrange("p a b -> p (a b)")
    w_r1 = p.dma(SY, i16f[16:32, :], i16f[0:16, :], s_idx)
    p.wait(SY, s_idx, w_r1)
    w_r2 = p.dma(SY, i16f[32:64, :], i16f[0:32, :], s_idx)
    p.wait(SY, s_idx, w_r2)
    w_idxrep = p.dma(SY, i16f[64:128, :], i16f[0:64, :], s_idx)
    p.wait(SY, s_idx, w_idxrep)
    p.ckpt("Gf")

    NSL = E * nIS
    te_end = {}
    wgwu_ready = {}

    def fetch_wgwu(k, wait_te=None):
        e_, isl = divmod(k, nIS)
        buf = k % 3
        if wait_te is not None:
            p.wait(GE, s_te, wait_te)
        p.dma(GE, wg_buf[:, buf].rearrange("p a b -> p (a b)"),
              wgT_p[e_, isl].rearrange("p a b -> p (a b)"), s_wg[buf])
        p.dma(GE, wu_buf[:, buf].rearrange("p a b -> p (a b)"),
              wuT_p[e_, isl].rearrange("p a b -> p (a b)"), s_wg[buf])
        wgwu_ready[k] = (buf, s_wg[buf].n)

    def fetch_wd(e_):
        for ws in range(I // 512):
            p.dma(GE, wd_buf[:, ws * 4:(ws + 1) * 4].rearrange("p a b -> p (a b)"),
                  wdT_p[e_, ws].rearrange("p a b -> p (a b)"), s_wd)
        return s_wd.n

    p.ckpt("G0")
    NSPL = [(0, 512), (512, CAP - 512)] if CAP > 512 else [(0, CAP)]
    gat_ready = [None] * E
    reg_n = [None] * E

    def emit_gather(e_, wait_te=None):
        if wait_te is not None:
            p.wait(GE, s_te, wait_te)
        p.wait(GE, s_ge, ge_mlp)
        p.wait(GE, s_ve, v_idx)
        p.wait(GE, s_idx, w_idxrep)

        def ld(eng, ee=e_):
            eng.load(nreg[ee % 2], nf_i[0:1, ee: ee + 1])
            reg_n[ee] = nreg[ee % 2]
        p.on(GE, ld)
        buf = e_ % 2
        gat_ready[e_] = s_gt[buf].inc(16)
        p._dma_raw(GE, lambda e, b=buf, ee=e_:
                   e.dma_gather(xg_sb[:, b], x1bf_dram[:, :],
                                i16f[:, ee * NIDX: (ee + 1) * NIDX],
                                CAP, reg_n[ee], D,
                                transpose=True).then_inc(s_gt[b].h, 16))

    p.wait(GE, s_x1s[0], x1s_totals[0])
    p.wait(GE, s_x1s[1], x1s_totals[1])
    emit_gather(0)
    fetch_wgwu(0)
    fetch_wgwu(1)
    wd_ready = [None] * E
    wd_ready[0] = fetch_wd(0)
    p.ckpt("G")

    te_after_up = [0] * E
    te_after_down = [0] * E
    sca_of = [0] * E
    for e_ in range(E):
        xg = xg_sb[:, e_ % 2]
        p.wait(TE, s_gt[e_ % 2], gat_ready[e_])
        v_ub = {}
        bi = 0
        for isl in range(nIS):
            k = e_ * nIS + isl
            wbuf_, wv_ = wgwu_ready[k]
            p.wait(TE, s_wg[wbuf_], wv_)
            for mc in range(IB // 128):
                img = isl * (IB // 128) + mc
                for (n0, nw) in NSPL:
                    pr = (PS[0], PS[1]) if bi % 2 == 0 else (PS[2], PS[3])
                    mg = moegelu[:, bi % 2, 0:nw]
                    if bi >= 2:
                        p.wait(TE, s_ve, v_ub[bi - 2])
                    for which, wb in ((0, wg_buf), (1, wu_buf)):
                        for dc in range(8):
                            mm(pr[which][:, 0:nw],
                               wb[:, wbuf_, dc, mc * 128: mc * 128 + 128],
                               xg[:, dc, n0: n0 + nw], dc == 0, dc == 7)
                    te_u = p.mark(TE, s_te)
                    p.wait(SE, s_te, te_u)
                    p.on(SE, lambda e, o=mg, i=pr[0][:, 0:nw]:
                         e.activation(o, i, AF.Gelu, bias=0.0, scale=1.0))
                    se_u = p.mark(SE, s_se)
                    p.wait(VE, s_se, se_u)
                    p.on(VE, lambda e, o=h_sb[:, img, n0: n0 + nw],
                         a=mg, b=pr[1][:, 0:nw]:
                         e.tensor_tensor(o, a, b, OP.mult))
                    v_ub[bi] = p.mark(VE, s_ve)
                    bi += 1
            te_end[k] = s_te.n
            if k + 2 < NSL:
                fetch_wgwu(k + 2, wait_te=(te_end[k - 1] if k >= 1 else None))
            if isl == 2 and e_ + 1 < E:
                emit_gather(e_ + 1,
                            wait_te=(te_after_down[e_ - 1] if e_ >= 1 else None))
        p.wait(TE, s_ve, v_ub[bi - 1])
        p.wait(TE, s_ve, v_ub[bi - 2])
        te_after_up[e_] = p.mark(TE, s_te)
        p.wait(TE, s_wd, wd_ready[e_])
        for tcq in range(NCAP):
            for dc in range(8):
                mm(PS[4][:, 0:16], xg[:, dc, tcq * 128: tcq * 128 + 128],
                   gwT_sb[:, dc * 16:(dc + 1) * 16], dc == 0, dc == 7)
            te_gl = p.mark(TE, s_te)
            p.wait(VE, s_te, te_gl)
            p.on(VE, lambda e, o=gat_s[:, 0:1], i=PS[4][:, 0:16]:
                 e.reduce_max(o, i, AX.X))
            p.on(VE, lambda e, o=gat_t[:, :], i=PS[4][:, 0:16], s=gat_s[:, 0:1]:
                 e.tensor_scalar(o, i, s, None, OP.subtract))
            v_gl = p.mark(VE, s_ve)
            p.wait(SE, s_ve, v_gl)
            p.on(SE, lambda e, o=gat_t[:, :], i=gat_t[:, :]:
                 e.activation(o, i, AF.Exp, bias=0.0, scale=1.0))
            se_gl = p.mark(SE, s_se)
            p.wait(VE, s_se, se_gl)
            p.on(VE, lambda e, o=gat_s[:, 1:2], i=gat_t[:, :]: e.reduce_sum(o, i, AX.X))
            p.on(VE, lambda e, o=gat_s[:, 2:3], i=gat_s[:, 1:2]: e.reciprocal(o, i))
            p.on(VE, lambda e, o=gat_s[:, 3:4], a=gat_t[:, e_: e_ + 1], b=gat_s[:, 2:3]:
                 e.tensor_tensor(o, a, b, OP.mult))
            v_gt = p.mark(VE, s_ve)
            for ns in range(2):
                for ic in range(IC):
                    mm(PS[5 + ns][:, :], h_sb[:, ic, tcq * 128: tcq * 128 + 128],
                       wd_buf[:, ic, ns * 512: ns * 512 + 512], ic == 0, ic == IC - 1)
            te_d = p.mark(TE, s_te)
            p.wait(VE, s_te, te_d)
            if tcq == 0 and e_ >= 1:
                p.wait(VE, s_sca, sca_of[e_ - 1])
                p.wait(SE, s_sca, sca_of[e_ - 1])
            p.on(VE, lambda e, o=eo_sb[:, tcq, 0:512],
                 i=PS[5][:, :], s=gat_s[:, 3:4]:
                 e.tensor_scalar(o, i, s, None, OP.mult))
            v_eo = p.mark(VE, s_ve)
            p.wait(SE, s_te, te_d)
            p.wait(SE, s_ve, v_gt)
            p.on(SE, lambda e, o=eo_sb[:, tcq, 512:1024],
                 i=PS[6][:, :], s=gat_s[:, 3:4]:
                 e.activation(o, i, AF.Copy, bias=0.0, scale=s))
            se_eo = p.mark(SE, s_se)
            p.wait(TE, s_ve, v_eo)
            p.wait(TE, s_se, se_eo)
        te_after_down[e_] = p.mark(TE, s_te)
        v_eo_done = s_ve.n
        se_eo_done = s_se.n
        if e_ + 1 < E:
            p.wait(GE, s_ve, v_eo_done)
            wd_ready[e_ + 1] = fetch_wd(e_ + 1)
        p.wait(GE, s_ve, v_eo_done)
        p.wait(GE, s_se, se_eo_done)
        if e_ == 0:
            p.wait(GE, s_msh[0], msh_totals[0])
            p.wait(GE, s_msh[1], msh_totals[1])
        if e_ >= 1:
            p.wait(GE, s_sca, sca_of[e_ - 1])
        sca_of[e_] = s_sca.inc(16)
        p._dma_raw(GE, lambda e, ee=e_:
                   e.dma_scatter_add(moe_dram[:, :], eo_sb,
                                     i16f[:, ee * NIDX: (ee + 1) * NIDX],
                                     CAP, reg_n[ee], D).then_inc(s_sca.h, 16))
    scatters_done = s_sca.n
    barrier()
    p.ckpt("H")

    p.wait(SY, s_sca, scatters_done)
    i_ve_done = [0] * TC
    ou_done = [0] * TC
    for tc in range(TC):
        buf = tc % 2
        mo = mo_t[:, buf]
        xf2 = xf2_t[:, buf]
        ot = o_t[:, buf]
        if tc >= 2:
            p.wait(SY, s_ve, i_ve_done[tc - 2])
        p.dma(SY, mo, moe_dram[tc * 128:(tc + 1) * 128, :], s_rb[buf])
        w_rb = p.dma(SY, xf2, x1f_dram[tc * 128:(tc + 1) * 128, :], s_rb[buf])
        p.wait(VE, s_rb[buf], w_rb)
        if tc >= 2:
            p.wait(VE, s_ou[buf], ou_done[tc - 2])
        emit_ln(lambda o, m_=mo, x_=xf2: p.on(
            VE, lambda e, oo=o, a=x_, b=m_: e.tensor_tensor(oo, a, b, OP.add)),
            tc, ot, None)
        i_ve_done[tc] = p.mark(VE, s_ve)
        p.wait(SY, s_ve, i_ve_done[tc])
        ou_done[tc] = p.dma(SY, out[tc * 128:(tc + 1) * 128, :], ot, s_ou[buf])
    p.wait(SY, s_ou[0], s_ou[0].n)
    p.wait(SY, s_ou[1], s_ou[1].n)

    if cfg.stop:
        n = p.ckpts[cfg.stop]
        for e in ENGINES:
            p.ops[e] = p.ops[e][:n[e]]
        srcd = x1f_dram if cfg.stop in ("D", "E", "F", "G", "G0", "Ga", "Gb", "Gc", "Gd", "Ge", "Gf") else \
            (moe_dram if cfg.stop == "H" else x1bf_dram)
        dbg_done = [0] * TC
        for tc in range(TC):
            buf = tc % 2
            ot = o_t[:, buf]
            if tc >= 2:
                p.wait(SY, s_dbo[buf], dbg_done[tc - 2])
            if cfg.stop in ("B", "C"):
                p.wait(SY, s_dbg[buf], s_dbg[buf].n)
                w = p.dma(SY, ot[0:128, 0:1], x_tm[tc * 128:(tc + 1) * 128, 0:1],
                          s_dbg[buf])
            else:
                w = p.dma(SY, ot, srcd[tc * 128:(tc + 1) * 128, :], s_dbg[buf])
            p.wait(SY, s_dbg[buf], w)
            dbg_done[tc] = p.dma(SY, out[tc * 128:(tc + 1) * 128, :], ot,
                                 s_dbo[buf])
        p.wait(SY, s_dbo[0], s_dbo[0].n)
        p.wait(SY, s_dbo[1], s_dbo[1].n)

    p.run()
    lower_extended_insts(nc)
    return nc


def kernel(**inputs):
    cfg = Cfg()
    com, cores = host_prep(cfg, inputs)
    nc = build(cfg)
    in_maps = []
    for b in range(cfg.n_cores):
        m = dict(com)
        m.update(cores[b])
        in_maps.append(m)
    res = run_bass_kernel_spmd(nc, in_maps, core_ids=list(range(cfg.n_cores)))
    outs = [np.asarray(r["out"]) for r in res.results]
    return np.stack(outs, axis=0)


# version 06
# speedup vs baseline: 1.7796x; 1.0610x over previous
import numpy as np
import concourse.bass as bass
import concourse.mybir as mybir
from concourse.bass_utils import run_bass_kernel_spmd
from concourse import library_config
from concourse.library_overlay import lower_extended_insts

F32 = mybir.dt.float32
BF16 = mybir.dt.bfloat16
I16 = mybir.dt.int16
I32 = mybir.dt.int32
U32 = mybir.dt.uint32
AX = mybir.AxisListType
OP = mybir.AluOpType
AF = mybir.ActivationFunctionType

TE, VE, SE, GE, SY = "tensor", "vector", "scalar", "gpsimd", "sync"
ENGINES = (TE, VE, SE, GE, SY)

PH = {n: i for i, n in enumerate("ABCDEFGHI")}


class Cfg:
    def __init__(self, T=4096, I=4096, IS=2048, CAP=640, BIS=36, n_cores=8):
        self.T = T
        self.D = 1024
        self.NH = 16
        self.HS = 64
        self.KVL = 256
        self.E = 16
        self.I = I
        self.IS = IS
        self.CAP = CAP
        self.BIS = BIS
        self.n_cores = n_cores
        self.CSEL = n_cores * T * 2 // 16
        self.QT = 512
        self.IB = min(256, I)
        import os
        self.stop = os.environ.get("KSTOP", "")
        assert T % 512 == 0 and I % self.IB == 0 and CAP % 128 == 0


class Sem:
    def __init__(self, nc, name):
        self.h = nc.semaphore(name).__enter__()
        self.n = 0

    def inc(self, k=1):
        self.n += k
        return self.n


class Arena:

    def __init__(self, nc, name, dtype, esz):
        self.nc, self.name, self.dtype, self.esz = nc, name, dtype, esz
        self.reqs = []
        self.views = {}

    def alloc(self, name, shape, ph0, ph1):
        self.reqs.append((name, shape, PH[ph0], PH[ph1]))
        return name

    def finalize(self):
        placed = []
        for name, shape, p0, p1 in sorted(self.reqs, key=lambda r: -np.prod(r[1])):
            cols = int(np.prod(shape[1:]))
            cands = sorted({0} | {off + c for off, c, q0, q1, *_ in placed
                                  if not (q1 < p0 or p1 < q0)})
            for off in cands:
                ok = True
                for o2, c2, q0, q1, *_ in placed:
                    if q1 < p0 or p1 < q0:
                        continue
                    if not (off + cols <= o2 or o2 + c2 <= off):
                        ok = False
                        break
                if ok:
                    placed.append((off, cols, p0, p1, name, shape))
                    break
        total = max(o + c for o, c, *_ in placed) if placed else 0
        import os
        if os.environ.get("ARENA_DEBUG"):
            print(f"arena {self.name}: total {total} cols = {total*self.esz} B/part")
            for off, cols, p0, p1, name, shape in sorted(placed):
                print(f"  {off:7d}..{off+cols:7d} ph{p0}-{p1} {name} {shape}")
        base = self.nc.alloc_sbuf_tensor(self.name, [128, total], self.dtype).ap()
        for off, cols, p0, p1, name, shape in placed:
            v = base[0:shape[0], off:off + cols]
            if len(shape) > 2:
                pat = "p (" + " ".join(f"a{i}" for i in range(len(shape) - 1)) + ") -> p " \
                      + " ".join(f"a{i}" for i in range(len(shape) - 1))
                kw = {f"a{i}": shape[i + 1] for i in range(len(shape) - 1)}
                v = v.rearrange(pat, **kw)
            self.views[name] = v
        self.total_bytes = total * self.esz
        return self.views


class Prog:
    def __init__(self, nc):
        self.nc = nc
        self.ops = {e: [] for e in ENGINES}
        self.last = {e: None for e in ENGINES}
        self.n_real = {e: 0 for e in ENGINES}
        self.mark_at = {e: -1 for e in ENGINES}
        self.mark_vals = {e: {} for e in ENGINES}
        self.auto = {}

    def on(self, eng, fn):
        a = self.auto.get(eng)
        if a is not None:
            prev = a.n
            self.ops[eng].append(lambda e, s=a.h, vv=prev: e.wait_ge(s, vv))
            v = a.inc()

            def wrapped(e, f=fn, eng_=eng, s=a.h):
                r = f(e)
                if r is not None:
                    self.last[eng_] = r
                    r.then_inc(s, 1)
                else:
                    e.nop().then_inc(s, 1)
            self.ops[eng].append(wrapped)
            self.n_real[eng] += 1
            return
        def wrapped(e, f=fn, eng_=eng):
            r = f(e)
            if r is not None:
                self.last[eng_] = r
        self.ops[eng].append(wrapped)
        self.n_real[eng] += 1

    def wait(self, eng, sem, val=None):
        v = sem.n if val is None else val
        self.ops[eng].append(lambda e, s=sem.h, vv=v: e.wait_ge(s, vv))

    def mark(self, eng, sem):
        a = self.auto.get(eng)
        if a is not None:
            assert sem is a, (eng, "marks on auto engines use the auto sem")
            return a.n
        if self.mark_at[eng] == self.n_real[eng]:
            if sem in self.mark_vals[eng]:
                return self.mark_vals[eng][sem]
            o_sem, o_v = next(iter(self.mark_vals[eng].items()))
            self.wait(eng, o_sem, o_v)
            v = sem.inc()
            self.ops[eng].append(lambda e, s=sem.h: e.nop().then_inc(s, 1))
            self.mark_vals[eng][sem] = v
            return v
        v = sem.inc()
        self.ops[eng].append(lambda e, s=sem.h, eng_=eng:
                             self.last[eng_].then_inc(s, 1))
        self.mark_at[eng] = self.n_real[eng]
        self.mark_vals[eng] = {sem: v}
        return v

    def ckpt(self, tag):
        if not hasattr(self, "ckpts"):
            self.ckpts = {}
        self.ckpts[tag] = {e: len(self.ops[e]) for e in ENGINES}

    def _dma_raw(self, eng, fn):
        def wrapped(e, f=fn, eng_=eng):
            r = f(e)
            if r is not None:
                self.last[eng_] = r
        self.ops[eng].append(wrapped)
        self.n_real[eng] += 1

    def dma(self, eng, out_ap, in_ap, sem, **kw):
        v = sem.inc(16)
        self._dma_raw(eng, lambda e, o=out_ap, i=in_ap, s=sem.h, k=dict(kw):
                      e.dma_start(out=o, in_=i, **k).then_inc(s, 16))
        return v

    def dmat(self, eng, out_ap, in_ap, sem):
        v = sem.inc(16)
        self._dma_raw(eng, lambda e, o=out_ap, i=in_ap, s=sem.h:
                      e.dma_start_transpose(o, i).then_inc(s, 16))
        return v

    def run(self):
        nc = self.nc
        with nc.Block() as block:
            @block.tensor
            def _(eng):
                for f in self.ops[TE]:
                    f(eng)

            @block.vector
            def _(eng):
                for f in self.ops[VE]:
                    f(eng)

            @block.scalar
            def _(eng):
                for f in self.ops[SE]:
                    f(eng)

            @block.gpsimd
            def _(eng):
                for f in self.ops[GE]:
                    f(eng)

            @block.sync
            def _(eng):
                for f in self.ops[SY]:
                    f(eng)


def host_prep(cfg, inputs):
    T, D, I, IS, E, IB = cfg.T, cfg.D, cfg.I, cfg.IS, cfg.E, cfg.IB
    f = np.ascontiguousarray
    x = np.asarray(inputs["L0_attn_x"], dtype=np.float32)
    com = {}
    com["wdqT"] = f(np.asarray(inputs["L0_attn_W_dq"], np.float32).T)
    com["wdq"] = f(np.asarray(inputs["L0_attn_W_dq"], np.float32))
    com["wuq"] = f(np.asarray(inputs["L0_attn_W_uq"], np.float32))
    com["wdkvT"] = f(np.asarray(inputs["L0_attn_W_dkv"], np.float32).T)
    com["wuk"] = f(np.asarray(inputs["L0_attn_W_uk"], np.float32))
    com["wuv"] = f(np.asarray(inputs["L0_attn_W_uv"], np.float32))
    com["woT"] = f(np.asarray(inputs["L0_attn_W_o"], np.float32).T)
    com["gwT"] = f(np.asarray(inputs["L0_ffn_gate_w"], np.float32).T)
    com["sgT"] = f(np.asarray(inputs["L0_ffn_Sg"], np.float32).T)
    com["suT"] = f(np.asarray(inputs["L0_ffn_Su"], np.float32).T)
    com["sdT"] = f(np.asarray(inputs["L0_ffn_Sd"], np.float32).T)
    wg = np.asarray(inputs["L0_ffn_Wg"], np.float32)
    wu = np.asarray(inputs["L0_ffn_Wu"], np.float32)
    wd = np.asarray(inputs["L0_ffn_Wd"], np.float32)
    nIS = I // IB
    com["wgT_p"] = f(wg.reshape(E, nIS, IB, 8, 128).transpose(0, 1, 4, 3, 2))
    com["wuT_p"] = f(wu.reshape(E, nIS, IB, 8, 128).transpose(0, 1, 4, 3, 2))
    nWS = I // 512
    com["wdT_p"] = f(wd.transpose(0, 2, 1).reshape(E, nWS, 4, 128, D)
                     .transpose(0, 1, 3, 2, 4))
    com["c_ident"] = np.eye(128, dtype=np.float32)
    m4 = np.zeros((4, 128, cfg.QT), dtype=np.float32)
    r = np.arange(128)[:, None]
    c = np.arange(cfg.QT)[None, :]
    for o in range(4):
        m4[o][(o * 128 + r) > c] = -1e30
    com["c_mask4"] = m4
    com["c_ones_col"] = np.ones((128, 1), dtype=np.float32)
    SX = T // 16
    ids = np.arange(16)[:, None] * SX + np.arange(SX)[None, :]
    com["c_idsp1"] = (ids + 1).astype(np.float32)
    NIDX = cfg.CAP // 16
    rk = np.arange(NIDX)[None, :] * 16 + np.arange(16)[:, None]
    com["c_rk1"] = (rk + 1).astype(np.float32)
    blkm = (np.arange(128)[:, None] // 8 == np.arange(16)[None, :]).astype(np.float32)
    com["c_blk"] = f(blkm)
    com["c_blkT"] = f(blkm.T)
    com["c_ones_1_16"] = np.ones((1, 16), dtype=np.float32)
    cores = []
    for b in range(cfg.n_cores):
        xb = f(x[b])
        cores.append({
            "x_tm": xb,
            "x_fmt": f(xb.T.reshape(D, T // 512, 512).transpose(1, 0, 2)),
        })
    return com, cores


def build(cfg):
    T, D, I, IS, E, CAP, IB, QT = (cfg.T, cfg.D, cfg.I, cfg.IS, cfg.E,
                                   cfg.CAP, cfg.IB, cfg.QT)
    NH = cfg.NH
    TC = T // 128
    T5 = T // 512
    nIS = I // IB
    IC = I // 128
    ISC = IS // 128
    NCAP = CAP // 128
    SX = T // 16
    NR = cfg.n_cores
    NKQ = QT // 128

    nc = bass.Bass("TRN2", target_bir_lowering=False, debug=False)

    def din(name, shape):
        return nc.dram_tensor(name, list(shape), F32, kind="ExternalInput").ap()

    x_tm = din("x_tm", (T, D))
    x_fmt = din("x_fmt", (T5, D, 512))
    wdqT = din("wdqT", (D, 256)); wdq = din("wdq", (256, D))
    wuq = din("wuq", (D, 256)); wdkvT = din("wdkvT", (D, 256))
    wuk = din("wuk", (D, 256)); wuv = din("wuv", (D, 256))
    woT = din("woT", (D, D)); gwT = din("gwT", (D, 16))
    sgT = din("sgT", (D, IS)); suT = din("suT", (D, IS)); sdT = din("sdT", (IS, D))
    wgT_p = din("wgT_p", (E, nIS, 128, 8, IB))
    wuT_p = din("wuT_p", (E, nIS, 128, 8, IB))
    wdT_p = din("wdT_p", (E, I // 512, 128, 4, D))
    c_ident = din("c_ident", (128, 128))
    c_mask4 = din("c_mask4", (4, 128, QT))
    c_ones_col = din("c_ones_col", (128, 1))
    c_idsp1 = din("c_idsp1", (16, SX))
    c_rk1 = din("c_rk1", (16, CAP // 16))
    c_blk = din("c_blk", (128, 16))
    c_blkT = din("c_blkT", (16, 128))
    c_ones_1_16 = din("c_ones_1_16", (1, 16))

    out = nc.dram_tensor("out", [T, D], F32, kind="ExternalOutput").ap()
    x1bf_dram = nc.dram_tensor("x1bf_dram", [T, D], BF16, kind="Internal").ap()
    x1f_dram = nc.dram_tensor("x1f_dram", [T, D], F32, kind="Internal").ap()
    moe_dram = nc.dram_tensor("moe_dram", [T, D], F32, kind="Internal").ap()
    cc_in = nc.dram_tensor("cc_in", [16, T], F32, kind="Internal").ap()
    cc_out = nc.dram_tensor("cc_out", [16 * NR, T], F32, kind="Internal").ap()

    a16 = Arena(nc, "a16", BF16, 2)
    a32 = Arena(nc, "a32", F32, 4)

    def A16(name, shape, ph0, ph1):
        return a16.alloc(name, shape, ph0, ph1)

    def A32(name, shape, ph0, ph1):
        return a32.alloc(name, shape, ph0, ph1)

    A16("ident16", (128, 128), "A", "I")
    A16("ones_col", (128, 1), "A", "I")
    A16("gwT_sb", (128, 8 * 16), "A", "I")
    A16("wdqT_sb", (128, 8 * 256), "A", "B")
    A16("wdkvT_sb", (128, 8 * 256), "A", "B")
    A16("wuq_sb", (128, 8 * 256), "A", "A")
    A16("wuk_sb", (128, 8 * 256), "A", "A")
    A16("wuv_sb", (128, 8 * 256), "A", "A")
    A16("wdq_sb", (128, 2 * D), "A", "A")
    A16("woT_sb", (128, 8 * D), "A", "A")
    A16("b_sb", (128, 2 * 256), "A", "A")
    A16("keff_sb", (128, 8 * 256), "A", "A")
    A16("m_sb", (128, NH * 2 * 256), "A", "C")
    A16("ve_sb", (128, 2 * D), "A", "C")
    A16("xfm_t", (128, 2, 8, 512), "B", "B")
    A16("q1T", (128, 2, T), "B", "C")
    A16("ckvT", (128, 2, T), "B", "C")
    A16("ckvtm", (128, TC, 256), "B", "C")
    A16("qlatT", (128, 2, T), "C", "C")
    A16("pT", (128, 3, QT), "C", "C")
    A16("ctxT", (128, 2, QT), "C", "C")
    A16("y_sb", (128, TC, D), "C", "D")
    A16("x1b_t", (128, 2, D), "D", "D")
    A16("x1g_t", (128, 2, 8, 512), "E", "E")
    A16("sgT_sb", (128, 8, IS), "D", "F")
    A16("suT_sb", (128, 8, IS), "D", "F")
    A16("sdT_sb", (128, ISC, D), "E", "F")
    A16("x1s_t", (128, 2, 8, 512), "F", "F")
    A16("hs_sb", (128, ISC, 512), "F", "F")
    A16("junk_bf", (128, T), "F", "F")
    A16("wg_buf", (128, 3, 8, IB), "G", "I")
    A16("wu_buf", (128, 3, 8, IB), "G", "I")
    A16("wd_buf", (128, IC, D), "G", "I")
    A16("xg_sb", (128, 2, 8, CAP), "G", "I")
    A16("h_sb", (128, IC, CAP), "G", "I")

    A32("identf", (128, 128), "A", "I")
    A32("mask4", (128, 4, QT), "A", "C")
    A32("blk", (128, 16), "A", "I")
    A32("blkT", (16, 128), "A", "I")
    A32("ones_1_16", (1, 16), "A", "I")
    A32("idsp1", (16, SX), "A", "I")
    A32("rk1", (16, CAP // 16), "A", "I")
    A32("nf16", (16, 16), "G", "H")
    A32("mwin", (16, CAP // 16), "G", "G")
    A32("dn_row", (1, QT), "C", "C")
    A32("dnacc", (128, 2, QT), "C", "C")
    A32("recip_pp", (128, 4), "C", "C")
    A32("xt_t", (128, 2, D), "D", "D")
    A32("x1f_t", (128, 2, D), "D", "D")
    A32("lnjunk", (128, D), "D", "I")
    A32("st_t", (128, 16), "D", "I")
    A32("logit_t", (128, 16), "E", "E")
    A32("sc_t", (128, 16), "E", "E")
    A32("scT", (16, T), "E", "E")
    A32("gsc", (128, T), "E", "F")
    A32("sloc", (16, E, SX), "F", "G")
    A32("cnt_pp", (128, 1), "F", "F")
    A32("cnt_e", (16, 1), "F", "F")
    A32("lo_v", (16, 1), "F", "G")
    A32("hi_v", (16, 1), "F", "F")
    A32("mid_v", (16, 4), "F", "F")
    A32("tau_pp", (128, 1), "F", "F")
    A32("lo_row", (1, 16), "F", "G")
    A32("tau16", (16, 16), "F", "G")
    A32("mids", (16, 2, SX), "G", "G")
    A32("comp", (16, 2, SX), "G", "G")
    A32("nf_f", (1, 16), "G", "H")
    A32("gtmp", (128, 2, 512), "F", "F")
    A32("sh_t", (128, 2, D), "F", "F")
    A32("gat_t", (128, 16), "H", "H")
    A32("gat_s", (128, 8), "E", "H")
    A32("moegelu", (128, 2, CAP), "H", "H")
    A32("eo_sb", (128, NCAP, D), "H", "H")
    A32("mo_t", (128, 2, D), "I", "I")
    A32("xf2_t", (128, 2, D), "I", "I")
    A32("o_t", (128, 2, D), "I", "I")

    V16 = a16.finalize()
    V32 = a32.finalize()
    V = {}
    V.update(V16)
    V.update(V32)
    assert a16.total_bytes + a32.total_bytes < 205 * 1024, \
        (a16.total_bytes // 1024, a32.total_bytes // 1024)

    NIDX = CAP // 16
    i16tmp = nc.alloc_sbuf_tensor("i16tmp", [16, NIDX], I16).ap()
    idx16 = nc.alloc_sbuf_tensor("idx16", [128, E * NIDX], I16).ap()
    idx16 = idx16.rearrange("p (a b) -> p a b", a=E)
    nf_u = nc.alloc_sbuf_tensor("nf_u", [1, 16], U32).ap()
    nf_i = nc.alloc_sbuf_tensor("nf_i", [1, 16], I32).ap()
    c_eps = nc.alloc_sbuf_tensor("c_eps", [128, 1], F32).ap()
    nc.const_aps.register(F32, 1e-5, c_eps)
    c_one = nc.alloc_sbuf_tensor("c_one", [128, 1], F32).ap()

    (ident16, ones_col, gwT_sb, wdqT_sb, wdkvT_sb, wuq_sb, wuk_sb, wuv_sb,
     wdq_sb, woT_sb, b_sb, keff_sb, m_sb, ve_sb, xfm_t, q1T, ckvT, ckvtm,
     qlatT, pT, ctxT, y_sb, x1b_t, x1g_t, sgT_sb, suT_sb, sdT_sb, x1s_t,
     hs_sb, junk_bf, wg_buf, wu_buf, wd_buf, xg_sb, h_sb) = (
        V[k] for k in (
            "ident16", "ones_col", "gwT_sb", "wdqT_sb", "wdkvT_sb", "wuq_sb",
            "wuk_sb", "wuv_sb", "wdq_sb", "woT_sb", "b_sb", "keff_sb", "m_sb",
            "ve_sb", "xfm_t", "q1T", "ckvT", "ckvtm", "qlatT", "pT", "ctxT",
            "y_sb", "x1b_t", "x1g_t", "sgT_sb", "suT_sb", "sdT_sb", "x1s_t",
            "hs_sb", "junk_bf", "wg_buf", "wu_buf", "wd_buf",
            "xg_sb", "h_sb"))
    (identf, mask4, blk, blkT, ones_1_16, idsp1, dn_row, recip_pp, xt_t,
     x1f_t, lnjunk, st_t, logit_t, sc_t, scT, gsc, sloc, cnt_pp, cnt_e, lo_v,
     hi_v, mid_v, tau_pp, lo_row, tau16, mids, comp, nf_f, gtmp, sh_t, gat_t,
     gat_s, moegelu, eo_sb, mo_t, xf2_t, o_t, rk1, nf16, mwin) = (
        V[k] for k in (
            "identf", "mask4", "blk", "blkT", "ones_1_16", "idsp1", "dn_row",
            "recip_pp", "xt_t", "x1f_t", "lnjunk", "st_t", "logit_t", "sc_t",
            "scT", "gsc", "sloc", "cnt_pp", "cnt_e", "lo_v", "hi_v", "mid_v",
            "tau_pp", "lo_row", "tau16", "mids", "comp", "nf_f", "gtmp",
            "sh_t", "gat_t", "gat_s", "moegelu", "eo_sb", "mo_t", "xf2_t",
            "o_t", "rk1", "nf16", "mwin"))
    dnacc = V["dnacc"]

    PS = [nc.alloc_psum_tensor(f"ps{i}", [128, 512], F32).ap() for i in range(8)]

    p = Prog(nc)
    dma_sems = {SY: [], GE: []}

    def dsem(eng, name):
        s = Sem(nc, name)
        dma_sems[eng].append(s)
        return s

    s_cst = dsem(SY, "s_cst")
    s_aw = dsem(GE, "s_aw")
    s_xf = [dsem(GE, "s_xf0"), dsem(GE, "s_xf1")]
    s_shw = dsem(GE, "s_shw")
    s_xt = [dsem(SY, "s_xt0"), dsem(SY, "s_xt1")]
    s_x1s = [dsem(SY, "s_x1s0"), dsem(SY, "s_x1s1")]
    s_tp = [dsem(SY, "s_tp0"), dsem(SY, "s_tp1")]
    s_ccb = dsem(SY, "s_ccb")
    s_msh = [dsem(SY, "s_msh0"), dsem(SY, "s_msh1")]
    s_idx = dsem(SY, "s_idx")
    s_wg = [dsem(GE, "s_wg0"), dsem(GE, "s_wg1"), dsem(GE, "s_wg2")]
    s_wd = dsem(GE, "s_wd")
    s_gt = [dsem(GE, "s_gt0"), dsem(GE, "s_gt1")]
    s_sca = dsem(GE, "s_sca")
    s_rb = [dsem(SY, "s_rb0"), dsem(SY, "s_rb1")]
    s_dbg = [dsem(SY, "s_dbg0"), dsem(SY, "s_dbg1")]
    s_dbo = [dsem(SY, "s_dbo0"), dsem(SY, "s_dbo1")]
    s_ou = [dsem(SY, "s_ou0"), dsem(SY, "s_ou1")]
    s_te = Sem(nc, "s_te")
    s_ve = Sem(nc, "s_ve")
    s_se = Sem(nc, "s_se")
    s_ge = Sem(nc, "s_ge")
    s_cc = Sem(nc, "s_cc")
    s_b = Sem(nc, "s_b")
    p.auto = {VE: s_ve, SE: s_se, GE: s_ge}

    def barrier():
        for s in dma_sems[SY]:
            p.wait(SY, s, s.n)
        for s in dma_sems[GE]:
            p.wait(GE, s, s.n)
        p.wait(GE, s_ge, s_ge.n)
        targets = [(s_te, p.mark(TE, s_te)), (s_ve, s_ve.n), (s_se, s_se.n)]
        for eng, s in ((GE, s_ge), (SY, s_b)):
            v = s.inc()
            p.ops[eng].append(lambda e, ss=s.h: e.nop().then_inc(ss, 1))
            targets.append((s, v))
        for eng in ENGINES:
            for s, v in targets:
                p.wait(eng, s, v)

    def mm(ps_ap, lhsT_ap, rhs_ap, start, stop):
        p.on(TE, lambda e, o=ps_ap, l=lhsT_ap, r=rhs_ap, st=start, sp=stop:
             e.matmul(o, l, r, start=st, stop=sp, skip_group_check=True))

    def trans(ps_ap, in_ap):
        np_ = in_ap.shape[0]
        idt = ident16 if in_ap.dtype == BF16 else identf
        p.on(TE, lambda e, o=ps_ap, i=in_ap, d=idt[0:np_, 0:np_]:
             e.transpose(o, i, d))

    def vcopy(eng, out_ap, in_ap):
        p.on(eng, lambda e, o=out_ap, i=in_ap: e.tensor_copy(o, i))

    def scopy(out_ap, in_ap, scale=1.0):
        p.on(SE, lambda e, o=out_ap, i=in_ap, s=scale:
             e.activation(o, i, AF.Copy, bias=0.0, scale=s))

    p.dma(SY, identf[:, :], c_ident[:, :], s_cst)
    for o_ in range(4):
        p.dma(SY, mask4[:, o_, :], c_mask4[o_, :, :], s_cst)
    p.dma(SY, idsp1[:, :], c_idsp1[:, :], s_cst)
    p.dma(SY, rk1[:, :], c_rk1[:, :], s_cst)
    p.dma(SY, blk[:, :], c_blk[:, :], s_cst)
    p.dma(SY, blkT[:, :], c_blkT[:, :], s_cst)
    p.dma(SY, ones_1_16[:, :], c_ones_1_16[:, :], s_cst)
    w_consts = s_cst.n

    def load_fm(dst, src, ncols, nch, sem):
        v = 0
        for ch in range(nch):
            v = p.dma(GE, dst[:, ch * ncols:(ch + 1) * ncols],
                      src[ch * 128:(ch + 1) * 128, :], sem)
        return v

    nreg = [None, None]

    def _alloc_regs(e):
        nreg[0] = e.alloc_register("nidx0")
        nreg[1] = e.alloc_register("nidx1")
    p.on(GE, _alloc_regs)

    p.dma(GE, ones_col[:, :], c_ones_col[:, :], s_aw)
    load_fm(wdqT_sb, wdqT, 256, 8, s_aw)
    load_fm(wdkvT_sb, wdkvT, 256, 8, s_aw)
    load_fm(wuq_sb, wuq, 256, 8, s_aw)
    load_fm(wuk_sb, wuk, 256, 8, s_aw)
    load_fm(wuv_sb, wuv, 256, 8, s_aw)
    load_fm(wdq_sb, wdq, D, 2, s_aw)
    load_fm(woT_sb, woT, D, 8, s_aw)
    w_attn = load_fm(gwT_sb, gwT, 16, 8, s_aw)

    p.on(VE, lambda e: e.memset(c_eps[:, :], 1e-5))
    p.on(VE, lambda e: e.memset(c_one[:, :], 1.0))
    p.wait(VE, s_cst, w_consts)
    vcopy(VE, ident16[:, :], identf[:, :])
    v_ident = p.mark(VE, s_ve)

    p.wait(TE, s_aw, w_attn)
    p.wait(TE, s_ve, v_ident)
    for mc in range(2):
        for dc in range(8):
            mm(PS[mc][:, 0:256], wuq_sb[:, dc * 256 + mc * 128: dc * 256 + mc * 128 + 128],
               wuk_sb[:, dc * 256: dc * 256 + 256], dc == 0, dc == 7)
    te_b = p.mark(TE, s_te)
    p.wait(VE, s_te, te_b)
    for mc in range(2):
        vcopy(VE, b_sb[:, mc * 256:(mc + 1) * 256], PS[mc][:, 0:256])
    v_b = p.mark(VE, s_ve)

    p.wait(TE, s_ve, v_b)
    for mc8 in range(8):
        psn = PS[2 + (mc8 % 2)]
        for cc in range(2):
            mm(psn[:, 0:256], wdq_sb[:, cc * D + mc8 * 128: cc * D + mc8 * 128 + 128],
               b_sb[:, cc * 256:(cc + 1) * 256], cc == 0, cc == 1)
        te_k = p.mark(TE, s_te)
        p.wait(SE, s_te, te_k)
        scopy(keff_sb[:, mc8 * 256:(mc8 + 1) * 256], psn[:, 0:256], scale=0.125)
        se_k = p.mark(SE, s_se)
        if mc8 >= 1:
            p.wait(TE, s_se, se_k - 1)
    se_keff = s_se.n

    for mc in range(2):
        for ns in range(2):
            for dc in range(8):
                mm(PS[4 + ns][:, :], wuv_sb[:, dc * 256 + mc * 128: dc * 256 + mc * 128 + 128],
                   woT_sb[:, dc * D + ns * 512: dc * D + ns * 512 + 512], dc == 0, dc == 7)
        te_v = p.mark(TE, s_te)
        p.wait(VE, s_te, te_v)
        for ns in range(2):
            vcopy(VE, ve_sb[:, mc * D + ns * 512: mc * D + ns * 512 + 512],
                  PS[4 + ns][:, :])
        v_v = p.mark(VE, s_ve)
        p.wait(TE, s_ve, v_v)

    p.wait(TE, s_se, se_keff)
    for h in range(NH):
        pr = (h % 2) * 64
        dc = h // 2
        for mc in range(2):
            mm(PS[6][:, 0:256],
               wuq_sb[pr:pr + 64, dc * 256 + mc * 128: dc * 256 + mc * 128 + 128],
               keff_sb[pr:pr + 64, dc * 256: dc * 256 + 256], True, True)
            te_m = p.mark(TE, s_te)
            p.wait(SE, s_te, te_m)
            scopy(m_sb[:, h * 512 + mc * 256: h * 512 + mc * 256 + 256], PS[6][:, 0:256])
            se_m = p.mark(SE, s_se)
            p.wait(TE, s_se, se_m)

    te_a_done = p.mark(TE, s_te)
    barrier()

    b_te_done = [0] * T5
    for t5 in range(T5):
        buf = t5 % 2
        xf = xfm_t[:, buf]
        if t5 == 0:
            p.wait(GE, s_te, te_a_done)
        if t5 >= 2:
            p.wait(GE, s_te, b_te_done[t5 - 2])
        w_x = p.dma(GE, xf,
                    x_fmt[t5, :, :].rearrange("(dc p) q -> p dc q", p=128),
                    s_xf[buf])
        p.wait(TE, s_xf[buf], w_x)
        evs = []
        for which, (wsb, dst) in enumerate(((wdqT_sb, q1T), (wdkvT_sb, ckvT))):
            for mc in range(2):
                psn = PS[which * 2 + mc]
                for dc in range(8):
                    mm(psn[:, :], wsb[:, dc * 256 + mc * 128: dc * 256 + mc * 128 + 128],
                       xf[:, dc, :], dc == 0, dc == 7)
                te_p = p.mark(TE, s_te)
                ev = VE if (which * 2 + mc) % 2 == 0 else SE
                p.wait(ev, s_te, te_p)
                dap = dst[:, mc, t5 * 512: t5 * 512 + 512]
                if ev == VE:
                    vcopy(VE, dap, psn[:, :])
                else:
                    scopy(dap, psn[:, :])
                evs.append(ev)
        for sub in range(4):
            tc = t5 * 4 + sub
            for dc in range(8):
                mm(PS[4][:, 0:256], xf[:, dc, sub * 128: sub * 128 + 128],
                   wdkvT_sb[:, dc * 256: dc * 256 + 256], dc == 0, dc == 7)
            te_c = p.mark(TE, s_te)
            p.wait(SE, s_te, te_c)
            scopy(ckvtm[:, tc, :], PS[4][:, 0:256])
            se_c = p.mark(SE, s_se)
            p.wait(TE, s_se, se_c)
        v_e = p.mark(VE, s_ve)
        se_e = p.mark(SE, s_se)
        p.wait(TE, s_ve, v_e)
        p.wait(TE, s_se, se_e)
        b_te_done[t5] = p.mark(TE, s_te)

    barrier()
    p.ckpt("B")

    for h in range(NH):
        ql_se = {}
        ql_list = [(mc, t5) for mc in range(2) for t5 in range(T5)]
        for qi, (mc, t5) in enumerate(ql_list):
            psn = PS[6 + qi % 2]
            if qi >= 2:
                p.wait(TE, s_se, ql_se[qi - 2])
            for cc in range(2):
                mm(psn[:, :],
                   m_sb[:, h * 512 + cc * 256 + mc * 128: h * 512 + cc * 256 + mc * 128 + 128],
                   q1T[:, cc, t5 * 512: t5 * 512 + 512], cc == 0, cc == 1)
            te_q = p.mark(TE, s_te)
            p.wait(SE, s_te, te_q)
            scopy(qlatT[:, mc, t5 * 512: t5 * 512 + 512], psn[:, :])
            ql_se[qi] = p.mark(SE, s_se)
        p.wait(TE, s_se, ql_se[len(ql_list) - 1])

        NQT = T // QT
        te_pv_q = [0] * NQT
        v_dnacc_q = [0] * NQT
        tail_state = {}

        def cpair_of(qt):
            return (PS[2], PS[3]) if qt % 2 == 0 else (PS[5], PS[6])

        def dnps_of(qt):
            return PS[4] if qt % 2 == 0 else PS[7]

        def tail_A(qt):
            cpair = cpair_of(qt)
            dnps = dnps_of(qt)
            p.wait(VE, s_te, te_pv_q[qt])
            vcopy(VE, ctxT[:, 0], cpair[0][:, :])
            p.wait(SE, s_te, te_pv_q[qt])
            scopy(ctxT[:, 1], cpair[1][:, :])
            se_cx = p.mark(SE, s_se)
            p.wait(TE, s_ve, v_dnacc_q[qt])
            mm(dnps[0:1, :], c_one[:, :], dnacc[:, qt % 2, :], True, True)
            te_dn = p.mark(TE, s_te)
            p.wait(VE, s_te, te_dn)
            p.on(VE, lambda e, o=dn_row[0:1, :], i=dnps[0:1, :]:
                 e.reciprocal(o, i))
            v_recip = p.mark(VE, s_ve)
            tail_state[qt] = [se_cx, v_recip]

        def tail_B(qt):
            dnps = dnps_of(qt)
            se_cx, v_recip = tail_state[qt]
            p.wait(TE, s_ve, v_recip)
            for tcq in range(NKQ):
                trans(dnps[:, tcq: tcq + 1],
                      dn_row[0:1, tcq * 128: tcq * 128 + 128])
            te_tr = p.mark(TE, s_te)
            p.wait(VE, s_te, te_tr)
            for tcq in range(NKQ):
                vcopy(VE, recip_pp[:, tcq: tcq + 1], dnps[:, tcq: tcq + 1])
            v_rc = p.mark(VE, s_ve)
            tail_state[qt] = [se_cx, v_rc]

        def tail_C(qt, h=h):
            dnps = dnps_of(qt)
            se_cx, v_rc = tail_state.pop(qt)
            p.wait(TE, s_ve, v_rc)
            p.wait(TE, s_se, se_cx)
            for tcq in range(NKQ):
                for lc in range(2):
                    mm(dnps[:, 0:64], ctxT[:, lc, tcq * 128: tcq * 128 + 128],
                       ve_sb[:, lc * D + h * 64: lc * D + h * 64 + 64],
                       lc == 0, lc == 1)
                te_y = p.mark(TE, s_te)
                p.wait(VE, s_te, te_y)
                tcg = qt * NKQ + tcq
                p.on(VE, lambda e, o=y_sb[:, tcg, h * 64: h * 64 + 64],
                     i=dnps[:, 0:64], s=recip_pp[:, tcq: tcq + 1]:
                     e.tensor_scalar(o, i, s, None, OP.mult))
                v_y = p.mark(VE, s_ve)
                p.wait(TE, s_ve, v_y)

        for qt in range(NQT):
            nk = (qt + 1) * NKQ
            cpair = cpair_of(qt)
            se_exp = [0] * nk
            v_dnadd = [0] * nk

            def emit_S(kc, h=h, qt=qt, nk=nk, se_exp=se_exp, v_dnadd=v_dnadd):
                sps = PS[kc % 2]
                if kc >= 2:
                    p.wait(TE, s_se, se_exp[kc - 2])
                for lc in range(2):
                    mm(sps[:, :], ckvT[:, lc, kc * 128: kc * 128 + 128],
                       qlatT[:, lc, qt * QT: qt * QT + QT], lc == 0, lc == 1)
                te_s = p.mark(TE, s_te)
                diag_o = kc - qt * NKQ
                if diag_o >= 0:
                    p.wait(VE, s_te, te_s)
                    p.on(VE, lambda e, o=sps[:, :], b=mask4[:, diag_o, :]:
                         e.tensor_tensor(o, o, b, OP.add))
                    v_m = p.mark(VE, s_ve)
                    p.wait(SE, s_ve, v_m)
                else:
                    p.wait(SE, s_te, te_s)
                if kc >= 3:
                    p.wait(SE, s_ve, v_dnadd[kc - 3])
                pt = pT[:, kc % 3]
                p.on(SE, lambda e, o=pt, i=sps[:, :]:
                     e.activation(o, i, AF.Exp, bias=0.0, scale=1.0))
                se_exp[kc] = p.mark(SE, s_se)

            emit_S(0)
            for kc in range(nk):
                if kc + 1 < nk:
                    emit_S(kc + 1)
                p.wait(TE, s_se, se_exp[kc])
                pt = pT[:, kc % 3]
                for lc in range(2):
                    mm(cpair[lc][:, :], ckvtm[:, kc, lc * 128: lc * 128 + 128],
                       pt, kc == 0, kc == nk - 1)
                p.wait(VE, s_se, se_exp[kc])
                da = dnacc[:, qt % 2, :]
                if kc == 0:
                    vcopy(VE, da, pt)
                else:
                    p.on(VE, lambda e, o=da, a=da, b=pt:
                         e.tensor_tensor(o, a, b, OP.add))
                v_dnadd[kc] = p.mark(VE, s_ve)
                if qt >= 1:
                    if kc == 1:
                        tail_A(qt - 1)
                    elif kc == 3:
                        tail_B(qt - 1)
                    elif kc == 5:
                        tail_C(qt - 1)
            v_dnacc_q[qt] = v_dnadd[nk - 1]
            te_pv_q[qt] = p.mark(TE, s_te)
        tail_A(NQT - 1)
        tail_B(NQT - 1)
        tail_C(NQT - 1)
    te_flash = p.mark(TE, s_te)
    barrier()
    p.ckpt("C")

    load_fm(sgT_sb.rearrange("p a b -> p (a b)"), sgT, IS, 8, s_shw)
    load_fm(suT_sb.rearrange("p a b -> p (a b)"), suT, IS, 8, s_shw)

    def emit_ln(src_f32_fn, tc, out_f32, out_bf16, extra_in=None):
        s1 = st_t[:, 0:1]; mean = st_t[:, 1:2]; s2 = st_t[:, 2:3]
        tt = st_t[:, 3:4]; rstd = st_t[:, 4:5]
        src_f32_fn(out_f32)
        p.on(VE, lambda e, o=s1, i=out_f32: e.reduce_sum(o, i, AX.X))
        p.on(VE, lambda e, o=mean, i=s1: e.tensor_scalar(o, i, 1.0 / D, None, OP.mult))
        p.on(VE, lambda e, o=lnjunk[:, :], i=out_f32, s=mean, b=out_f32, a=s2:
             e.scalar_tensor_tensor(o, i, s, b, OP.subtract, OP.mult, accum_out=a))
        v_st = p.mark(VE, s_ve)
        p.wait(SE, s_ve, v_st)
        p.on(SE, lambda e, o=tt, i=s2:
             e.activation(o, i, AF.Ln, bias=1e-5, scale=1.0 / D))
        p.on(SE, lambda e, o=rstd, i=tt:
             e.activation(o, i, AF.Exp, bias=0.0, scale=-0.5))
        se_r = p.mark(SE, s_se)
        p.wait(VE, s_se, se_r)
        p.on(VE, lambda e, o=out_f32, i=out_f32, s=mean, s2_=rstd:
             e.tensor_scalar(o, i, s, s2_, OP.subtract, OP.mult))
        if out_bf16 is not None:
            vcopy(VE, out_bf16, out_f32)

    d_ve_done = [0] * TC
    d_sy_done = [0] * TC
    for tc in range(TC):
        buf = tc % 2
        xt = xt_t[:, buf]
        x1f = x1f_t[:, buf]
        x1b = x1b_t[:, buf]
        if tc >= 2:
            p.wait(SY, s_ve, d_ve_done[tc - 2])
        w_xt = p.dma(SY, xt, x_tm[tc * 128:(tc + 1) * 128, :], s_xt[buf])
        p.wait(VE, s_xt[buf], w_xt)
        if tc >= 2:
            p.wait(VE, s_x1s[buf], d_sy_done[tc - 2])
        emit_ln(lambda o, tc_=tc, xt_=xt: p.on(
            VE, lambda e, oo=o, a=xt_, b=y_sb[:, tc_, :]:
            e.tensor_tensor(oo, a, b, OP.add)), tc, x1f, x1b)
        d_ve_done[tc] = p.mark(VE, s_ve)
        p.wait(SY, s_ve, d_ve_done[tc])
        p.dma(SY, x1f_dram[tc * 128:(tc + 1) * 128, :], x1f, s_x1s[buf])
        d_sy_done[tc] = p.dma(SY, x1bf_dram[tc * 128:(tc + 1) * 128, :], x1b,
                              s_x1s[buf])
    x1s_totals = [s_x1s[0].n, s_x1s[1].n]
    barrier()
    p.ckpt("D")
    load_fm(sdT_sb.rearrange("p a b -> p (a b)"), sdT, D, ISC, s_shw)
    w_shw = s_shw.n

    e_te_done = [0] * T5
    for t5 in range(T5):
        buf = t5 % 2
        xg5 = x1g_t[:, buf]
        if t5 >= 2:
            p.wait(SY, s_te, e_te_done[t5 - 2])
        if t5 == 0:
            p.wait(SY, s_x1s[0], x1s_totals[0])
            p.wait(SY, s_x1s[1], x1s_totals[1])
        w_tp = 0
        for dc in range(8):
            w_tp = p.dmat(SY, xg5[:, dc, :],
                          x1bf_dram[t5 * 512:(t5 + 1) * 512, dc * 128:(dc + 1) * 128],
                          s_tp[buf])
        p.wait(TE, s_tp[buf], w_tp)
        for sub in range(4):
            tc = t5 * 4 + sub
            for dc in range(8):
                mm(PS[0][:, 0:16], xg5[:, dc, sub * 128: sub * 128 + 128],
                   gwT_sb[:, dc * 16:(dc + 1) * 16], dc == 0, dc == 7)
            te_l = p.mark(TE, s_te)
            p.wait(VE, s_te, te_l)
            p.on(VE, lambda e, o=gat_s[:, 0:1], i=PS[0][:, 0:16]:
                 e.reduce_max(o, i, AX.X))
            p.on(VE, lambda e, o=logit_t[:, :], i=PS[0][:, 0:16], s=gat_s[:, 0:1]:
                 e.tensor_scalar(o, i, s, None, OP.subtract))
            v_l = p.mark(VE, s_ve)
            p.wait(SE, s_ve, v_l)
            p.on(SE, lambda e, o=sc_t[:, :], i=logit_t[:, :]:
                 e.activation(o, i, AF.Exp, bias=0.0, scale=1.0))
            se_l = p.mark(SE, s_se)
            p.wait(VE, s_se, se_l)
            p.on(VE, lambda e, o=gat_s[:, 1:2], i=sc_t[:, :]: e.reduce_sum(o, i, AX.X))
            p.on(VE, lambda e, o=gat_s[:, 2:3], i=gat_s[:, 1:2]: e.reciprocal(o, i))
            p.on(VE, lambda e, o=sc_t[:, :], i=sc_t[:, :], s=gat_s[:, 2:3]:
                 e.tensor_scalar(o, i, s, None, OP.mult))
            v_s = p.mark(VE, s_ve)
            p.wait(TE, s_ve, v_s)
            trans(PS[1][0:16, 0:128], sc_t[:, :])
            te_sc = p.mark(TE, s_te)
            p.wait(VE, s_te, te_sc)
            vcopy(VE, scT[0:16, tc * 128:(tc + 1) * 128], PS[1][0:16, 0:128])
            v_sc = p.mark(VE, s_ve)
            p.wait(TE, s_ve, v_sc)
        e_te_done[t5] = p.mark(TE, s_te)
    v_gate = s_ve.n
    p.wait(SY, s_ve, v_gate)
    w_ccin = p.dma(SY, cc_in[:, :], scT[0:16, :], s_ccb)
    p.wait(GE, s_ccb, w_ccin)
    v_cc = s_cc.inc()
    p._dma_raw(GE, lambda e: e.collective_compute(
        "AllGather", OP.bypass, replica_groups=[list(range(NR))],
        ins=[cc_in[:, :]], outs=[cc_out[:, :]]).then_inc(s_cc.h, 1))
    p.wait(SY, s_cc, v_cc)
    cc_v = cc_out.rearrange("(r e) j -> r e j", e=16)
    for e_ in range(E):
        p.dma(SY, gsc[e_ * (NR): (e_ + 1) * NR, :], cc_v[:, e_, :], s_ccb)
    w_gsc = s_ccb.n

    barrier()
    p.ckpt("E")

    for e_ in range(E):
        p.dma(SY, sloc[:, e_, :],
              cc_in[e_: e_ + 1, :].rearrange("o (q s) -> o q s", q=16), s_ccb)
    w_route = s_ccb.n
    p.wait(VE, s_ccb, w_route)
    p.on(VE, lambda e: e.memset(lo_v[:, :], 0.0))
    p.on(VE, lambda e: e.memset(hi_v[:, :], 1.0))
    p.on(VE, lambda e: e.memset(tau_pp[:, :], 0.5))
    p.on(VE, lambda e: e.memset(mid_v[:, 0:1], 0.5))
    p.mark(VE, s_ve)

    CSEL = float(cfg.CSEL)
    tau_ready = []

    def emit_bisect_iter(it):
        if it > 0:
            p.wait(VE, s_ve, tau_ready[it - 1])
        p.on(VE, lambda e, o=junk_bf[:, :], i=gsc[:, :], s=tau_pp[:, :], a=cnt_pp[:, :]:
             e.tensor_scalar(o, i, s, None, OP.is_ge, OP.add, accum_out=a))
        v_c = p.mark(VE, s_ve)
        p.wait(TE, s_ve, v_c)
        mm(PS[7][0:16, 0:1], blk[:, :], cnt_pp[:, :], True, True)
        te_s_ = p.mark(TE, s_te)
        p.wait(VE, s_te, te_s_)
        vcopy(VE, cnt_e[:, :], PS[7][0:16, 0:1])
        ge_ = mid_v[:, 1:2]; t0 = mid_v[:, 2:3]; t1 = mid_v[:, 3:4]; mid = mid_v[:, 0:1]
        p.on(VE, lambda e, o=ge_, i=cnt_e[:, :]:
             e.tensor_scalar(o, i, CSEL, None, OP.is_ge))
        p.on(VE, lambda e, o=t0, a=mid, b=lo_v[:, :]: e.tensor_tensor(o, a, b, OP.subtract))
        p.on(VE, lambda e, o=t0, a=t0, b=ge_: e.tensor_tensor(o, a, b, OP.mult))
        p.on(VE, lambda e, o=lo_v[:, :], a=lo_v[:, :], b=t0: e.tensor_tensor(o, a, b, OP.add))
        p.on(VE, lambda e, o=t1, a=mid, b=hi_v[:, :]: e.tensor_tensor(o, a, b, OP.subtract))
        p.on(VE, lambda e, o=t0, i=ge_: e.tensor_scalar(o, i, -1.0, 1.0, OP.mult, OP.add))
        p.on(VE, lambda e, o=t1, a=t1, b=t0: e.tensor_tensor(o, a, b, OP.mult))
        p.on(VE, lambda e, o=hi_v[:, :], a=hi_v[:, :], b=t1: e.tensor_tensor(o, a, b, OP.add))
        p.on(VE, lambda e, o=mid, a=lo_v[:, :], b=hi_v[:, :]: e.tensor_tensor(o, a, b, OP.add))
        p.on(VE, lambda e, o=mid, i=mid: e.tensor_scalar(o, i, 0.5, None, OP.mult))
        v_u = p.mark(VE, s_ve)
        p.wait(TE, s_ve, v_u)
        mm(PS[7][:, 0:1], blkT[:, :], mid, True, True)
        te_b_ = p.mark(TE, s_te)
        p.wait(VE, s_te, te_b_)
        vcopy(VE, tau_pp[:, :], PS[7][:, 0:1])
        tau_ready.append(p.mark(VE, s_ve))

    f_te_done = [0] * T5
    sh_hist = []

    def emit_shared_t5(t5):
        buf = t5 % 2
        xs5 = x1s_t[:, buf]
        if t5 == 0:
            p.wait(SY, s_ccb, w_route)
        if t5 >= 2:
            p.wait(SY, s_te, f_te_done[t5 - 2])
        w_tp = 0
        for dc in range(8):
            w_tp = p.dmat(SY, xs5[:, dc, :],
                          x1bf_dram[t5 * 512:(t5 + 1) * 512, dc * 128:(dc + 1) * 128],
                          s_tp[buf])
        p.wait(TE, s_tp[buf], w_tp)
        if t5 == 0:
            p.wait(TE, s_shw, w_shw)
        v_hb = {}
        for ic in range(ISC):
            pr = (PS[0], PS[1]) if ic % 2 == 0 else (PS[4], PS[5])
            gt = gtmp[:, ic % 2, :]
            if ic >= 2:
                p.wait(TE, s_ve, v_hb[ic - 2])
            for which, wsb in ((0, sgT_sb), (1, suT_sb)):
                for dc in range(8):
                    mm(pr[which][:, :], wsb[:, dc, ic * 128: ic * 128 + 128],
                       xs5[:, dc, :], dc == 0, dc == 7)
            te_g = p.mark(TE, s_te)
            p.wait(SE, s_te, te_g)
            p.on(SE, lambda e, o=gt, i=pr[0][:, :]:
                 e.activation(o, i, AF.Gelu, bias=0.0, scale=1.0))
            se_g = p.mark(SE, s_se)
            p.wait(VE, s_se, se_g)
            p.on(VE, lambda e, o=hs_sb[:, ic, :], a=gt, b=pr[1][:, :]:
                 e.tensor_tensor(o, a, b, OP.mult))
            v_hb[ic] = p.mark(VE, s_ve)
        p.wait(TE, s_ve, v_hb[ISC - 1])
        p.wait(TE, s_ve, v_hb[ISC - 2])
        for tcq in range(4):
            for ns in range(2):
                for ic in range(ISC):
                    mm(PS[2 + ns][:, :], hs_sb[:, ic, tcq * 128: tcq * 128 + 128],
                       sdT_sb[:, ic, ns * 512: ns * 512 + 512], ic == 0, ic == ISC - 1)
            te_d = p.mark(TE, s_te)
            p.wait(VE, s_te, te_d)
            sbuf_ = tcq % 2
            sh = sh_t[:, sbuf_]
            if len(sh_hist) >= 2:
                slot, vv = sh_hist[-2]
                p.wait(VE, s_msh[slot], vv)
            for ns in range(2):
                vcopy(VE, sh[:, ns * 512:(ns + 1) * 512], PS[2 + ns][:, :])
            v_sh = p.mark(VE, s_ve)
            p.wait(TE, s_ve, v_sh)
            p.wait(SY, s_ve, v_sh)
            tcg = t5 * 4 + tcq
            slot = len(sh_hist) % 2
            sh_hist.append(
                (slot, p.dma(SY, moe_dram[tcg * 128:(tcg + 1) * 128, :], sh,
                             s_msh[slot])))
        f_te_done[t5] = p.mark(TE, s_te)

    for u in range(max(cfg.BIS, T5)):
        if u < cfg.BIS:
            emit_bisect_iter(u)
        if u < T5:
            emit_shared_t5(u)
    msh_totals = [s_msh[0].n, s_msh[1].n]

    p.wait(TE, s_ve, s_ve.n)
    trans(PS[7][0:1, 0:16], lo_v[:, :])
    te_lr = p.mark(TE, s_te)
    p.wait(VE, s_te, te_lr)
    vcopy(VE, lo_row[0:1, :], PS[7][0:1, 0:16])
    v_lr = p.mark(VE, s_ve)
    p.wait(TE, s_ve, v_lr)
    mm(PS[7][0:16, 0:16], ones_1_16[0:1, :], lo_row[0:1, :], True, True)
    te_t16 = p.mark(TE, s_te)
    p.wait(VE, s_te, te_t16)
    vcopy(VE, tau16[:, :], PS[7][0:16, 0:16])
    v_tau = p.mark(VE, s_ve)

    barrier()
    p.ckpt("F")

    p._dma_raw(GE, lambda e: e.load_library(library_config.sparse_gather))
    p.ckpt("Ga")
    sg_done = [0] * E
    cast_done = [0] * E
    for e_ in range(E):
        sl = e_ % 2
        if e_ >= 2:
            p.wait(VE, s_ge, sg_done[e_ - 2])
        p.on(VE, lambda e, o=mids[:, sl, :], i=sloc[:, e_, :],
             s=tau16[:, e_: e_ + 1]: e.tensor_scalar(o, i, s, None, OP.is_ge))
        p.on(VE, lambda e, o=mids[:, sl, :], a=mids[:, sl, :], b=idsp1[:, :]:
             e.tensor_tensor(o, a, b, OP.mult))
        p.on(VE, lambda e, o=mids[:, sl, :], i=mids[:, sl, :]:
             e.tensor_scalar(o, i, -1.0, None, OP.add))
        v_m = p.mark(VE, s_ve)
        p.wait(GE, s_ve, v_m)
        if e_ >= 2:
            p.wait(GE, s_ve, cast_done[e_ - 2])
        p.on(GE, lambda e, o=comp[:, sl, :], i=mids[:, sl, :],
             n=nf_u[0:1, e_: e_ + 1]: e.sparse_gather(o, i, num_found=n))
        sg_done[e_] = p.mark(GE, s_ge)
        p.wait(VE, s_ge, sg_done[e_])
        vcopy(VE, idx16[0:16, e_, :], comp[:, sl, 0:NIDX])
        cast_done[e_] = p.mark(VE, s_ve)
        if e_ == 0:
            p.ckpt("Gb")
    p.ckpt("Gc")
    p._dma_raw(GE, lambda e: e.load_library(library_config.mlp))
    p.on(GE, lambda e: e.nop())
    ge_mlp = s_ge.n
    p.ckpt("Gd")
    p.wait(VE, s_ge, sg_done[E - 1])
    vcopy(VE, nf_f[:, :], nf_u[:, :])
    p.on(VE, lambda e, o=nf_f[:, :], i=nf_f[:, :]:
         e.tensor_scalar(o, i, float(CAP), None, OP.min))
    vcopy(VE, nf_i[:, :], nf_f[:, :])
    v_nf = p.mark(VE, s_ve)
    p.wait(TE, s_ve, v_nf)
    mm(PS[7][0:16, 0:16], ones_1_16[0:1, :], nf_f[0:1, :], True, True)
    te_nf = p.mark(TE, s_te)
    p.wait(VE, s_te, te_nf)
    vcopy(VE, nf16[:, :], PS[7][0:16, 0:16])
    for e_ in range(E):
        p.on(VE, lambda e, o=mwin[:, :], i=rk1[:, :], s=nf16[:, e_: e_ + 1]:
             e.tensor_scalar(o, i, s, None, OP.is_le))
        p.on(VE, lambda e, o=i16tmp[:, :], a=idx16[0:16, e_, :],
             b=mwin[:, :]: e.tensor_tensor(o, a, b, OP.mult))
        p.on(VE, lambda e, o=idx16[0:16, e_, :], a=mwin[:, :],
             b=i16tmp[:, :]:
             e.scalar_tensor_tensor(o, a, -1.0, b, OP.add, OP.add))
    v_idx = p.mark(VE, s_ve)
    p.ckpt("Ge")
    p.wait(SY, s_tp[0], s_tp[0].n)
    p.wait(SY, s_tp[1], s_tp[1].n)
    p.wait(SY, s_ve, v_idx)
    i16f = idx16.rearrange("p a b -> p (a b)")
    w_r1 = p.dma(SY, i16f[16:32, :], i16f[0:16, :], s_idx)
    p.wait(SY, s_idx, w_r1)
    w_r2 = p.dma(SY, i16f[32:64, :], i16f[0:32, :], s_idx)
    p.wait(SY, s_idx, w_r2)
    w_idxrep = p.dma(SY, i16f[64:128, :], i16f[0:64, :], s_idx)
    p.wait(SY, s_idx, w_idxrep)
    p.ckpt("Gf")

    NSL = E * nIS
    te_end = {}
    wgwu_ready = {}

    def fetch_wgwu(k, wait_te=None):
        e_, isl = divmod(k, nIS)
        buf = k % 3
        if wait_te is not None:
            p.wait(GE, s_te, wait_te)
        p.dma(GE, wg_buf[:, buf].rearrange("p a b -> p (a b)"),
              wgT_p[e_, isl].rearrange("p a b -> p (a b)"), s_wg[buf])
        p.dma(GE, wu_buf[:, buf].rearrange("p a b -> p (a b)"),
              wuT_p[e_, isl].rearrange("p a b -> p (a b)"), s_wg[buf])
        wgwu_ready[k] = (buf, s_wg[buf].n)

    def fetch_wd(e_):
        for ws in range(I // 512):
            p.dma(GE, wd_buf[:, ws * 4:(ws + 1) * 4].rearrange("p a b -> p (a b)"),
                  wdT_p[e_, ws].rearrange("p a b -> p (a b)"), s_wd)
        return s_wd.n

    p.ckpt("G0")
    NSPL = [(0, 512), (512, CAP - 512)] if CAP > 512 else [(0, CAP)]
    gat_ready = [None] * E
    reg_n = [None] * E

    def emit_gather(e_, wait_te=None):
        if wait_te is not None:
            p.wait(GE, s_te, wait_te)
        p.wait(GE, s_ge, ge_mlp)
        p.wait(GE, s_ve, v_idx)
        p.wait(GE, s_idx, w_idxrep)

        def ld(eng, ee=e_):
            eng.load(nreg[ee % 2], nf_i[0:1, ee: ee + 1])
            reg_n[ee] = nreg[ee % 2]
        p.on(GE, ld)
        buf = e_ % 2
        gat_ready[e_] = s_gt[buf].inc(16)
        p._dma_raw(GE, lambda e, b=buf, ee=e_:
                   e.dma_gather(xg_sb[:, b], x1bf_dram[:, :],
                                i16f[:, ee * NIDX: (ee + 1) * NIDX],
                                CAP, reg_n[ee], D,
                                transpose=True).then_inc(s_gt[b].h, 16))

    p.wait(GE, s_x1s[0], x1s_totals[0])
    p.wait(GE, s_x1s[1], x1s_totals[1])
    emit_gather(0)
    fetch_wgwu(0)
    fetch_wgwu(1)
    wd_ready = [None] * E
    wd_ready[0] = fetch_wd(0)
    p.ckpt("G")

    te_after_up = [0] * E
    te_after_down = [0] * E
    sca_of = [0] * E
    for e_ in range(E):
        xg = xg_sb[:, e_ % 2]
        p.wait(TE, s_gt[e_ % 2], gat_ready[e_])
        v_ub = {}
        bi = 0
        for isl in range(nIS):
            k = e_ * nIS + isl
            wbuf_, wv_ = wgwu_ready[k]
            p.wait(TE, s_wg[wbuf_], wv_)
            for mc in range(IB // 128):
                img = isl * (IB // 128) + mc
                for (n0, nw) in NSPL:
                    pr = (PS[0], PS[1]) if bi % 2 == 0 else (PS[2], PS[3])
                    mg = moegelu[:, bi % 2, 0:nw]
                    if bi >= 2:
                        p.wait(TE, s_ve, v_ub[bi - 2])
                    for which, wb in ((0, wg_buf), (1, wu_buf)):
                        for dc in range(8):
                            mm(pr[which][:, 0:nw],
                               wb[:, wbuf_, dc, mc * 128: mc * 128 + 128],
                               xg[:, dc, n0: n0 + nw], dc == 0, dc == 7)
                    te_u = p.mark(TE, s_te)
                    p.wait(SE, s_te, te_u)
                    p.on(SE, lambda e, o=mg, i=pr[0][:, 0:nw]:
                         e.activation(o, i, AF.Gelu, bias=0.0, scale=1.0))
                    se_u = p.mark(SE, s_se)
                    p.wait(VE, s_se, se_u)
                    p.on(VE, lambda e, o=h_sb[:, img, n0: n0 + nw],
                         a=mg, b=pr[1][:, 0:nw]:
                         e.tensor_tensor(o, a, b, OP.mult))
                    v_ub[bi] = p.mark(VE, s_ve)
                    bi += 1
            te_end[k] = s_te.n
            if k + 2 < NSL:
                fetch_wgwu(k + 2, wait_te=(te_end[k - 1] if k >= 1 else None))
            if isl == 2 and e_ + 1 < E:
                emit_gather(e_ + 1,
                            wait_te=(te_after_down[e_ - 1] if e_ >= 1 else None))
        p.wait(TE, s_ve, v_ub[bi - 1])
        p.wait(TE, s_ve, v_ub[bi - 2])
        te_after_up[e_] = p.mark(TE, s_te)
        p.wait(TE, s_wd, wd_ready[e_])
        for tcq in range(NCAP):
            for dc in range(8):
                mm(PS[4][:, 0:16], xg[:, dc, tcq * 128: tcq * 128 + 128],
                   gwT_sb[:, dc * 16:(dc + 1) * 16], dc == 0, dc == 7)
            te_gl = p.mark(TE, s_te)
            p.wait(VE, s_te, te_gl)
            p.on(VE, lambda e, o=gat_s[:, 0:1], i=PS[4][:, 0:16]:
                 e.reduce_max(o, i, AX.X))
            p.on(VE, lambda e, o=gat_t[:, :], i=PS[4][:, 0:16], s=gat_s[:, 0:1]:
                 e.tensor_scalar(o, i, s, None, OP.subtract))
            v_gl = p.mark(VE, s_ve)
            p.wait(SE, s_ve, v_gl)
            p.on(SE, lambda e, o=gat_t[:, :], i=gat_t[:, :]:
                 e.activation(o, i, AF.Exp, bias=0.0, scale=1.0))
            se_gl = p.mark(SE, s_se)
            p.wait(VE, s_se, se_gl)
            p.on(VE, lambda e, o=gat_s[:, 1:2], i=gat_t[:, :]: e.reduce_sum(o, i, AX.X))
            p.on(VE, lambda e, o=gat_s[:, 2:3], i=gat_s[:, 1:2]: e.reciprocal(o, i))
            p.on(VE, lambda e, o=gat_s[:, 3:4], a=gat_t[:, e_: e_ + 1], b=gat_s[:, 2:3]:
                 e.tensor_tensor(o, a, b, OP.mult))
            v_gt = p.mark(VE, s_ve)
            for ns in range(2):
                for ic in range(IC):
                    mm(PS[5 + ns][:, :], h_sb[:, ic, tcq * 128: tcq * 128 + 128],
                       wd_buf[:, ic, ns * 512: ns * 512 + 512], ic == 0, ic == IC - 1)
            te_d = p.mark(TE, s_te)
            p.wait(VE, s_te, te_d)
            if tcq == 0 and e_ >= 1:
                p.wait(VE, s_sca, sca_of[e_ - 1])
                p.wait(SE, s_sca, sca_of[e_ - 1])
            p.on(VE, lambda e, o=eo_sb[:, tcq, 0:512],
                 i=PS[5][:, :], s=gat_s[:, 3:4]:
                 e.tensor_scalar(o, i, s, None, OP.mult))
            v_eo = p.mark(VE, s_ve)
            p.wait(SE, s_te, te_d)
            p.wait(SE, s_ve, v_gt)
            p.on(SE, lambda e, o=eo_sb[:, tcq, 512:1024],
                 i=PS[6][:, :], s=gat_s[:, 3:4]:
                 e.activation(o, i, AF.Copy, bias=0.0, scale=s))
            se_eo = p.mark(SE, s_se)
            p.wait(TE, s_ve, v_eo)
            p.wait(TE, s_se, se_eo)
        te_after_down[e_] = p.mark(TE, s_te)
        v_eo_done = s_ve.n
        se_eo_done = s_se.n
        if e_ + 1 < E:
            p.wait(GE, s_ve, v_eo_done)
            wd_ready[e_ + 1] = fetch_wd(e_ + 1)
        p.wait(GE, s_ve, v_eo_done)
        p.wait(GE, s_se, se_eo_done)
        if e_ == 0:
            p.wait(GE, s_msh[0], msh_totals[0])
            p.wait(GE, s_msh[1], msh_totals[1])
        if e_ >= 1:
            p.wait(GE, s_sca, sca_of[e_ - 1])
        sca_of[e_] = s_sca.inc(16)
        p._dma_raw(GE, lambda e, ee=e_:
                   e.dma_scatter_add(moe_dram[:, :], eo_sb,
                                     i16f[:, ee * NIDX: (ee + 1) * NIDX],
                                     CAP, reg_n[ee], D).then_inc(s_sca.h, 16))
    scatters_done = s_sca.n
    barrier()
    p.ckpt("H")

    p.wait(SY, s_sca, scatters_done)
    i_ve_done = [0] * TC
    ou_done = [0] * TC
    for tc in range(TC):
        buf = tc % 2
        mo = mo_t[:, buf]
        xf2 = xf2_t[:, buf]
        ot = o_t[:, buf]
        if tc >= 2:
            p.wait(SY, s_ve, i_ve_done[tc - 2])
        p.dma(SY, mo, moe_dram[tc * 128:(tc + 1) * 128, :], s_rb[buf])
        w_rb = p.dma(SY, xf2, x1f_dram[tc * 128:(tc + 1) * 128, :], s_rb[buf])
        p.wait(VE, s_rb[buf], w_rb)
        if tc >= 2:
            p.wait(VE, s_ou[buf], ou_done[tc - 2])
        emit_ln(lambda o, m_=mo, x_=xf2: p.on(
            VE, lambda e, oo=o, a=x_, b=m_: e.tensor_tensor(oo, a, b, OP.add)),
            tc, ot, None)
        i_ve_done[tc] = p.mark(VE, s_ve)
        p.wait(SY, s_ve, i_ve_done[tc])
        ou_done[tc] = p.dma(SY, out[tc * 128:(tc + 1) * 128, :], ot, s_ou[buf])
    p.wait(SY, s_ou[0], s_ou[0].n)
    p.wait(SY, s_ou[1], s_ou[1].n)

    if cfg.stop:
        n = p.ckpts[cfg.stop]
        for e in ENGINES:
            p.ops[e] = p.ops[e][:n[e]]
        srcd = x1f_dram if cfg.stop in ("D", "E", "F", "G", "G0", "Ga", "Gb", "Gc", "Gd", "Ge", "Gf") else \
            (moe_dram if cfg.stop == "H" else x1bf_dram)
        dbg_done = [0] * TC
        for tc in range(TC):
            buf = tc % 2
            ot = o_t[:, buf]
            if tc >= 2:
                p.wait(SY, s_dbo[buf], dbg_done[tc - 2])
            if cfg.stop in ("B", "C"):
                p.wait(SY, s_dbg[buf], s_dbg[buf].n)
                w = p.dma(SY, ot[0:128, 0:1], x_tm[tc * 128:(tc + 1) * 128, 0:1],
                          s_dbg[buf])
            else:
                w = p.dma(SY, ot, srcd[tc * 128:(tc + 1) * 128, :], s_dbg[buf])
            p.wait(SY, s_dbg[buf], w)
            dbg_done[tc] = p.dma(SY, out[tc * 128:(tc + 1) * 128, :], ot,
                                 s_dbo[buf])
        p.wait(SY, s_dbo[0], s_dbo[0].n)
        p.wait(SY, s_dbo[1], s_dbo[1].n)

    p.run()
    lower_extended_insts(nc)
    return nc


def kernel(**inputs):
    cfg = Cfg()
    com, cores = host_prep(cfg, inputs)
    nc = build(cfg)
    in_maps = []
    for b in range(cfg.n_cores):
        m = dict(com)
        m.update(cores[b])
        in_maps.append(m)
    res = run_bass_kernel_spmd(nc, in_maps, core_ids=list(range(cfg.n_cores)))
    outs = [np.asarray(r["out"]) for r in res.results]
    return np.stack(outs, axis=0)
